# version 04
# baseline (speedup 1.0000x reference)
import numpy as np
import concourse.bass as bass
import concourse.mybir as mybir

F32 = mybir.dt.float32
BF16 = mybir.dt.bfloat16
I32 = mybir.dt.int32
U32 = mybir.dt.uint32
AF = mybir.ActivationFunctionType
ALU = mybir.AluOpType
AX = mybir.AxisListType

ENGINES = ("sync", "gpsimd", "scalar", "vector", "tensor")


class Env:

    NDMA = 4

    def __init__(self, nc):
        self.nc = nc
        self.prog = {e: [] for e in ENGINES}
        self.count = {}
        self.sems = {}
        self.dma_rr = {}

    def make_sems(self, stack):
        for e in ENGINES:
            self.sems[e] = stack.enter_context(self.nc.semaphore(f"sem_{e}"))
            self.count[e] = 0
        for e in ("sync", "gpsimd", "scalar"):
            self.dma_rr[e] = 0
            for k in range(self.NDMA):
                key = f"{e}_d{k}"
                self.sems[key] = stack.enter_context(self.nc.semaphore(f"sem_{key}"))
                self.count[key] = 0

    def _wait(self, engine, deps):
        if not deps:
            return
        items = [(e, v) for e, v in deps.items() if v > 0]

        def go(eng, items=items):
            for e, v in items:
                eng.wait_ge(self.sems[e], v)
        self.prog[engine].append(go)

    def op(self, engine, fn, deps=None, inc=1, pe_row="full"):
        if engine != "tensor" and self.count[engine] > 0:
            deps = merge(deps, {engine: self.count[engine]})
        if engine == "tensor":
            if getattr(self, "_pe_row", None) is not None \
               and self._pe_row != pe_row and self.count[engine] > 0:
                deps = merge(deps, {engine: self.count[engine]})
            self._pe_row = pe_row
        if deps:
            self._wait(engine, deps)
        self.count[engine] += inc
        tgt = self.count[engine]
        sem = self.sems[engine]

        def go(eng, fn=fn, sem=sem, inc=inc):
            fn(eng).then_inc(sem, inc)
        self.prog[engine].append(go)
        return {engine: tgt}

    def ops(self, engine, fns, deps=None, inc_each=1):
        out = None
        for f in fns:
            out = self.op(engine, f, deps=deps, inc=inc_each)
            deps = None
        return out

    def raw(self, engine, fn):
        self.prog[engine].append(fn)

    def dma(self, engine, fn, deps=None):
        k = self.dma_rr[engine]
        self.dma_rr[engine] = (k + 1) % self.NDMA
        key = f"{engine}_d{k}"
        prev = {key: self.count[key]}
        self._wait(engine, merge(prev, deps))
        self.count[key] += 16
        tgt = self.count[key]
        sem = self.sems[key]

        def go(eng, fn=fn, sem=sem):
            fn(eng).then_inc(sem, 16)
        self.prog[engine].append(go)
        return {key: tgt}

    def now(self, engine):
        return {engine: self.count[engine]}

    def all_now(self):
        return dict(self.count)

    def barrier(self):
        snap = self.all_now()
        for e in ENGINES:
            deps = {k: v for k, v in snap.items() if k != e and v > 0}
            self._wait(e, deps)
        return snap

    def finalize(self, block):
        for e in ENGINES:
            prog = self.prog[e]

            def body(eng, prog=prog):
                for fn in prog:
                    fn(eng)
            getattr(block, e)(body)


def cp(e, d, src):
    return e.tensor_copy(d, src) if hasattr(e, "tensor_copy") else e.copy(d, src)


def merge(*deps):
    out = {}
    for d in deps:
        if not d:
            continue
        for k, v in d.items():
            out[k] = max(out.get(k, 0), v)
    return out



def host_constants():
    c = {}
    c["c_ident"] = np.eye(128, dtype=np.float32)
    import ml_dtypes
    c["c_ident_bf"] = np.eye(128, dtype=ml_dtypes.bfloat16)
    tril = np.tril(np.ones((64, 64), np.float32))
    strict = np.tril(np.ones((64, 64), np.float32), -1)
    c["c_tril2"] = np.concatenate([tril, tril], 0)
    c["c_strict2"] = np.concatenate([strict, strict], 0)
    triu = tril.T
    striu = strict.T
    c["c_triu2"] = np.concatenate([triu, triu], 0)
    c["c_negtriu2"] = -c["c_triu2"]
    c["c_striu2"] = np.concatenate([striu, striu], 0)
    cum = np.triu(np.ones((64, 64), np.float32))
    z = np.zeros((64, 64), np.float32)
    c["c_cum2"] = np.block([[cum, z], [z, cum]])
    sel = np.zeros((128, 128), np.float32)
    sel[63, 0:64] = 1.0
    sel[127, 64:128] = 1.0
    c["c_sellast"] = sel
    c["c_selh"] = np.eye(16, dtype=np.float32)
    om = np.zeros((16, 16 * 64), np.float32)
    for h in range(16):
        om[h, 64 * h:64 * (h + 1)] = 1.0
    c["c_onesmask"] = om
    c["c_ones128"] = np.ones((128, 1), np.float32)
    c["c_iota8"] = np.tile(np.arange(8, dtype=np.float32), (128, 1))
    c["c_iota8r"] = np.tile(np.arange(7, -1, -1, dtype=np.float32), (128, 1))
    return c



class Bufs:
    pass


def alloc_psum(env):
    nc = env.nc
    b = Bufs()
    b.ps_mm0 = nc.alloc_psum_tensor("ps_mm0", [128, 512], F32)
    b.ps_mm1 = nc.alloc_psum_tensor("ps_mm1", [128, 512], F32)
    b.ps_s = nc.alloc_psum_tensor("ps_s", [128, 512], F32)
    b.ps_aux = nc.alloc_psum_tensor("ps_aux", [128, 512], F32)
    b.ps_d = nc.alloc_psum_tensor("ps_d", [128, 512], F32)
    b.ps_tr = nc.alloc_psum_tensor("ps_tr", [128, 512], BF16)
    b.ps_tr2 = nc.alloc_psum_tensor("ps_tr2", [128, 512], BF16)
    b.ps_sc = nc.alloc_psum_tensor("ps_sc", [128, 512], F32)
    return b



def emit_load_weight(env, w_hbm, w_sb, K, N, deps=None):
    ap = w_hbm.ap().rearrange("(c p) n -> p c n", p=128)
    return env.dma("gpsimd", lambda e: e.dma_start(
        out=w_sb.ap().rearrange("p (c n) -> p c n", n=N), in_=ap), deps=deps)



def emit_ln_tile(env, b, x_f32, tn_bf, eps_ap, d_model=1024, deps=None):
    d1 = env.op("vector", lambda e: e.reduce_sum(b.ln_mean[:, 0:1], x_f32, axis=AX.X),
                deps=deps)
    d2 = env.op("vector", lambda e: e.tensor_scalar(b.ln_mean[:, 1:2], b.ln_mean[:, 0:1],
                                                    1.0 / d_model, None, ALU.mult),
                deps=d1)
    d3 = env.op("vector", lambda e: e.tensor_scalar(b.ln_xm[:, :], x_f32,
                                                    b.ln_mean[:, 1:2], None, ALU.subtract),
                deps=d2)
    d4 = env.op("scalar", lambda e: e.activation(b.ln_sq[:, :], b.ln_xm[:, :], AF.Square),
                deps=d3)
    d5 = env.op("vector", lambda e: e.reduce_sum(b.ln_var[:, 0:1], b.ln_sq[:, :], axis=AX.X),
                deps=d4)
    d6 = env.op("scalar", lambda e: e.activation(b.ln_var[:, 1:2], b.ln_var[:, 0:1],
                                                 AF.Sqrt, bias=eps_ap, scale=1.0 / d_model),
                deps=d5)
    d7 = env.op("vector", lambda e: e.reciprocal(b.ln_rstd[:, 0:1], b.ln_var[:, 1:2]),
                deps=d6)
    d8 = env.op("vector", lambda e: e.tensor_scalar(tn_bf, b.ln_xm[:, :],
                                                    b.ln_rstd[:, 0:1], None, ALU.mult),
                deps=merge(d3, d7))
    return d8


def emit_transpose_blocks(env, b, blocks, ident, deps=None, evac="vector"):
    banks = [b.ps_tr, b.ps_tr2]
    evac_dep = [None, None]
    last = None
    for i, (s_ap, d_ap) in enumerate(blocks):
        bk = banks[i % 2]
        war = merge(deps, evac_dep[i % 2])
        deps = None
        dtr = env.op("tensor", lambda e, s=s_ap, p=bk[:, 0:128]:
                     e.transpose(p, s, ident), deps=war)
        last = env.op(evac, lambda e, p=bk[:, 0:128], d=d_ap: e.tensor_copy(d, p),
                      deps=dtr)
        evac_dep[i % 2] = last
    return last



class MMGroups:

    def __init__(self, env, b, banks=None):
        self.env = env
        self.banks = banks or [b.ps_mm0, b.ps_mm1]
        self.evac_dep = [None] * len(self.banks)
        self.i = 0

    def group(self, mms, evac_fn, deps=None, evac_engine="vector", width=512):
        k = self.i % len(self.banks)
        self.i += 1
        bk = self.banks[k]
        p_ap = bk[:, 0:width]
        war = merge(deps, self.evac_dep[k])
        n = len(mms)
        dmm = None
        for ci, fn in enumerate(mms):
            dmm = self.env.op("tensor",
                              lambda e, fn=fn, st=(ci == 0), sp=(ci == n - 1),
                              p=p_ap: fn(e, p, st, sp),
                              deps=war if ci == 0 else None)
            war = None
        out = self.env.op(evac_engine, lambda e, p=p_ap: evac_fn(e, p), deps=dmm)
        self.evac_dep[k] = out
        return out


def emit_proj_f(env, b, tT, w_sb, out_f, n_tok, mm, w_col0=0, w_ncols=1024,
                deps=None, evac_engine="vector", tT_w=None, nblk=512,
                out_w=None):
    if tT_w is None:
        tT_w = n_tok
    if out_w is None:
        out_w = n_tok
    last = None
    for j in range(w_ncols // 128):
        for nb in range(0, n_tok, nblk):
            nw = min(nblk, n_tok - nb)
            mms = []
            for c in range(8):
                lhs = w_sb[:, w_ncols * c + w_col0 + 128 * j:
                           w_ncols * c + w_col0 + 128 * (j + 1)]
                rhs = tT[:, tT_w * c + nb: tT_w * c + nb + nw]
                mms.append(lambda e, p, st, sp, l=lhs, r=rhs:
                           e.matmul(p[:, 0:r.shape[1]], l, r, start=st, stop=sp))
            dst = out_f[:, out_w * j + nb: out_w * j + nb + nw]
            last = mm.group(mms, lambda e, p, d=dst, nw=nw: cp(e, d, p[:, 0:nw]),
                            deps=deps, evac_engine=evac_engine, width=nw)
            deps = None
    return last


def emit_proj_t(env, b, tT, w_sb, out_t, n_tok, mm, w_ncols=1024,
                deps=None, tT_w=None, out_stride=None, out_off=0,
                evac_engine="scalar"):
    if tT_w is None:
        tT_w = n_tok
    if out_stride is None:
        out_stride = w_ncols
    last = None
    for tt in range(n_tok // 128):
        for nb in range(0, w_ncols, 512):
            nw = min(512, w_ncols - nb)
            mms = []
            for c in range(8):
                lhs = tT[:, tT_w * c + 128 * tt: tT_w * c + 128 * (tt + 1)]
                rhs = w_sb[:, w_ncols * c + nb: w_ncols * c + nb + nw]
                mms.append(lambda e, p, st, sp, l=lhs, r=rhs:
                           e.matmul(p[:, 0:r.shape[1]], l, r, start=st, stop=sp))
            dst = out_t[:, out_off + out_stride * tt + nb:
                        out_off + out_stride * tt + nb + nw]
            last = mm.group(mms, lambda e, p, d=dst, nw=nw: cp(e, d, p[:, 0:nw]),
                            deps=deps, evac_engine=evac_engine, width=nw)
            deps = None
    return last



def alloc_attnb_bufs(nc, b, seq=2048):
    nt = seq // 128
    b.tT = nc.alloc_sbuf_tensor("tT", [128, 8 * seq], BF16)
    b.qT = nc.alloc_sbuf_tensor("qT", [128, 8 * seq], BF16)
    b.kT = nc.alloc_sbuf_tensor("kT", [128, 8 * seq], BF16)
    b.v_tok = nc.alloc_sbuf_tensor("v_tok", [128, nt * 1024], BF16)
    b.srow = nc.alloc_sbuf_tensor("srow", [128, seq], F32)
    b.sexp = nc.alloc_sbuf_tensor("sexp", [128, seq], BF16)
    b.attnT = nc.alloc_sbuf_tensor("attnT", [128, nt * 512], BF16)
    b.stats = nc.alloc_sbuf_tensor("stats", [128, 8], F32)
    b.tn = nc.alloc_sbuf_tensor("tn", [128, 1024], BF16)
    b.xf = nc.alloc_sbuf_tensor("xf", [128, 1024], F32)
    b.xo = nc.alloc_sbuf_tensor("xo_sb", [128, 1024], F32)
    b.ln_mean = nc.alloc_sbuf_tensor("ln_mean", [128, 2], F32)
    b.ln_var = nc.alloc_sbuf_tensor("ln_var", [128, 2], F32)
    b.ln_rstd = nc.alloc_sbuf_tensor("ln_rstd", [128, 1], F32)
    b.ln_xm = nc.alloc_sbuf_tensor("ln_xm", [128, 1024], F32)
    b.ln_sq = nc.alloc_sbuf_tensor("ln_sq", [128, 1024], F32)


def emit_ln_transpose_all(env, b, x_hbm, row0, n_tiles, tT, tT_w, ident,
                          eps_ap, deps=None, x_dst_hbm=None, tok_dst=None,
                          tok_stride=1024, tok_off=0):
    last = None
    for t in range(n_tiles):
        dl = env.dma("sync", lambda e, r=row0 + 128 * t: e.dma_start(
            out=b.xf[:, :], in_=x_hbm[r:r + 128, :]), deps=deps)
        deps = None
        dln = emit_ln_tile(env, b, b.xf[:, :], b.tn[:, :], eps_ap, deps=dl)
        if tok_dst is not None:
            env.op("vector", lambda e, t=t: e.tensor_copy(
                tok_dst[:, tok_off + tok_stride * t: tok_off + tok_stride * t + 1024],
                b.tn[:, :]), deps=dln)
        if x_dst_hbm is not None:
            env.dma("sync", lambda e, r=row0 + 128 * t, t=t: e.dma_start(
                out=x_dst_hbm[r:r + 128, :], in_=b.tn[:, :]), deps=dln)
        blocks = [(b.tn[:, 128 * c:128 * (c + 1)],
                   tT[:, tT_w * c + 128 * t: tT_w * c + 128 * t + 128])
                  for c in range(8)]
        last = emit_transpose_blocks(env, b, blocks, ident, deps=dln)
        deps = merge(last, dln)
    return last


def emit_attnb_batch(env, b, mm, x_hbm, xo_hbm, row0, wq_h, wk_h, wv_h, wo_h,
                     w_buf, ident, negmask, eps_ap, seq=2048, deps=None):
    nt = seq // 128

    d_tT = emit_ln_transpose_all(env, b, x_hbm, row0, nt, b.tT, seq, ident,
                                 eps_ap, deps=deps)
    snap = env.barrier()

    dw = emit_load_weight(env, wq_h, w_buf, 1024, 1024)
    d_q = emit_proj_f(env, b, b.tT, w_buf, b.qT, seq, mm, deps=dw)
    dw = emit_load_weight(env, wk_h, w_buf, 1024, 1024, deps=d_q)
    d_k = emit_proj_f(env, b, b.tT, w_buf, b.kT, seq, mm, deps=dw)
    dw = emit_load_weight(env, wv_h, w_buf, 1024, 1024, deps=d_k)
    d_v = emit_proj_t(env, b, b.tT, w_buf, b.v_tok, seq, mm, deps=dw)
    env.barrier()

    for sg in range(seq // 512):
        n_kt = 4 * sg + 4
        kw = 512 * sg + 512
        d_grp = []
        prev_s = None
        for si in range(4):
            s = 4 * sg + si
            for kb in range(sg + 1):
                mms = []
                for c in range(8):
                    lhs = b.qT[:, seq * c + 128 * s: seq * c + 128 * (s + 1)]
                    rhs = b.kT[:, seq * c + 512 * kb: seq * c + 512 * (kb + 1)]
                    mms.append(lambda e, p, st, sp, l=lhs, r=rhs:
                               e.matmul(p, l, r, start=st, stop=sp))
                if kb == sg:
                    def evac(e, p, si=si, kb=kb):
                        return e.tensor_add(b.srow[:, 512 * kb:512 * (kb + 1)],
                                            p, negmask[:, 512 * si:512 * (si + 1)])
                else:
                    def evac(e, p, kb=kb):
                        return e.tensor_copy(b.srow[:, 512 * kb:512 * (kb + 1)], p)
                d_sc = mm.group(mms, evac, deps=prev_s if kb == 0 else None)
            d_m = env.op("vector", lambda e, kw=kw: e.reduce_max(
                b.stats[:, 0:1], b.srow[:, 0:kw], axis=AX.X), deps=d_sc)
            d_b = env.op("vector", lambda e: e.tensor_scalar(
                b.stats[:, 1:2], b.stats[:, 0:1], -1.0 / 32.0, None, ALU.mult),
                deps=merge(d_m, prev_s))
            d_e = env.op("scalar", lambda e, kw=kw: e.activation(
                b.sexp[:, 0:kw], b.srow[:, 0:kw], AF.Exp,
                bias=b.stats[:, 1:2], scale=1.0 / 32.0,
                accum_out=b.stats[:, 2:3]), deps=d_b)
            d_r = env.op("vector", lambda e: e.reciprocal(
                b.stats[:, 3:4], b.stats[:, 2:3]), deps=d_e)
            d_s = env.op("vector", lambda e, kw=kw: e.tensor_scalar(
                b.sexp[:, 0:kw], b.sexp[:, 0:kw], b.stats[:, 3:4], None,
                ALU.mult), deps=d_r)
            blocks = [(b.sexp[:, 128 * kt:128 * (kt + 1)],
                       b.attnT[:, 512 * kt + 128 * si: 512 * kt + 128 * si + 128])
                      for kt in range(n_kt)]
            d_tr = emit_transpose_blocks(env, b, blocks, ident, deps=d_s)
            d_grp.append(d_tr)
            prev_s = merge(d_e, d_s, d_tr)
        dep_attnT = merge(*d_grp)
        for j in range(8):
            mms = []
            for kt in range(n_kt):
                lhs = b.v_tok[:, 1024 * kt + 128 * j: 1024 * kt + 128 * (j + 1)]
                rhs = b.attnT[:, 512 * kt: 512 * (kt + 1)]
                mms.append(lambda e, p, st, sp, l=lhs, r=rhs:
                           e.matmul(p, l, r, start=st, stop=sp))
            dst = b.tT[:, seq * j + 512 * sg: seq * j + 512 * sg + 512]
            mm.group(mms, lambda e, p, d=dst: cp(e, d, p),
                     deps=dep_attnT, evac_engine="scalar")
    env.barrier()

    dw = emit_load_weight(env, wo_h, w_buf, 1024, 1024)
    d_o = emit_proj_f(env, b, b.tT, w_buf, b.qT, seq, mm, tT_w=seq, deps=dw)
    env.barrier()

    prev_tr = None
    for t in range(nt):
        dl = env.dma("sync", lambda e, r=row0 + 128 * t: e.dma_start(
            out=b.xf[:, :], in_=x_hbm[r:r + 128, :]))
        blocks = [(b.qT[:, seq * c + 128 * t: seq * c + 128 * t + 128],
                   b.tn[:, 128 * c:128 * (c + 1)]) for c in range(8)]
        d_tr = emit_transpose_blocks(env, b, blocks, ident, deps=prev_tr)
        prev_tr = d_tr
        d_add = env.op("vector", lambda e: e.tensor_add(
            b.xo[:, :], b.xf[:, :], b.tn[:, :]), deps=merge(dl, d_tr))
        d_st = env.dma("sync", lambda e, r=row0 + 128 * t: e.dma_start(
            out=xo_hbm[r:r + 128, :], in_=b.xo[:, :]), deps=d_add)
        env._wait("sync", d_st)
        env._wait("vector", d_st)
    return env.all_now()



def alloc_attna_bufs(nc, b, seg=512):
    nt = seg // 128
    b.a_tT = nc.alloc_sbuf_tensor("a_tT", [128, 8 * seg], BF16)
    b.a_q = nc.alloc_sbuf_tensor("a_q", [128, nt * 1024], BF16)
    b.a_k = nc.alloc_sbuf_tensor("a_k", [128, nt * 1024], BF16)
    b.a_v = nc.alloc_sbuf_tensor("a_v", [128, nt * 1024], BF16)
    b.a_gate = nc.alloc_sbuf_tensor("a_gate", [128, nt * 1024], BF16)
    b.a_kbg = nc.alloc_sbuf_tensor("a_kbg", [128, nt * 1024], BF16)
    b.a_kw = nc.alloc_sbuf_tensor("a_kw", [128, nt * 1024], BF16)
    b.a_qT = nc.alloc_sbuf_tensor("a_qT", [128, 8 * seg], BF16)
    b.a_kT = nc.alloc_sbuf_tensor("a_kT", [128, 8 * seg], BF16)
    b.a_P = nc.alloc_sbuf_tensor("a_P", [128, nt * 1024], BF16)
    b.a_attnT = nc.alloc_sbuf_tensor("a_attnT", [128, nt * 1024], BF16)
    b.a_vp = nc.alloc_sbuf_tensor("a_vp", [128, nt * 1024], BF16)
    b.a_kcd = nc.alloc_sbuf_tensor("a_kcd", [128, nt * 1024], BF16)
    b.a_kcdT = nc.alloc_sbuf_tensor("a_kcdT", [128, nt * 1024], BF16)
    b.a_vn = nc.alloc_sbuf_tensor("a_vn", [128, nt * 1024], BF16)
    b.a_o = nc.alloc_sbuf_tensor("a_o", [128, nt * 1024], BF16)
    for nm in ("a_beta", "a_g", "a_eg", "a_egl", "a_betaeg", "a_sc1",
               "a_sc2", "a_raw"):
        setattr(b, nm, nc.alloc_sbuf_tensor(nm, [128, nt * 16], F32))
    b.a_gF = nc.alloc_sbuf_tensor("a_gF", [16, seg], F32)
    b.a_glb = nc.alloc_sbuf_tensor("a_glb", [128, (seg // 128) * 16], F32)
    b.a_ngF = nc.alloc_sbuf_tensor("a_ngF", [16, seg], F32)
    b.a_S = nc.alloc_sbuf_tensor("a_S", [128, 512], F32)
    b.a_Sbf = nc.alloc_sbuf_tensor("a_Sbf", [128, 512], BF16)
    b.a_egS = nc.alloc_sbuf_tensor("a_egS", [128, 16], F32)
    b.a_bdpk = nc.alloc_sbuf_tensor("a_bdpk", [128, 128], BF16)
    b.a_bdpkt = nc.alloc_sbuf_tensor("a_bdpkt", [128, 128], BF16)
    b.a_u = nc.alloc_sbuf_tensor("a_u", [128, 128], BF16)
    b.a_w1 = nc.alloc_sbuf_tensor("a_w1", [128, 512], F32)
    b.a_w2 = nc.alloc_sbuf_tensor("a_w2", [128, 512], F32)
    b.a_Etl = nc.alloc_sbuf_tensor("a_Etl", [128, 512], BF16)
    b.a_Etu = nc.alloc_sbuf_tensor("a_Etu", [128, 512], BF16)
    b.a_sqsc = nc.alloc_sbuf_tensor("a_sqsc", [128, 1024], F32)


def emit_attna_batch(env, b, mm, x_hbm, xo_hbm, row0, W, consts, seq=2048,
                     seg=512, deps=None):
    import os
    STOP = int(os.environ.get("ATTNA_STOP", "99"))
    """One batch of deltanet. W: dict with HBM handles wq,wk,wv,wg,wo and
    SBUF wa,wb (preloaded [128,8*16]), w_buf streaming buffer.
    consts: dict of SBUF const APs: ident, cum2, sellast, onesmask, trilw,
    strictw, negtriuw, triuw, dtb (dt_bias bcast [128,16]), negexpA
    ([128,16], prebuilt), normw64 ([128,64]), eps_ln, eps_q64, eps_k,
    eps_rms, selb0/selb64 ([128,64] row-selector consts).
    """
    nc = env.nc
    nt = seg // 128
    n_seg = seq // seg
    ident = consts["ident"]

    d0 = env.op("vector", lambda e: e.memset(b.a_S[:, :], 0.0), deps=deps)
    env.op("vector", lambda e: e.memset(b.a_Sbf[:, :], 0.0), deps=d0)
    env.op("vector", lambda e: e.memset(b.a_bdpk[:, :], 0.0))
    env.op("vector", lambda e: e.memset(b.a_u[:, :], 0.0))
    env.barrier()

    for s in range(n_seg):
        srow = row0 + seg * s
        emit_ln_transpose_all(env, b, x_hbm, srow, nt, b.a_tT, seg, ident,
                              consts["eps_ln"])
        env.barrier()

        dw = emit_load_weight(env, W["wq_h"], W["w_buf"], 1024, 1024)
        emit_proj_t(env, b, b.a_tT, W["w_buf"], b.a_q, seg, mm, deps=dw,
                    tT_w=seg)
        dw = emit_load_weight(env, W["wk_h"], W["w_buf"], 1024, 1024,
                              deps=env.now("tensor"))
        emit_proj_t(env, b, b.a_tT, W["w_buf"], b.a_k, seg, mm, deps=dw,
                    tT_w=seg)
        dw = emit_load_weight(env, W["wv_h"], W["w_buf"], 1024, 1024,
                              deps=env.now("tensor"))
        emit_proj_t(env, b, b.a_tT, W["w_buf"], b.a_v, seg, mm, deps=dw,
                    tT_w=seg)
        dw = emit_load_weight(env, W["wg_h"], W["w_buf"], 1024, 1024,
                              deps=env.now("tensor"))
        emit_proj_t(env, b, b.a_tT, W["w_buf"], b.a_gate, seg, mm, deps=dw,
                    tT_w=seg)
        emit_proj_t(env, b, b.a_tT, W["wb"], b.a_raw, seg, mm, w_ncols=16,
                    tT_w=seg, out_stride=16)
        env.barrier()
        for t in range(nt):
            env.op("vector", lambda e, t=t: e.tensor_copy(
                b.a_sc1[:, 16 * t:16 * (t + 1)], b.a_raw[:, 16 * t:16 * (t + 1)]))
        env.barrier()
        emit_proj_t(env, b, b.a_tT, W["wa"], b.a_raw, seg, mm, w_ncols=16,
                    tT_w=seg, out_stride=16)
        env.barrier()

        for t in range(nt):
            rb = b.a_sc1[:, 16 * t:16 * (t + 1)]
            ra = b.a_raw[:, 16 * t:16 * (t + 1)]
            bcol = b.a_beta[:, 16 * t:16 * (t + 1)]
            gcol = b.a_g[:, 16 * t:16 * (t + 1)]
            d_sg = env.op("scalar", lambda e, rb=rb, bcol=bcol: e.activation(
                bcol, rb, AF.Sigmoid))
            z = b.a_sc2[:, 16 * t:16 * (t + 1)]
            d_z = env.op("vector", lambda e, ra=ra, z=z: e.tensor_add(
                z, ra, consts["dtb"]))
            tmp = b.a_sc1[:, 16 * t:16 * (t + 1)]
            d_ab = env.op("scalar", lambda e, z=z, tmp=tmp: e.activation(
                tmp, z, AF.Abs), deps=merge(d_z, d_sg))
            env.op("scalar", lambda e, tmp=tmp: e.activation(
                tmp, tmp, AF.Exp, scale=-1.0))
            d_ln = env.op("scalar", lambda e, tmp=tmp: e.activation(
                tmp, tmp, AF.Ln, bias=1.0))
            env.op("vector", lambda e, z=z: e.tensor_scalar_max(z, z, 0.0),
                   deps=d_ab)
            env.op("vector", lambda e, tmp=tmp, z=z: e.tensor_add(
                tmp, tmp, z), deps=d_ln)
            dec = z
            env.op("vector", lambda e, tmp=tmp, dec=dec: e.tensor_mul(
                dec, tmp, consts["negexpA"]))
            dmm = env.op("tensor", lambda e, dec=dec: e.matmul(
                b.ps_d[:, 0:16], consts["cum2"], dec, start=True, stop=True),
                deps=env.now("vector"))
            dgv = env.op("vector", lambda e, gcol=gcol: e.tensor_copy(
                gcol, b.ps_d[:, 0:16]), deps=dmm)
            env.op("scalar", lambda e, gcol=gcol, t=t: e.activation(
                b.a_eg[:, 16 * t:16 * (t + 1)], gcol, AF.Exp), deps=dgv)
            dm2 = env.op("tensor", lambda e, gcol=gcol: e.matmul(
                b.ps_d[:, 16:32], consts["sellast"], gcol, start=True,
                stop=True), deps=dgv)
            env.op("vector", lambda e, t=t: e.tensor_copy(
                b.a_glb[:, 16 * t:16 * (t + 1)], b.ps_d[:, 16:32]), deps=dm2)
            dsub = env.op("vector", lambda e, gcol=gcol, t=t: e.tensor_sub(
                b.a_sc2[:, 16 * t:16 * (t + 1)], b.ps_d[:, 16:32], gcol))
            env.op("scalar", lambda e, t=t: e.activation(
                b.a_egl[:, 16 * t:16 * (t + 1)],
                b.a_sc2[:, 16 * t:16 * (t + 1)], AF.Exp), deps=dsub)
            env.op("vector", lambda e, t=t: e.tensor_mul(
                b.a_betaeg[:, 16 * t:16 * (t + 1)],
                b.a_beta[:, 16 * t:16 * (t + 1)],
                b.a_eg[:, 16 * t:16 * (t + 1)]),
                deps=env.now("scalar"))
        env.barrier()
        blocks = []
        prev_ev = None
        for t in range(nt):
            dtr = env.op("tensor", lambda e, t=t: e.transpose(
                b.ps_sc[0:16, 0:128], b.a_g[:, 16 * t:16 * (t + 1)],
                consts["identf"]), deps=prev_ev)
            prev_ev = env.op("vector", lambda e, t=t: e.tensor_copy(
                b.a_gF[:, 128 * t:128 * (t + 1)], b.ps_sc[0:16, 0:128]),
                deps=dtr)
        denf = env.op("vector", lambda e: e.tensor_scalar_mul(
            b.a_ngF[:, :], b.a_gF[:, :], -1.0))
        env.barrier()

        if STOP < 2:
            continue
        for (src, dst, epsk) in ((b.a_q, b.a_qT, "eps_q64"),
                                 (b.a_k, b.a_kT, "eps_k")):
            for t in range(nt):
                dsq = env.op("scalar", lambda e, t=t, src=src: e.activation(
                    b.a_sqsc[:, :], src[:, 1024 * t:1024 * (t + 1)], AF.Square))
                for h in range(16):
                    env.op("vector", lambda e, h=h: e.reduce_sum(
                        b.a_sc2[:, h:h + 1], b.a_sqsc[:, 64 * h:64 * (h + 1)],
                        axis=AX.X), deps=dsq if h == 0 else None)
                if epsk == "eps_q64":
                    dsq2 = env.op("scalar", lambda e: e.activation(
                        b.a_sc2[:, 16:32], b.a_sc2[:, 0:16], AF.Sqrt,
                        bias=consts["eps_q64"], scale=64.0),
                        deps=env.now("vector"))
                else:
                    dsq2 = env.op("scalar", lambda e: e.activation(
                        b.a_sc2[:, 16:32], b.a_sc2[:, 0:16], AF.Sqrt,
                        bias=consts["eps_k"], scale=1.0),
                        deps=env.now("vector"))
                drc = env.op("vector", lambda e: e.reciprocal(
                    b.a_sc2[:, 32:48], b.a_sc2[:, 16:32]), deps=dsq2)
                for h in range(16):
                    env.op("vector", lambda e, t=t, h=h, src=src:
                           e.tensor_scalar(
                               src[:, 1024 * t + 64 * h:1024 * t + 64 * (h + 1)],
                               src[:, 1024 * t + 64 * h:1024 * t + 64 * (h + 1)],
                               b.a_sc2[:, 32 + h:33 + h], None, ALU.mult),
                           deps=drc if h == 0 else None)
            env.barrier()
            for t in range(nt):
                blocks = [(src[:, 1024 * t + 128 * c:1024 * t + 128 * (c + 1)],
                           dst[:, seg * c + 128 * t:seg * c + 128 * t + 128])
                          for c in range(8)]
                emit_transpose_blocks(env, b, blocks, ident,
                                      deps=env.now("vector"))
            env.barrier()

        for t in range(nt):
            for h in range(16):
                cc = 1024 * t + 64 * h
                env.op("vector", lambda e, cc=cc, t=t, h=h: e.tensor_scalar(
                    b.a_kbg[:, cc:cc + 64], b.a_k[:, cc:cc + 64],
                    b.a_betaeg[:, 16 * t + h:16 * t + h + 1], None, ALU.mult))
                env.op("vector", lambda e, cc=cc, t=t, h=h: e.tensor_scalar(
                    b.a_kw[:, cc:cc + 64], b.a_k[:, cc:cc + 64],
                    b.a_egl[:, 16 * t + h:16 * t + h + 1], None, ALU.mult))
                env.op("vector", lambda e, cc=cc, t=t, h=h: e.tensor_scalar(
                    b.a_v[:, cc:cc + 64], b.a_v[:, cc:cc + 64],
                    b.a_beta[:, 16 * t + h:16 * t + h + 1], None, ALU.mult))
        env.barrier()

        if STOP < 3:
            continue
        SUB = int(os.environ.get("ATTNA_SUB", "9"))
        for t in range(nt):
            dl = {}
            for i2 in range(2):
                i = 2 * t + i2
                bi = 64 * i2
                gsl = slice(128 * t + bi, 128 * t + bi + 64)
                for h in range(16):
                    ps = b.ps_d if h < 8 else b.ps_s
                    co = 64 * (h % 8)
                    dmm = env.op("tensor", lambda e, ps=ps, co=co, bi=bi,
                                 gsl=gsl, h=h: e.matmul(
                                     ps[bi:bi + 64, co:co + 64],
                                     b.a_gF[:, gsl],
                                     consts["onesmask"][:, 64 * h:64 * (h + 1)],
                                     start=True, stop=False,
                                     tile_position=(0, bi)),
                                 deps=env.now("vector") if (h == 0 and i2 == 0)
                                 else None, pe_row="k16")
                    dl = env.op("tensor", lambda e, ps=ps, co=co, bi=bi,
                                gsl=gsl, h=h: e.matmul(
                                    ps[bi:bi + 64, co:co + 64],
                                    consts["onesmask"][:, 64 * h:64 * (h + 1)],
                                    b.a_ngF[:, gsl],
                                    start=False, stop=True,
                                    tile_position=(0, bi)), pe_row="k16")
            if SUB < 2:
                env.op("vector", lambda e: e.tensor_copy(b.a_w1[:, :],
                                                         b.ps_d[:, :]),
                       deps=dl)
                continue
            for (ps, off) in ((b.ps_d, 0), (b.ps_s, 1)):
                d1 = env.op("vector", lambda e, ps=ps: e.tensor_mul(
                    b.a_w1[:, :], ps[:, :], consts["trilw"]), deps=dl)
                d2 = env.op("vector", lambda e, ps=ps: e.tensor_mul(
                    b.a_w2[:, :], ps[:, :], consts["negtriuw"]), deps=d1)
                d3 = env.op("scalar", lambda e: e.activation(
                    b.a_Etl[:, :], b.a_w1[:, :], AF.Exp), deps=d1)
                d4 = env.op("scalar", lambda e: e.activation(
                    b.a_Etu[:, :], b.a_w2[:, :], AF.Exp), deps=d2)
                if SUB < 3:
                    continue
                first = True
                for hh in (0, 2, 4, 6, 1, 3, 5, 7):
                    h = 8 * off + hh
                    hb = 64 * (h % 2)
                    jc = 512 * (h // 2)
                    for i2 in range(2):
                        i = 2 * t + i2
                        bi = 64 * i2
                        tc = jc + 128 * t + bi
                        ksl = b.a_kT[hb:hb + 64, tc:tc + 64]
                        qsl = b.a_qT[hb:hb + 64, tc:tc + 64]
                        env.op("tensor", lambda e, ksl=ksl, bi=bi, hh=hh, hb=hb:
                               e.matmul(b.ps_mm0[bi:bi + 64,
                                                 64 * hh:64 * hh + 64],
                                        ksl, ksl, start=True, stop=True,
                                        tile_position=(hb, bi)),
                               deps=merge(d3, d4) if first
                               else None, pe_row=hb)
                        first = False
                        env.op("tensor", lambda e, ksl=ksl, qsl=qsl, bi=bi,
                               hh=hh, hb=hb:
                               e.matmul(b.ps_mm1[bi:bi + 64,
                                                 64 * hh:64 * hh + 64],
                                        ksl, qsl, start=True, stop=True,
                                        tile_position=(hb, bi)), pe_row=hb)
                dmm = env.now("tensor")
                if SUB < 4:
                    continue
                d5 = env.op("vector", lambda e: e.tensor_mul(
                    b.a_w1[:, :], b.ps_mm0[:, :], b.a_Etl[:, :]),
                    deps=merge(dmm, d3))
                d6 = env.op("vector", lambda e: e.tensor_mul(
                    b.a_w2[:, :], b.ps_mm1[:, :], b.a_Etu[:, :]),
                    deps=merge(dmm, d4))
                d7 = env.op("vector", lambda e, t=t, off=off: e.tensor_mul(
                    b.a_attnT[:, 1024 * t + 512 * off:1024 * t + 512 * off + 512],
                    b.a_w2[:, :], consts["triuw"]), deps=d6)
                for hh in range(8):
                    h = 8 * off + hh
                    env.op("vector", lambda e, t=t, h=h, hh=hh:
                           e.scalar_tensor_tensor(
                               b.a_P[:, 1024 * t + 64 * h:1024 * t + 64 * h + 64],
                               b.a_w1[:, 64 * hh:64 * hh + 64],
                               b.a_beta[:, 16 * t + h:16 * t + h + 1],
                               consts["strictw"][:, 64 * hh:64 * hh + 64],
                               ALU.mult, ALU.mult),
                           deps=d5 if hh == 0 else None)
        env.barrier()

        if STOP < 4:
            continue
        prev_g = None
        for t in range(nt):
            for h in range(16):
                ct = 1024 * t + 64 * h
                d_i1 = env.op("vector", lambda e, ct=ct: e.tensor_copy(
                    b.a_bdpk[0:64, 0:64], b.a_P[0:64, ct:ct + 64]),
                    deps=prev_g)
                d_i2 = env.op("vector", lambda e, ct=ct: e.tensor_copy(
                    b.a_bdpk[64:128, 64:128], b.a_P[64:128, ct:ct + 64]))
                d_tr = env.op("tensor", lambda e: e.transpose(
                    b.ps_tr[:, 0:128], b.a_bdpk[:, :], ident),
                    deps=merge(d_i1, d_i2))
                d_pt = env.op("vector", lambda e: e.tensor_copy(
                    b.a_bdpkt[:, :], b.ps_tr[:, 0:128]), deps=d_tr)
                d_u1 = env.op("vector", lambda e, ct=ct: e.tensor_copy(
                    b.a_u[:, 0:64], b.a_v[:, ct:ct + 64]))
                d_u2 = env.op("vector", lambda e, ct=ct: e.tensor_copy(
                    b.a_u[:, 64:128], b.a_kbg[:, ct:ct + 64]))
                dep = merge(d_pt, d_u1, d_u2)
                ESUB = int(os.environ.get("ATTNA_ESUB", "9"))
                for j in range(6 if ESUB >= 4 else (1 if ESUB >= 2 else 0)):
                    d_um = env.op("tensor", lambda e: e.matmul(
                        b.ps_mm0[:, 0:128], b.a_bdpkt[:, :], b.a_u[:, :],
                        start=True, stop=True), deps=dep)
                    d_ua = env.op("vector", lambda e: e.tensor_add(
                        b.a_u[:, :], b.ps_mm0[:, 0:128], b.a_u[:, :]),
                        deps=d_um)
                    if j < 5 and ESUB >= 3:
                        d_s1 = env.op("tensor", lambda e: e.matmul(
                            b.ps_mm1[:, 0:128], b.a_bdpkt[:, :],
                            b.a_bdpk[:, :], start=True, stop=True),
                            deps=dep)
                        d_s2 = env.op("tensor", lambda e: e.matmul(
                            b.ps_aux[:, 0:128], b.a_bdpk[:, :],
                            b.a_bdpkt[:, :], start=True, stop=True))
                        d_e1 = env.op("vector", lambda e: e.tensor_copy(
                            b.a_bdpk[:, :], b.ps_mm1[:, 0:128]),
                            deps=merge(d_s2, d_um))
                        d_e2 = env.op("scalar", lambda e: e.copy(
                            b.a_bdpkt[:, :], b.ps_aux[:, 0:128]),
                            deps=merge(d_s2, d_um))
                        dep = merge(d_ua, d_e1, d_e2)
                    else:
                        dep = d_ua
                d_x1 = env.op("vector", lambda e, ct=ct: e.tensor_copy(
                    b.a_vp[:, ct:ct + 64], b.a_u[:, 0:64]), deps=dep)
                d_x2 = env.op("vector", lambda e, ct=ct: e.tensor_copy(
                    b.a_kcd[:, ct:ct + 64], b.a_u[:, 64:128]))
                prev_g = merge(d_x1, d_x2)
        env.barrier()
        for t in range(nt):
            blocks = [(b.a_kcd[:, 1024 * t + 128 * jp:1024 * t + 128 * (jp + 1)],
                       b.a_kcdT[:, 1024 * t + 128 * jp:1024 * t + 128 * (jp + 1)])
                      for jp in range(8)]
            emit_transpose_blocks(env, b, blocks, ident,
                                  deps=env.now("vector"))
        env.barrier()

        if STOP < 5:
            continue
        prev_dg3 = None
        for i in range(2 * nt):
            t = i // 2
            bi = 64 * (i % 2)
            dS = env.now("vector")
            dmm = None
            first = True
            for h in (0, 2, 4, 6, 8, 10, 12, 14, 1, 3, 5, 7, 9, 11, 13, 15):
                hb = 64 * (h % 2)
                jc = 64 * (h // 2)
                ps = b.ps_aux if h < 8 else b.ps_sc
                lhs = b.a_kcdT[hb:hb + 64,
                               1024 * t + 128 * (h // 2) + bi:
                               1024 * t + 128 * (h // 2) + bi + 64]
                rhs = b.a_Sbf[hb:hb + 64, jc:jc + 64]
                dmm = env.op("tensor", lambda e, l=lhs, r=rhs, ps=ps, bi=bi,
                             h=h, hb=hb: e.matmul(
                                 ps[bi:bi + 64, 64 * (h % 8):64 * (h % 8) + 64],
                                 l, r, start=True, stop=True,
                                 tile_position=(hb, bi)),
                             deps=dS if first else None, pe_row=hb)
                first = False
            d_vn = None
            for (ps, off) in ((b.ps_aux, 0), (b.ps_sc, 1)):
                d_vn = env.op("vector", lambda e, ps=ps, off=off, t=t, bi=bi:
                              e.tensor_sub(
                                  b.a_vn[bi:bi + 64,
                                         1024 * t + 512 * off:
                                         1024 * t + 512 * off + 512],
                                  b.a_vp[bi:bi + 64,
                                         1024 * t + 512 * off:
                                         1024 * t + 512 * off + 512],
                                  ps[bi:bi + 64, 0:512]), deps=dmm)
            dmm2 = None
            for h in range(16):
                ps = b.ps_mm0 if h < 8 else b.ps_mm1
                ct = 1024 * t + 64 * h
                dmm2 = env.op("tensor", lambda e, ct=ct, ps=ps, bi=bi, h=h:
                              e.matmul(
                                  ps[bi:bi + 64, 64 * (h % 8):64 * (h % 8) + 64],
                                  b.a_attnT[bi:bi + 64, ct:ct + 64],
                                  b.a_vn[bi:bi + 64, ct:ct + 64],
                                  start=True, stop=True,
                                  tile_position=(bi, bi)),
                              deps=d_vn if h == 0 else None, pe_row=bi)
            d_o2 = None
            for (ps, off) in ((b.ps_mm0, 0), (b.ps_mm1, 1)):
                d_o2 = env.op("vector", lambda e, ps=ps, off=off, t=t, bi=bi:
                              e.tensor_copy(
                                  b.a_o[bi:bi + 64,
                                        1024 * t + 512 * off:
                                        1024 * t + 512 * off + 512],
                                  ps[bi:bi + 64, 0:512]), deps=dmm2)
            dmm3 = None
            first = True
            for h in (0, 2, 4, 6, 8, 10, 12, 14, 1, 3, 5, 7, 9, 11, 13, 15):
                hb = 64 * (h % 2)
                ps = b.ps_aux if h < 8 else b.ps_sc
                lhs = b.a_qT[hb:hb + 64,
                             512 * (h // 2) + 128 * t + bi:
                             512 * (h // 2) + 128 * t + bi + 64]
                rhs = b.a_Sbf[hb:hb + 64, 64 * (h // 2):64 * (h // 2) + 64]
                dmm3 = env.op("tensor", lambda e, l=lhs, r=rhs, ps=ps, bi=bi,
                              h=h, hb=hb: e.matmul(
                                  ps[bi:bi + 64,
                                     64 * (h % 8):64 * (h % 8) + 64],
                                  l, r, start=True, stop=True,
                                  tile_position=(hb, bi)),
                              deps=merge(d_vn, d_o2) if first else None,
                              pe_row=hb)
                first = False
            d_o1 = None
            for h in range(16):
                ps = b.ps_aux if h < 8 else b.ps_sc
                ct = 1024 * t + 64 * h
                d_o1 = env.op("vector", lambda e, ct=ct, ps=ps, bi=bi, h=h,
                              t=t: e.scalar_tensor_tensor(
                                  b.a_o[bi:bi + 64, ct:ct + 64],
                                  ps[bi:bi + 64, 64 * (h % 8):64 * (h % 8) + 64],
                                  b.a_eg[bi:bi + 64, 16 * t + h:16 * t + h + 1],
                                  b.a_o[bi:bi + 64, ct:ct + 64],
                                  ALU.mult, ALU.add),
                              deps=dmm3 if h == 0 else None)
            dmm4 = None
            for h in range(16):
                hb = 64 * (h % 2)
                ct = 1024 * t + 64 * h
                dmm4 = env.op("tensor", lambda e, ct=ct, bi=bi, h=h, hb=hb:
                              e.matmul(
                                  b.ps_d[hb:hb + 64,
                                         64 * (h // 2):64 * (h // 2) + 64],
                                  b.a_kw[bi:bi + 64, ct:ct + 64],
                                  b.a_vn[bi:bi + 64, ct:ct + 64],
                                  start=True, stop=True,
                                  tile_position=(bi, hb)),
                              deps=merge(d_o1, d_vn) if h == 0 else None,
                              pe_row=bi)
            selb = consts["selb0"] if bi == 0 else consts["selb64"]
            dg1 = env.op("tensor", lambda e, t=t, selb=selb: e.matmul(
                b.ps_s[0:64, 0:16], selb,
                b.a_glb[:, 16 * t:16 * (t + 1)],
                start=True, stop=True, tile_position=(0, 0)),
                deps=merge(env.now("vector"), prev_dg3))
            dg2 = env.op("tensor", lambda e, t=t, selb=selb: e.matmul(
                b.ps_s[64:128, 0:16], selb,
                b.a_glb[:, 16 * t:16 * (t + 1)],
                start=True, stop=True, tile_position=(0, 64)))
            dg3 = env.op("scalar", lambda e: e.activation(
                b.a_egS[:, 0:16], b.ps_s[:, 0:16], AF.Exp), deps=dg2)
            prev_dg3 = dg3
            d_su = None
            for h in range(16):
                hb = 64 * (h % 2)
                jc = 64 * (h // 2)
                d_su = env.op("vector", lambda e, h=h, hb=hb, jc=jc:
                              e.scalar_tensor_tensor(
                                  b.a_S[hb:hb + 64, jc:jc + 64],
                                  b.a_S[hb:hb + 64, jc:jc + 64],
                                  b.a_egS[hb:hb + 64, h:h + 1],
                                  b.ps_d[hb:hb + 64, jc:jc + 64],
                                  ALU.mult, ALU.add),
                              deps=merge(dmm4, dg3) if h == 0 else None)
            env.op("vector", lambda e: e.tensor_copy(b.a_Sbf[:, :],
                                                     b.a_S[:, :]), deps=d_su)
        env.barrier()

        if STOP < 6:
            continue
        for t in range(nt):
            dsq = env.op("scalar", lambda e, t=t: e.activation(
                b.a_sqsc[:, :], b.a_o[:, 1024 * t:1024 * (t + 1)], AF.Square))
            for h in range(16):
                env.op("vector", lambda e, h=h: e.reduce_sum(
                    b.a_sc2[:, h:h + 1], b.a_sqsc[:, 64 * h:64 * (h + 1)],
                    axis=AX.X), deps=dsq if h == 0 else None)
            dsq2 = env.op("scalar", lambda e: e.activation(
                b.a_sc2[:, 16:32], b.a_sc2[:, 0:16], AF.Sqrt,
                bias=consts["eps_rms"], scale=1.0 / 64.0),
                deps=env.now("vector"))
            drc = env.op("vector", lambda e: e.reciprocal(
                b.a_sc2[:, 32:48], b.a_sc2[:, 16:32]), deps=dsq2)
            dsil = env.op("scalar", lambda e, t=t: e.activation(
                b.a_gate[:, 1024 * t:1024 * (t + 1)],
                b.a_gate[:, 1024 * t:1024 * (t + 1)], AF.Silu), deps=drc)
            for h in range(16):
                cc = 1024 * t + 64 * h
                env.op("vector", lambda e, cc=cc, h=h: e.scalar_tensor_tensor(
                    b.a_o[:, cc:cc + 64], b.a_o[:, cc:cc + 64],
                    b.a_sc2[:, 32 + h:33 + h], consts["normw64"],
                    ALU.mult, ALU.mult), deps=drc if h == 0 else None)
            dog = env.op("vector", lambda e, t=t: e.tensor_mul(
                b.a_o[:, 1024 * t:1024 * (t + 1)],
                b.a_o[:, 1024 * t:1024 * (t + 1)],
                b.a_gate[:, 1024 * t:1024 * (t + 1)]),
                deps=dsil)
            blocks = [(b.a_o[:, 1024 * t + 128 * c:1024 * t + 128 * (c + 1)],
                       b.a_tT[:, seg * c + 128 * t:seg * c + 128 * t + 128])
                      for c in range(8)]
            emit_transpose_blocks(env, b, blocks, ident, deps=dog)
        env.barrier()
        dw = emit_load_weight(env, W["wo_h"], W["w_buf"], 1024, 1024)
        emit_proj_f(env, b, b.a_tT, W["w_buf"], b.a_qT, seg, mm, deps=dw,
                    tT_w=seg)
        env.barrier()
        prev_tr = None
        for t in range(nt):
            dl = env.dma("sync", lambda e, r=srow + 128 * t: e.dma_start(
                out=b.xf[:, :], in_=x_hbm[r:r + 128, :]))
            blocks = [(b.a_qT[:, seg * c + 128 * t: seg * c + 128 * t + 128],
                       b.tn[:, 128 * c:128 * (c + 1)]) for c in range(8)]
            d_tr = emit_transpose_blocks(env, b, blocks, ident, deps=prev_tr)
            prev_tr = d_tr
            d_add = env.op("vector", lambda e: e.tensor_add(
                b.xo[:, :], b.xf[:, :], b.tn[:, :]), deps=merge(dl, d_tr))
            d_st = env.dma("sync", lambda e, r=srow + 128 * t: e.dma_start(
                out=xo_hbm[r:r + 128, :], in_=b.xo[:, :]), deps=d_add)
            env._wait("sync", d_st)
            env._wait("vector", d_st)
        env.barrier()
    return env.all_now()



class Branch:

    def __init__(self, env, cond):
        self.env = env
        self.cond = cond

    def __enter__(self):
        env = self.env
        self.saved_prog = {e: env.prog[e] for e in ENGINES}
        self.saved_count = dict(env.count)
        for e in ENGINES:
            env.prog[e] = []
        return self

    def __exit__(self, et, ev, tb):
        env = self.env
        if et is not None:
            return False
        body = {e: env.prog[e] for e in ENGINES}
        for e in ENGINES:
            env.prog[e] = self.saved_prog[e]
        deltas = {}
        for k, v in env.count.items():
            d = v - self.saved_count.get(k, 0)
            if d > 0:
                deltas[k] = d
        cond = self.cond
        for e in ENGINES:
            fns = body[e]
            eng_deltas = [(k, d) for k, d in deltas.items()
                          if k == e or k.startswith(e + "_d")]
            if not fns and not eng_deltas:
                continue

            pe_drain = (e == "tensor"
                        and self.saved_count.get("tensor", 0) > 0)

            def go(eng, fns=fns, eng_deltas=eng_deltas, cond=cond, env=env,
                   pe_drain=pe_drain, pe_tgt=self.saved_count.get("tensor",
                                                                  0)):
                if pe_drain:
                    eng.wait_ge(env.sems["tensor"], pe_tgt)
                left, right, op = cond(eng)
                with eng.If_cmp(left, right, op):
                    for f in fns:
                        f(eng)
                with eng.Else():
                    for k, d in eng_deltas:
                        if "_d" in k:
                            for _ in range(d // 16):
                                eng.sem_inc(env.sems[k], 16)
                        else:
                            eng.sem_inc(env.sems[k], d)
            env.prog[e].append(go)
        return False



def alloc_moe_bufs(nc, b, Tc):
    b.m_tT = nc.alloc_sbuf_tensor("m_tT", [128, 1024], BF16)
    b.m_logit = nc.alloc_sbuf_tensor("m_logit", [128, 8], F32)
    b.m_exp = nc.alloc_sbuf_tensor("m_exp", [128, 8], F32)
    b.m_stat = nc.alloc_sbuf_tensor("m_stat", [128, 16], F32)
    b.m_oh = nc.alloc_sbuf_tensor("m_oh", [128, 8], F32)
    b.m_pref = nc.alloc_sbuf_tensor("m_pref", [128, 8], F32)
    b.m_runbase = nc.alloc_sbuf_tensor("m_runbase", [1, 8], F32)
    b.m_tot = nc.alloc_sbuf_tensor("m_tot", [1, 8], F32)
    b.m_rw = nc.alloc_sbuf_tensor("m_rw", [128, 4], F32)
    b.m_base = nc.alloc_sbuf_tensor("m_base", [1, 8], F32)
    b.m_basebc = nc.alloc_sbuf_tensor("m_basebc", [128, 8], F32)
    b.m_cnti = nc.alloc_sbuf_tensor("m_cnti", [1, 8], I32)
    b.m_cntf = nc.alloc_sbuf_tensor("m_cntf", [1, 8], F32)
    b.m_cnts8 = nc.alloc_sbuf_tensor("m_cnts8", [8, 8], F32)
    b.m_mym = nc.alloc_sbuf_tensor("m_mym", [8, 1], F32)
    b.m_ein = nc.alloc_sbuf_tensor("m_ein", [128, 1024], BF16)
    b.m_einT = nc.alloc_sbuf_tensor("m_einT", [128, 1024], BF16)
    b.m_h = nc.alloc_sbuf_tensor("m_h", [128, 4096], BF16)
    b.m_hT = nc.alloc_sbuf_tensor("m_hT", [128, 4096], BF16)
    b.m_out = nc.alloc_sbuf_tensor("m_out", [128, 1024], BF16)
    b.m_eo = nc.alloc_sbuf_tensor("m_eo", [128, 1024], BF16)
    b.m_t2s = nc.alloc_sbuf_tensor("m_t2s", [128, 1024], BF16)
    b.m_slotf = nc.alloc_sbuf_tensor("m_slotf", [128, 1], F32)
    b.m_sloti = nc.alloc_sbuf_tensor("m_sloti", [128, 1], I32)
    b.m_w1 = nc.alloc_sbuf_tensor("m_w1", [128, 32768], BF16)
    b.m_w2 = nc.alloc_sbuf_tensor("m_w2", [128, 32768], BF16)


def emit_moe_prep(env, b, mm, x_hbm, row0, t2_hbm, t2row0, rout_hbm,
                  cnt_hbm, wr_sb, consts, n_tiles, ident):
    env.op("vector", lambda e: e.memset(b.m_runbase[:, :], 0.0))
    env.op("vector", lambda e: e.memset(b.m_rw[:, :], 0.0))
    prev = None
    for t in range(n_tiles):
        dl = env.dma("sync", lambda e, r=row0 + 128 * t: e.dma_start(
            out=b.xf[:, :], in_=x_hbm[r:r + 128, :]), deps=prev)
        dln = emit_ln_tile(env, b, b.xf[:, :], b.tn[:, :], consts["eps_ln"],
                           deps=dl)
        dst = env.dma("sync", lambda e, r=t2row0 + 128 * t: e.dma_start(
            out=t2_hbm[r:r + 128, :], in_=b.tn[:, :]), deps=dln)
        blocks = [(b.tn[:, 128 * c:128 * (c + 1)],
                   b.m_tT[:, 128 * c:128 * (c + 1)]) for c in range(8)]
        dtr = emit_transpose_blocks(env, b, blocks, ident, deps=dln)
        mms = []
        for c in range(8):
            mms.append(lambda e, p, st, sp, c=c: e.matmul(
                p[:, 0:8], b.m_tT[:, 128 * c:128 * (c + 1)],
                wr_sb[:, 8 * c:8 * (c + 1)], start=st, stop=sp))
        d_lg = mm.group(mms, lambda e, p: e.tensor_copy(b.m_logit[:, :],
                                                        p[:, 0:8]),
                        deps=dtr, width=8)
        d1 = env.op("vector", lambda e: e.reduce_max(
            b.m_stat[:, 0:1], b.m_logit[:, :], axis=AX.X), deps=d_lg)
        d2 = env.op("vector", lambda e: e.tensor_scalar_mul(
            b.m_stat[:, 1:2], b.m_stat[:, 0:1], -1.0))
        d3 = env.op("vector", lambda e: e.tensor_scalar(
            b.m_oh[:, :], b.m_logit[:, :], b.m_stat[:, 0:1], None,
            ALU.is_equal))
        d4 = env.op("vector", lambda e: e.tensor_mul(
            b.m_exp[:, :], b.m_oh[:, :], consts["iota8r"]))
        d5 = env.op("vector", lambda e: e.reduce_max(
            b.m_stat[:, 2:3], b.m_exp[:, :], axis=AX.X))
        d6 = env.op("vector", lambda e: e.tensor_scalar(
            b.m_stat[:, 3:4], b.m_stat[:, 2:3], -1.0, 7.0, ALU.mult, ALU.add))
        d7 = env.op("scalar", lambda e: e.activation(
            b.m_exp[:, :], b.m_logit[:, :], AF.Exp, bias=b.m_stat[:, 1:2]),
            deps=merge(d2, d5))
        d8 = env.op("vector", lambda e: e.reduce_sum(
            b.m_stat[:, 4:5], b.m_exp[:, :], axis=AX.X), deps=d7)
        d9 = env.op("vector", lambda e: e.reciprocal(
            b.m_stat[:, 5:6], b.m_stat[:, 4:5]))
        d10 = env.op("vector", lambda e: e.tensor_scalar(
            b.m_oh[:, :], consts["iota8"], b.m_stat[:, 3:4], None,
            ALU.is_equal), deps=d6)
        d11 = env.op("tensor", lambda e: e.matmul(
            b.ps_aux[:, 0:8], consts["cumt"], b.m_oh[:, :], start=True,
            stop=True), deps=d10)
        d12 = env.op("tensor", lambda e: e.matmul(
            b.ps_aux[0:1, 8:16], consts["ones128c"], b.m_oh[:, :],
            start=True, stop=True), deps=None)
        d13 = env.op("vector", lambda e: e.tensor_copy(
            b.m_pref[:, :], b.ps_aux[:, 0:8]), deps=d12)
        d14 = env.op("vector", lambda e: e.tensor_copy(
            b.m_tot[:, :], b.ps_aux[0:1, 8:16]))
        d15 = env.op("tensor", lambda e: e.matmul(
            b.ps_sc[:, 0:8], consts["onesrow"], b.m_runbase[:, :],
            start=True, stop=True), deps=merge(d13, d14))
        d16 = env.op("vector", lambda e: e.tensor_add(
            b.m_exp[:, :], b.m_pref[:, :], b.ps_sc[:, 0:8]), deps=d15)
        d17 = env.op("vector", lambda e: e.tensor_mul(
            b.m_exp[:, :], b.m_exp[:, :], b.m_oh[:, :]))
        d18 = env.op("vector", lambda e: e.reduce_sum(
            b.m_stat[:, 6:7], b.m_exp[:, :], axis=AX.X))
        d19 = env.op("vector", lambda e: e.tensor_scalar(
            b.m_stat[:, 6:7], b.m_stat[:, 6:7], -1.0, None, ALU.add))
        d20 = env.op("vector", lambda e: e.tensor_add(
            b.m_runbase[:, :], b.m_runbase[:, :], b.m_tot[:, :]))
        d21 = env.op("vector", lambda e: e.tensor_copy(
            b.m_rw[:, 0:1], b.m_stat[:, 3:4]), deps=d9)
        d22 = env.op("vector", lambda e: e.tensor_copy(
            b.m_rw[:, 1:2], b.m_stat[:, 5:6]))
        d23 = env.op("vector", lambda e: e.tensor_copy(
            b.m_rw[:, 2:3], b.m_stat[:, 6:7]), deps=d19)
        dw = env.dma("sync", lambda e, r=128 * t: e.dma_start(
            out=rout_hbm[r:r + 128, :], in_=b.m_rw[:, :]),
            deps=merge(d21, d22, d23))
        prev = merge(dw, dst, d20)
    dcnt = env.dma("sync", lambda e: e.dma_start(
        out=cnt_hbm[:, :], in_=b.m_runbase[:, :]), deps=prev)
    return dcnt


def emit_moe_base(env, b, cnts_hbm, mym_hbm, mycnt_hbm, consts, deps=None):
    d1 = env.dma("sync", lambda e: e.dma_start(out=b.m_cnts8[:, :],
                                               in_=cnts_hbm[:, :]), deps=deps)
    d1b = env.dma("sync", lambda e: e.dma_start(out=b.m_mym[:, :],
                                                in_=mym_hbm[:, :]))
    d1c = env.dma("sync", lambda e: e.dma_start(out=b.m_cntf[:, :],
                                                in_=mycnt_hbm[:, :]))
    d2 = env.op("tensor", lambda e: e.matmul(
        b.ps_aux[0:1, 0:8], b.m_mym[:, :], b.m_cnts8[:, :], start=True,
        stop=True), deps=merge(d1, d1b))
    d3 = env.op("vector", lambda e: e.tensor_copy(b.m_base[:, :],
                                                  b.ps_aux[0:1, 0:8]), deps=d2)
    d4 = env.op("tensor", lambda e: e.matmul(
        b.ps_aux[:, 8:16], consts["onesrow"], b.m_base[:, :], start=True,
        stop=True), deps=d3)
    d5 = env.op("vector", lambda e: e.tensor_copy(b.m_basebc[:, :],
                                                  b.ps_aux[:, 8:16]), deps=d4)
    d6 = env.op("vector", lambda e: e.tensor_copy(b.m_cnti[:, :],
                                                  b.m_cntf[:, :]), deps=d1c)
    return merge(d5, d6)


def emit_moe_zero_ein(env, b, ein_hbm, Tc, deps=None):
    dz = env.op("vector", lambda e: e.memset(b.m_h[:, :], 0.0), deps=deps)
    er = ein_hbm.ap().rearrange("(a p) n -> p a n", p=128)
    na = 8 * Tc // 128
    last = None
    for a0 in range(0, na, 4):
        last = env.dma("gpsimd", lambda e, a0=a0: e.dma_start(
            out=er[:, a0:a0 + 4, :],
            in_=b.m_h.ap().rearrange("p (a n) -> p a n", n=1024)), deps=dz)
        dz = None
    return last


def emit_moe_scatter(env, b, t2_hbm, t2row0, rout_hbm, ein_hbm, consts,
                     n_tiles, Tc):
    prev = None
    for t in range(n_tiles):
        d1 = env.dma("sync", lambda e, r=128 * t: e.dma_start(
            out=b.m_rw[:, :], in_=rout_hbm[r:r + 128, :]), deps=prev)
        d2 = env.dma("sync", lambda e, r=t2row0 + 128 * t: e.dma_start(
            out=b.m_t2s[:, :], in_=t2_hbm[r:r + 128, :]))
        d3 = env.op("vector", lambda e: e.tensor_scalar(
            b.m_slotf[:, :], b.m_rw[:, 0:1], float(Tc), b.m_rw[:, 2:3],
            ALU.mult, ALU.add), deps=d1)
        d4 = env.op("vector", lambda e: e.tensor_copy(
            b.m_sloti[:, :], b.m_slotf[:, :]))
        d5 = env.dma("gpsimd", lambda e: e.indirect_dma_start(
            ein_hbm[:, :], bass.IndirectOffsetOnAxis(b.m_sloti[:, 0:1], 0),
            b.m_t2s[:, :], None), deps=merge(d4, d2))
        prev = d5
    return prev


def emit_moe_expert(env, b, mm, ein_hbm, eout_hbm, w1_hbm, w2_hbm, ident,
                    Tc, n_exp=8):
    import os
    XSUB = int(os.environ.get("MOE_XSUB", "9"))
    NEXP = int(os.environ.get("MOE_NEXP", str(n_exp)))
    NJ = int(os.environ.get("MOE_NJ", "8"))
    NK = int(os.environ.get("MOE_NK", "0"))
    nc = env.nc
    if not hasattr(env, "_moe_regs"):
        env._moe_regs = {}
    regs = env._moe_regs

    def loadreg(engine, e_idx):
        def go(eng, engine=engine, e_idx=e_idx):
            key = (engine, "cnt")
            if key not in regs:
                regs[key] = eng.alloc_register(name=f"cnt_{engine}")
                regs[(engine, "k")] = eng.alloc_register(name=f"k_{engine}")
            eng.reg_load(regs[key], b.m_cnti[0:1, e_idx:e_idx + 1])
        env.raw(engine, go)

    w1r = w1_hbm.ap().rearrange("E (c p) n -> p E c n", p=128)
    w2r = w2_hbm.ap().rearrange("E (c p) n -> p E c n", p=128)
    for e_idx in range(NEXP):
        dw1 = env.dma("gpsimd", lambda e, e_idx=e_idx: e.dma_start(
            out=b.m_w1.ap().rearrange("p (c n) -> p c n", n=4096),
            in_=w1r[:, e_idx, :, :]), deps=env.all_now())
        dw2 = env.dma("gpsimd", lambda e, e_idx=e_idx: e.dma_start(
            out=b.m_w2.ap().rearrange("p (c n) -> p c n", n=1024),
            in_=w2r[:, e_idx, :, :]))
        for engine in ENGINES:
            loadreg(engine, e_idx)
        if XSUB < 2:
            continue
        for k in range(NK if NK else Tc // 128):
            def cond(eng, k=k):
                kreg = regs[(eng_name_of(eng), "k")]
                eng.reg_mov(kreg, 128 * k)
                return (kreg, regs[(eng_name_of(eng), "cnt")], "IS_LT")
            with Branch(env, cond):
                row = e_idx * Tc + 128 * k
                d1 = env.dma("sync", lambda e, row=row: e.dma_start(
                    out=b.m_ein[:, :], in_=ein_hbm[row:row + 128, :]),
                    deps=merge(dw1 if k == 0 else None,
                               dw2 if k == 0 else None,
                               env.now("tensor"), env.now("vector"),
                               env.now("scalar")))
                if XSUB < 3:
                    continue
                blocks = [(b.m_ein[:, 128 * c:128 * (c + 1)],
                           b.m_einT[:, 128 * c:128 * (c + 1)])
                          for c in range(8)]
                dtr = emit_transpose_blocks(env, b, blocks, ident, deps=d1)
                if XSUB < 4:
                    continue
                for j in range(NJ):
                    mms = []
                    for c in range(8):
                        mms.append(lambda e, p, st, sp, c=c, j=j: e.matmul(
                            p, b.m_einT[:, 128 * c:128 * (c + 1)],
                            b.m_w1[:, 4096 * c + 512 * j:
                                   4096 * c + 512 * j + 512],
                            start=st, stop=sp))
                    if os.environ.get("MOE_RELU", "1") == "1":
                        mm.group(mms, lambda e, p, j=j: e.activation(
                            b.m_h[:, 512 * j:512 * (j + 1)], p, AF.Relu),
                            deps=dtr if j == 0 else None,
                            evac_engine="scalar")
                    else:
                        mm.group(mms, lambda e, p, j=j: e.tensor_copy(
                            b.m_h[:, 512 * j:512 * (j + 1)], p),
                            deps=dtr if j == 0 else None)
                if XSUB < 5:
                    continue
                blocks = [(b.m_h[:, 128 * fc:128 * (fc + 1)],
                           b.m_hT[:, 128 * fc:128 * (fc + 1)])
                          for fc in range(32)]
                dtr2 = emit_transpose_blocks(env, b, blocks, ident,
                                             deps=env.now("scalar"))
                for j2 in range(2):
                    mms = []
                    for fc in range(32):
                        mms.append(lambda e, p, st, sp, fc=fc, j2=j2:
                                   e.matmul(
                                       p, b.m_hT[:, 128 * fc:128 * (fc + 1)],
                                       b.m_w2[:, 1024 * fc + 512 * j2:
                                              1024 * fc + 512 * j2 + 512],
                                       start=st, stop=sp))
                    mm.group(mms, lambda e, p, j2=j2: e.tensor_copy(
                        b.m_out[:, 512 * j2:512 * (j2 + 1)], p),
                        deps=dtr2 if j2 == 0 else None)
                dst = env.dma("sync", lambda e, row=row: e.dma_start(
                    out=eout_hbm[row:row + 128, :], in_=b.m_out[:, :]),
                    deps=env.now("vector"))
                env._wait("sync", dst)
            for i, d in enumerate(mm.evac_dep):
                if d:
                    mm.evac_dep[i] = {kk: env.count[kk] for kk in d}
    env.barrier()


def eng_name_of(eng):
    import concourse.mybir as mb
    m = {mb.EngineType.PE: "tensor", mb.EngineType.Activation: "scalar",
         mb.EngineType.Pool: "gpsimd", mb.EngineType.DVE: "vector",
         mb.EngineType.SP: "sync"}
    return m[eng.engine]


def emit_moe_combine(env, b, x_hbm, row0, xo_hbm, t2_hbm, t2row0, rout_hbm,
                     eout_hbm, consts, n_tiles, Tc, cap):
    prev = None
    for t in range(n_tiles):
        d1 = env.dma("sync", lambda e, r=128 * t: e.dma_start(
            out=b.m_rw[:, :], in_=rout_hbm[r:r + 128, :]), deps=prev)
        d2 = env.dma("sync", lambda e, r=t2row0 + 128 * t: e.dma_start(
            out=b.m_t2s[:, :], in_=t2_hbm[r:r + 128, :]))
        d2b = env.dma("sync", lambda e, r=row0 + 128 * t: e.dma_start(
            out=b.xf[:, :], in_=x_hbm[r:r + 128, :]))
        d3 = env.op("vector", lambda e: e.tensor_scalar(
            b.m_slotf[:, :], b.m_rw[:, 0:1], float(Tc), b.m_rw[:, 2:3],
            ALU.mult, ALU.add), deps=d1)
        d4 = env.op("vector", lambda e: e.tensor_copy(
            b.m_sloti[:, :], b.m_slotf[:, :]))
        d5 = env.dma("gpsimd", lambda e: e.indirect_dma_start(
            b.m_eo[:, :], None,
            eout_hbm[:, :], bass.IndirectOffsetOnAxis(b.m_sloti[:, 0:1], 0)),
            deps=d4)
        d6 = env.op("vector", lambda e: e.tensor_scalar(
            b.m_oh[:, :], consts["iota8"], b.m_rw[:, 0:1], None,
            ALU.is_equal), deps=d1)
        d7 = env.op("vector", lambda e: e.tensor_mul(
            b.m_exp[:, :], b.m_oh[:, :], b.m_basebc[:, :]))
        d8 = env.op("vector", lambda e: e.reduce_sum(
            b.m_stat[:, 8:9], b.m_exp[:, :], axis=AX.X))
        d9 = env.op("vector", lambda e: e.tensor_add(
            b.m_stat[:, 9:10], b.m_stat[:, 8:9], b.m_rw[:, 2:3]))
        d10 = env.op("vector", lambda e: e.tensor_scalar(
            b.m_stat[:, 10:11], b.m_stat[:, 9:10], float(cap), None,
            ALU.is_lt), deps=d9)
        d11 = env.op("vector", lambda e: e.scalar_tensor_tensor(
            b.m_eo[:, :], b.m_eo[:, :], b.m_rw[:, 1:2], b.m_t2s[:, :],
            ALU.mult, ALU.subtract), deps=merge(d5, d2, d10))
        d12 = env.op("vector", lambda e: e.scalar_tensor_tensor(
            b.m_eo[:, :], b.m_eo[:, :], b.m_stat[:, 10:11], b.m_t2s[:, :],
            ALU.mult, ALU.add))
        d13 = env.op("vector", lambda e: e.tensor_add(
            b.xo[:, :], b.xf[:, :], b.m_eo[:, :]), deps=d2b)
        d14 = env.dma("sync", lambda e, r=row0 + 128 * t: e.dma_start(
            out=xo_hbm[r:r + 128, :], in_=b.xo[:, :]), deps=d13)
        prev = merge(d14, d12)
    return prev

import ml_dtypes
from contextlib import ExitStack
import concourse.bass as bass

SEQK = 2048
TCK = 4096
NTK = TCK // 128
CAPK = 5120



def host_const_arrays():
    import numpy as np
    cc = {}
    cc["c_ident_bf"] = np.eye(128, dtype=ml_dtypes.bfloat16)
    cc["c_ident_f"] = np.eye(128, dtype=np.float32)
    cum = np.triu(np.ones((64, 64), np.float32))
    z = np.zeros((64, 64), np.float32)
    cc["c_cum2"] = np.block([[cum, z], [z, cum]])
    sel = np.zeros((128, 128), np.float32)
    sel[63, 0:64] = 1.0
    sel[127, 64:128] = 1.0
    cc["c_sellast"] = sel
    om = np.zeros((16, 1024), np.float32)
    for h in range(16):
        om[h, 64 * h:64 * (h + 1)] = 1.0
    cc["c_onesmask"] = om
    tril = np.tril(np.ones((64, 64), np.float32))
    strict = np.tril(np.ones((64, 64), np.float32), -1)
    trilw = np.tile(np.concatenate([tril, tril], 0), (1, 8))
    strictw = np.tile(np.concatenate([strict, strict], 0), (1, 8))
    triuw = np.tile(np.concatenate([tril.T, tril.T], 0), (1, 8))
    cc["c_masks"] = np.concatenate([trilw, strictw, triuw, -triuw], 1)
    selb = np.zeros((128, 128), np.float32)
    selb[0, 0:64] = 1.0
    selb[64, 64:128] = 1.0
    cc["c_selb"] = selb
    scal = np.zeros((128, 8), np.float32)
    scal[:, 0] = 1e-5
    scal[:, 1] = 1e-6
    scal[:, 2] = 64e-6
    scal[:, 3] = 1e-5
    cc["c_scal"] = scal
    cc["c_iota8"] = np.tile(np.arange(8, dtype=np.float32), (128, 1))
    cc["c_iota8r"] = np.tile(np.arange(7, -1, -1, dtype=np.float32),
                             (128, 1))
    cc["c_cumt"] = np.triu(np.ones((128, 128), np.float32))
    cc["c_ones128c"] = np.ones((128, 1), np.float32)
    cc["c_onesrow"] = np.ones((1, 128), np.float32)
    nm = np.zeros((128, 2048), np.float32)
    for v in range(4):
        for p in range(128):
            nm[p, 512 * v: 512 * v + 512] = np.where(
                np.arange(512) <= 128 * v + p, 0.0, -1e30)
    cc["c_negmask"] = nm
    return cc


def _declare_consts(nc, names):
    shapes = {
        "c_ident_bf": ([128, 128], BF16), "c_ident_f": ([128, 128], F32),
        "c_cum2": ([128, 128], F32), "c_sellast": ([128, 128], F32),
        "c_onesmask": ([16, 1024], F32), "c_masks": ([128, 2048], F32),
        "c_selb": ([128, 128], F32), "c_scal": ([128, 8], F32),
        "c_iota8": ([128, 8], F32), "c_iota8r": ([128, 8], F32),
        "c_cumt": ([128, 128], F32), "c_ones128c": ([128, 1], F32),
        "c_onesrow": ([1, 128], F32), "c_negmask": ([128, 2048], F32),
    }
    out = {}
    for n in names:
        sh, dt = shapes[n]
        out[n] = nc.dram_tensor(n, sh, dt, kind="ExternalInput")
    return out


def _alloc_prep_bufs(nc, b):
    b.m_tT = nc.alloc_sbuf_tensor("m_tT", [128, 1024], BF16)
    b.m_logit = nc.alloc_sbuf_tensor("m_logit", [128, 8], F32)
    b.m_exp = nc.alloc_sbuf_tensor("m_exp", [128, 8], F32)
    b.m_stat = nc.alloc_sbuf_tensor("m_stat", [128, 16], F32)
    b.m_oh = nc.alloc_sbuf_tensor("m_oh", [128, 8], F32)
    b.m_pref = nc.alloc_sbuf_tensor("m_pref", [128, 8], F32)
    b.m_runbase = nc.alloc_sbuf_tensor("m_runbase", [1, 8], F32)
    b.m_tot = nc.alloc_sbuf_tensor("m_tot", [1, 8], F32)
    b.m_rw = nc.alloc_sbuf_tensor("m_rw", [128, 4], F32)


def _load_consts(env, hd, sbufs):
    for n, ss in sbufs.items():
        env.dma("sync", lambda e, hh=hd[n], ss=ss: e.dma_start(
            out=ss[:, :], in_=hh[:, :]))



def build_launch_A():
    nc = bass.Bass("TRN2", target_bir_lowering=False)
    x_h = nc.dram_tensor("x", [TCK, 1024], F32, kind="ExternalInput")
    wn = ["wq", "wk", "wv", "wg", "wo"]
    H = {n: nc.dram_tensor(n, [1024, 1024], F32, kind="ExternalInput")
         for n in wn}
    wb_h = nc.dram_tensor("wb", [1024, 16], F32, kind="ExternalInput")
    wa_h = nc.dram_tensor("wa", [1024, 16], F32, kind="ExternalInput")
    dtb_h = nc.dram_tensor("dtb", [128, 16], F32, kind="ExternalInput")
    alog_h = nc.dram_tensor("alog", [128, 16], F32, kind="ExternalInput")
    nw_h = nc.dram_tensor("nw64", [128, 64], F32, kind="ExternalInput")
    wr_h = nc.dram_tensor("wr", [1024, 8], F32, kind="ExternalInput")
    hd = _declare_consts(nc, ["c_ident_bf", "c_ident_f", "c_cum2",
                              "c_sellast", "c_onesmask", "c_masks", "c_selb",
                              "c_scal", "c_iota8", "c_iota8r", "c_cumt",
                              "c_ones128c", "c_onesrow"])
    xo_h = nc.dram_tensor("xo", [TCK, 1024], F32, kind="ExternalOutput")
    t2_h = nc.dram_tensor("t2", [TCK, 1024], BF16, kind="ExternalOutput")
    rout_h = nc.dram_tensor("rout", [TCK, 4], F32, kind="ExternalOutput")
    cnts_h = nc.dram_tensor("cnts", [1, 8], F32, kind="ExternalOutput")

    env = Env(nc)
    b = alloc_psum(env)
    alloc_attna_bufs(nc, b, seg=512)
    b.tn = nc.alloc_sbuf_tensor("tn", [128, 1024], BF16)
    b.xf = nc.alloc_sbuf_tensor("xf", [128, 1024], F32)
    b.xo = nc.alloc_sbuf_tensor("xo_sb", [128, 1024], F32)
    b.ln_mean = nc.alloc_sbuf_tensor("ln_mean", [128, 2], F32)
    b.ln_var = nc.alloc_sbuf_tensor("ln_var", [128, 2], F32)
    b.ln_rstd = nc.alloc_sbuf_tensor("ln_rstd", [128, 1], F32)
    b.ln_xm = nc.alloc_sbuf_tensor("ln_xm", [128, 1024], F32)
    b.ln_sq = nc.alloc_sbuf_tensor("ln_sq", [128, 1024], F32)
    _alloc_prep_bufs(nc, b)
    w_buf = nc.alloc_sbuf_tensor("w_buf", [128, 8192], BF16)
    wa_sb = nc.alloc_sbuf_tensor("wa_sb", [128, 128], BF16)
    wb_sb = nc.alloc_sbuf_tensor("wb_sb", [128, 128], BF16)
    wr_sb = nc.alloc_sbuf_tensor("wr_sb", [128, 64], BF16)
    cs = {n: nc.alloc_sbuf_tensor(n + "_s", hd[n].shape, hd[n].dtype)
          for n in hd}
    dtb = nc.alloc_sbuf_tensor("dtb_s", [128, 16], F32)
    alog = nc.alloc_sbuf_tensor("alog_s", [128, 16], F32)
    negexpA = nc.alloc_sbuf_tensor("negexpA_s", [128, 16], F32)
    nw64 = nc.alloc_sbuf_tensor("nw64_s", [128, 64], F32)

    with ExitStack() as stack:
        env.make_sems(stack)
        with nc.Block() as block:
            _load_consts(env, hd, cs)
            for (hh, ss) in ((dtb_h, dtb), (alog_h, alog), (nw_h, nw64)):
                env.dma("sync", lambda e, hh=hh, ss=ss: e.dma_start(
                    out=ss[:, :], in_=hh[:, :]))
            emit_load_weight(env, wa_h, wa_sb, 1024, 16)
            emit_load_weight(env, wb_h, wb_sb, 1024, 16)
            emit_load_weight(env, wr_h, wr_sb, 1024, 8)
            env.barrier()
            d1 = env.op("scalar", lambda e: e.activation(
                negexpA[:, :], alog[:, :], AF.Exp))
            env.op("vector", lambda e: e.tensor_scalar_mul(
                negexpA[:, :], negexpA[:, :], -1.0), deps=d1)
            env.barrier()
            masks = cs["c_masks"]
            c_scal = cs["c_scal"]
            selb = cs["c_selb"]
            consts = {
                "ident": cs["c_ident_bf"][:, :],
                "identf": cs["c_ident_f"][:, :],
                "cum2": cs["c_cum2"][:, :], "sellast": cs["c_sellast"][:, :],
                "onesmask": cs["c_onesmask"],
                "trilw": masks[:, 0:512], "strictw": masks[:, 512:1024],
                "triuw": masks[:, 1024:1536],
                "negtriuw": masks[:, 1536:2048],
                "selb0": selb[:, 0:64], "selb64": selb[:, 64:128],
                "dtb": dtb[:, :], "negexpA": negexpA[:, :],
                "normw64": nw64[:, :],
                "eps_ln": c_scal[:, 0:1], "eps_k": c_scal[:, 1:2],
                "eps_q64": c_scal[:, 2:3], "eps_rms": c_scal[:, 3:4],
                "iota8": cs["c_iota8"][:, :], "iota8r": cs["c_iota8r"][:, :],
                "cumt": cs["c_cumt"][:, :],
                "ones128c": cs["c_ones128c"][:, :],
                "onesrow": cs["c_onesrow"][:, :],
            }
            W = {"wq_h": H["wq"], "wk_h": H["wk"], "wv_h": H["wv"],
                 "wg_h": H["wg"], "wo_h": H["wo"], "wa": wa_sb, "wb": wb_sb,
                 "w_buf": w_buf}
            mm = MMGroups(env, b)
            for bi in range(2):
                emit_attna_batch(env, b, mm, x_h, xo_h, SEQK * bi, W, consts,
                                 seq=SEQK, seg=512)
                env.barrier()
            emit_moe_prep(env, b, mm, xo_h, 0, t2_h, 0, rout_h, cnts_h,
                          wr_sb, consts, NTK, cs["c_ident_bf"][:, :])
            env.barrier()
            env.finalize(block)
    return env



def build_launch_B():
    nc = bass.Bass("TRN2", target_bir_lowering=False,
                   detect_race_conditions=False)
    x_h = nc.dram_tensor("x", [TCK, 1024], F32, kind="ExternalInput")
    t2_h = nc.dram_tensor("t2", [TCK, 1024], BF16, kind="ExternalInput")
    rout_h = nc.dram_tensor("rout", [TCK, 4], F32, kind="ExternalInput")
    cnts8_h = nc.dram_tensor("cnts8", [8, 8], F32, kind="ExternalInput")
    mym_h = nc.dram_tensor("mym", [8, 1], F32, kind="ExternalInput")
    mycnt_h = nc.dram_tensor("mycnt", [1, 8], F32, kind="ExternalInput")
    w1_h = nc.dram_tensor("w1", [8, 1024, 4096], F32, kind="ExternalInput")
    w2_h = nc.dram_tensor("w2", [8, 4096, 1024], F32, kind="ExternalInput")
    hd = _declare_consts(nc, ["c_ident_bf", "c_iota8", "c_onesrow"])
    xo_h = nc.dram_tensor("xo", [TCK, 1024], F32, kind="ExternalOutput")
    ein_hbm = nc.dram_tensor("ein_s", [8 * TCK, 1024], BF16)
    eout_hbm = nc.dram_tensor("eout_s", [8 * TCK, 1024], BF16)

    env = Env(nc)
    b = alloc_psum(env)
    alloc_moe_bufs(nc, b, TCK)
    b.xf = nc.alloc_sbuf_tensor("xf", [128, 1024], F32)
    b.xo = nc.alloc_sbuf_tensor("xo_sb", [128, 1024], F32)
    cs = {n: nc.alloc_sbuf_tensor(n + "_s", hd[n].shape, hd[n].dtype)
          for n in hd}

    with ExitStack() as stack:
        env.make_sems(stack)
        with nc.Block() as block:
            _load_consts(env, hd, cs)
            env.barrier()
            consts = {
                "ident": cs["c_ident_bf"][:, :],
                "iota8": cs["c_iota8"][:, :],
                "onesrow": cs["c_onesrow"][:, :],
            }
            mm = MMGroups(env, b)
            emit_moe_base(env, b, cnts8_h, mym_h, mycnt_h, consts)
            env.barrier()
            emit_moe_scatter(env, b, t2_h, 0, rout_h, ein_hbm, consts,
                             NTK, TCK)
            env.barrier()
            emit_moe_expert(env, b, mm, ein_hbm, eout_hbm, w1_h, w2_h,
                            cs["c_ident_bf"][:, :], TCK)
            env.barrier()
            emit_moe_combine(env, b, x_h, 0, xo_h, t2_h, 0, rout_h,
                             eout_hbm, consts, NTK, TCK, CAPK)
            env.barrier()
            env.finalize(block)
    return env



def build_launch_C():
    nc = bass.Bass("TRN2", target_bir_lowering=False)
    x_h = nc.dram_tensor("x", [TCK, 1024], F32, kind="ExternalInput")
    wq_h = nc.dram_tensor("wq", [1024, 1024], F32, kind="ExternalInput")
    wk_h = nc.dram_tensor("wk", [1024, 1024], F32, kind="ExternalInput")
    wv_h = nc.dram_tensor("wv", [1024, 1024], F32, kind="ExternalInput")
    wo_h = nc.dram_tensor("wo", [1024, 1024], F32, kind="ExternalInput")
    wr_h = nc.dram_tensor("wr", [1024, 8], F32, kind="ExternalInput")
    hd = _declare_consts(nc, ["c_ident_bf", "c_negmask", "c_scal",
                              "c_iota8", "c_iota8r", "c_cumt", "c_ones128c",
                              "c_onesrow"])
    xo_h = nc.dram_tensor("xo", [TCK, 1024], F32, kind="ExternalOutput")
    t2_h = nc.dram_tensor("t2", [TCK, 1024], BF16, kind="ExternalOutput")
    rout_h = nc.dram_tensor("rout", [TCK, 4], F32, kind="ExternalOutput")
    cnts_h = nc.dram_tensor("cnts", [1, 8], F32, kind="ExternalOutput")

    env = Env(nc)
    b = alloc_psum(env)
    alloc_attnb_bufs(nc, b, seq=SEQK)
    _alloc_prep_bufs(nc, b)
    w_buf = nc.alloc_sbuf_tensor("w_buf", [128, 8192], BF16)
    wr_sb = nc.alloc_sbuf_tensor("wr_sb", [128, 64], BF16)
    cs = {n: nc.alloc_sbuf_tensor(n + "_s", hd[n].shape, hd[n].dtype)
          for n in hd}

    with ExitStack() as stack:
        env.make_sems(stack)
        with nc.Block() as block:
            _load_consts(env, hd, cs)
            emit_load_weight(env, wr_h, wr_sb, 1024, 8)
            env.barrier()
            c_scal = cs["c_scal"]
            consts = {
                "ident": cs["c_ident_bf"][:, :],
                "iota8": cs["c_iota8"][:, :], "iota8r": cs["c_iota8r"][:, :],
                "cumt": cs["c_cumt"][:, :],
                "ones128c": cs["c_ones128c"][:, :],
                "onesrow": cs["c_onesrow"][:, :],
                "eps_ln": c_scal[:, 0:1],
            }
            mm = MMGroups(env, b)
            for bi in range(2):
                emit_attnb_batch(env, b, mm, x_h, xo_h, SEQK * bi, wq_h,
                                 wk_h, wv_h, wo_h, w_buf,
                                 cs["c_ident_bf"][:, :], cs["c_negmask"],
                                 c_scal[:, 0:1], seq=SEQK)
                env.barrier()
            emit_moe_prep(env, b, mm, xo_h, 0, t2_h, 0, rout_h, cnts_h,
                          wr_sb, consts, NTK, cs["c_ident_bf"][:, :])
            env.barrier()
            env.finalize(block)
    return env



_DSZ = {F32: 4, BF16: 2, I32: 4, U32: 4}


class ArenaNC:

    def __init__(self, nc, base, suffix, alias=None):
        self.nc = nc
        self.cur = base
        self.suffix = suffix
        self.alias = alias or {}
        self.offs = {}

    def alloc_sbuf_tensor(self, name, shape, dtype):
        if name in self.alias:
            off = self.offs[self.alias[name]]
        else:
            off = self.cur
            nbytes = shape[1] * _DSZ.get(dtype, 4)
            self.cur = (off + nbytes + 31) & ~31
        self.offs[name] = off
        return self.nc.alloc_sbuf_tensor_at(name + self.suffix, shape, dtype,
                                            offset=off)


def _use(b, d):
    for k, v in d.items():
        setattr(b, k, v)


def _alloc_lnset(an, b):
    d = {}
    d["tn"] = an.alloc_sbuf_tensor("tn", [128, 1024], BF16)
    d["xf"] = an.alloc_sbuf_tensor("xf", [128, 1024], F32)
    d["xo"] = an.alloc_sbuf_tensor("xo_sb", [128, 1024], F32)
    d["ln_mean"] = an.alloc_sbuf_tensor("ln_mean", [128, 2], F32)
    d["ln_var"] = an.alloc_sbuf_tensor("ln_var", [128, 2], F32)
    d["ln_rstd"] = an.alloc_sbuf_tensor("ln_rstd", [128, 1], F32)
    d["ln_xm"] = an.alloc_sbuf_tensor("ln_xm", [128, 1024], F32)
    d["ln_sq"] = an.alloc_sbuf_tensor("ln_sq", [128, 1024], F32)
    _use(b, d)
    return d


def build_fused():
    nc = bass.Bass("TRN2", target_bir_lowering=False,
                   detect_race_conditions=False)
    x_h = nc.dram_tensor("x", [TCK, 1024], F32, kind="ExternalInput")
    wn = ["wq", "wk", "wv", "wg", "wo"]
    H = {n: nc.dram_tensor(n, [1024, 1024], F32, kind="ExternalInput")
         for n in wn}
    wb_h = nc.dram_tensor("wb", [1024, 16], F32, kind="ExternalInput")
    wa_h = nc.dram_tensor("wa", [1024, 16], F32, kind="ExternalInput")
    dtb_h = nc.dram_tensor("dtb", [128, 16], F32, kind="ExternalInput")
    alog_h = nc.dram_tensor("alog", [128, 16], F32, kind="ExternalInput")
    nw_h = nc.dram_tensor("nw64", [128, 64], F32, kind="ExternalInput")
    wr_h = nc.dram_tensor("wr", [1024, 8], F32, kind="ExternalInput")
    w1_h = nc.dram_tensor("w1", [8, 1024, 4096], F32, kind="ExternalInput")
    w2_h = nc.dram_tensor("w2", [8, 4096, 1024], F32, kind="ExternalInput")
    B = {n: nc.dram_tensor("b" + n[1], [1024, 1024], F32,
                           kind="ExternalInput") for n in ["wq", "wk", "wv",
                                                           "wo"]}
    mym_h = nc.dram_tensor("mym", [8, 1], F32, kind="ExternalInput")
    hd = _declare_consts(nc, ["c_ident_bf", "c_ident_f", "c_cum2",
                              "c_sellast", "c_onesmask", "c_masks", "c_selb",
                              "c_scal", "c_iota8", "c_iota8r", "c_cumt",
                              "c_ones128c", "c_onesrow", "c_negmask"])
    xo_h = nc.dram_tensor("xo", [TCK, 1024], F32, kind="ExternalOutput")
    xo1_h = nc.dram_tensor("xo1_s", [TCK, 1024], F32)
    xo2_h = nc.dram_tensor("xo2_s", [TCK, 1024], F32)
    xo3_h = nc.dram_tensor("xo3_s", [TCK, 1024], F32)
    t2_h = nc.dram_tensor("t2_s", [TCK, 1024], BF16)
    rout_h = nc.dram_tensor("rout_s", [TCK, 4], F32)
    ein_hbm = nc.dram_tensor("ein_s", [8 * TCK, 1024], BF16)
    eout_hbm = nc.dram_tensor("eout_s", [8 * TCK, 1024], BF16)
    cc_in = [nc.dram_tensor(f"ccin{i}_s", [1, 8], F32) for i in range(2)]
    cc_out = [nc.dram_tensor(f"ccout{i}_s", [8, 8], F32) for i in range(2)]

    env = Env(nc)
    b = alloc_psum(env)
    cs = {n: nc.alloc_sbuf_tensor(n + "_s", hd[n].shape, hd[n].dtype)
          for n in hd}
    wa_sb = nc.alloc_sbuf_tensor("wa_sb", [128, 128], BF16)
    wb_sb = nc.alloc_sbuf_tensor("wb_sb", [128, 128], BF16)
    wr_sb = nc.alloc_sbuf_tensor("wr_sb", [128, 64], BF16)
    dtb = nc.alloc_sbuf_tensor("dtb_s", [128, 16], F32)
    alog = nc.alloc_sbuf_tensor("alog_s", [128, 16], F32)
    negexpA = nc.alloc_sbuf_tensor("negexpA_s", [128, 16], F32)
    nw64 = nc.alloc_sbuf_tensor("nw64_s", [128, 64], F32)
    _alloc_prep_bufs(nc, b)
    arena0 = (nc.sbuf_base + 31) & ~31
    cap = nc.sbuf_top

    anA = ArenaNC(nc, arena0, "_pa", alias={"a_sqsc": "ln_xm"})
    lnA = _alloc_lnset(anA, b)
    w_bufA = anA.alloc_sbuf_tensor("w_buf", [128, 8192], BF16)
    alloc_attna_bufs(anA, b, seg=512)
    assert anA.cur <= cap, ("A arena overflow", anA.cur, cap)

    anB = ArenaNC(nc, arena0, "_pb")
    lnB = {"xf": anB.alloc_sbuf_tensor("xf", [128, 1024], F32),
           "xo": anB.alloc_sbuf_tensor("xo_sb", [128, 1024], F32)}
    moeB = {}
    moeB["m_ein"] = anB.alloc_sbuf_tensor("m_ein", [128, 1024], BF16)
    moeB["m_einT"] = anB.alloc_sbuf_tensor("m_einT", [128, 1024], BF16)
    moeB["m_h"] = anB.alloc_sbuf_tensor("m_h", [128, 4096], BF16)
    moeB["m_hT"] = anB.alloc_sbuf_tensor("m_hT", [128, 4096], BF16)
    moeB["m_out"] = anB.alloc_sbuf_tensor("m_out", [128, 1024], BF16)
    moeB["m_eo"] = anB.alloc_sbuf_tensor("m_eo", [128, 1024], BF16)
    moeB["m_t2s"] = anB.alloc_sbuf_tensor("m_t2s", [128, 1024], BF16)
    moeB["m_slotf"] = anB.alloc_sbuf_tensor("m_slotf", [128, 1], F32)
    moeB["m_sloti"] = anB.alloc_sbuf_tensor("m_sloti", [128, 1], I32)
    moeB["m_w1"] = anB.alloc_sbuf_tensor("m_w1", [128, 32768], BF16)
    moeB["m_w2"] = anB.alloc_sbuf_tensor("m_w2", [128, 32768], BF16)
    moeB["m_base"] = anB.alloc_sbuf_tensor("m_base", [1, 8], F32)
    moeB["m_basebc"] = anB.alloc_sbuf_tensor("m_basebc", [128, 8], F32)
    moeB["m_cnti"] = anB.alloc_sbuf_tensor("m_cnti", [1, 8], I32)
    moeB["m_cntf"] = anB.alloc_sbuf_tensor("m_cntf", [1, 8], F32)
    moeB["m_cnts8"] = anB.alloc_sbuf_tensor("m_cnts8", [8, 8], F32)
    moeB["m_mym"] = anB.alloc_sbuf_tensor("m_mym", [8, 1], F32)
    assert anB.cur <= cap, ("B arena overflow", anB.cur, cap)
    _use(b, moeB)

    anC = ArenaNC(nc, arena0, "_pc", alias={"srow": "ln_xm", "sexp": "xf"})
    lnC = _alloc_lnset(anC, b)
    w_bufC = anC.alloc_sbuf_tensor("w_buf", [128, 8192], BF16)
    cb = {}
    cb["tT"] = anC.alloc_sbuf_tensor("tT", [128, 8 * SEQK], BF16)
    cb["qT"] = anC.alloc_sbuf_tensor("qT", [128, 8 * SEQK], BF16)
    cb["kT"] = anC.alloc_sbuf_tensor("kT", [128, 8 * SEQK], BF16)
    cb["v_tok"] = anC.alloc_sbuf_tensor("v_tok", [128, 16 * 1024], BF16)
    cb["attnT"] = anC.alloc_sbuf_tensor("attnT", [128, 16 * 512], BF16)
    cb["stats"] = anC.alloc_sbuf_tensor("stats", [128, 8], F32)
    cb["srow"] = anC.alloc_sbuf_tensor("srow", [128, SEQK], F32)
    cb["sexp"] = anC.alloc_sbuf_tensor("sexp", [128, SEQK], BF16)
    assert anC.cur <= cap, ("C arena overflow", anC.cur, cap)

    with ExitStack() as stack:
        env.make_sems(stack)
        with nc.Block() as block:
            _load_consts(env, hd, cs)
            for (hh, ss) in ((dtb_h, dtb), (alog_h, alog), (nw_h, nw64)):
                env.dma("sync", lambda e, hh=hh, ss=ss: e.dma_start(
                    out=ss[:, :], in_=hh[:, :]))
            emit_load_weight(env, wa_h, wa_sb, 1024, 16)
            emit_load_weight(env, wb_h, wb_sb, 1024, 16)
            emit_load_weight(env, wr_h, wr_sb, 1024, 8)
            env.barrier()
            d1 = env.op("scalar", lambda e: e.activation(
                negexpA[:, :], alog[:, :], AF.Exp))
            env.op("vector", lambda e: e.tensor_scalar_mul(
                negexpA[:, :], negexpA[:, :], -1.0), deps=d1)
            env.barrier()
            masks = cs["c_masks"]
            c_scal = cs["c_scal"]
            selb = cs["c_selb"]
            consts = {
                "ident": cs["c_ident_bf"][:, :],
                "identf": cs["c_ident_f"][:, :],
                "cum2": cs["c_cum2"][:, :], "sellast": cs["c_sellast"][:, :],
                "onesmask": cs["c_onesmask"],
                "trilw": masks[:, 0:512], "strictw": masks[:, 512:1024],
                "triuw": masks[:, 1024:1536],
                "negtriuw": masks[:, 1536:2048],
                "selb0": selb[:, 0:64], "selb64": selb[:, 64:128],
                "dtb": dtb[:, :], "negexpA": negexpA[:, :],
                "normw64": nw64[:, :],
                "eps_ln": c_scal[:, 0:1], "eps_k": c_scal[:, 1:2],
                "eps_q64": c_scal[:, 2:3], "eps_rms": c_scal[:, 3:4],
                "iota8": cs["c_iota8"][:, :], "iota8r": cs["c_iota8r"][:, :],
                "cumt": cs["c_cumt"][:, :],
                "ones128c": cs["c_ones128c"][:, :],
                "onesrow": cs["c_onesrow"][:, :],
            }
            ident = cs["c_ident_bf"][:, :]
            W = {"wq_h": H["wq"], "wk_h": H["wk"], "wv_h": H["wv"],
                 "wg_h": H["wg"], "wo_h": H["wo"], "wa": wa_sb, "wb": wb_sb,
                 "w_buf": w_bufA}
            mm = MMGroups(env, b)

            _use(b, lnA)
            for bi in range(2):
                emit_attna_batch(env, b, mm, x_h, xo1_h, SEQK * bi, W,
                                 consts, seq=SEQK, seg=512)
                env.barrier()
            emit_moe_prep(env, b, mm, xo1_h, 0, t2_h, 0, rout_h, cc_in[0],
                          wr_sb, consts, NTK, ident)
            env.barrier()

            def moe_round(cci, cco, xin_h, xout_h):
                dcc = env.dma("gpsimd", lambda e: e.collective_compute(
                    "AllGather", ALU.bypass,
                    replica_groups=[list(range(8))],
                    ins=[cci.ap().opt()], outs=[cco.ap().opt()]))
                emit_moe_base(env, b, cco, mym_h, cci, consts, deps=dcc)
                env.barrier()
                _use(b, lnB)
                emit_moe_zero_ein(env, b, ein_hbm, TCK)
                emit_moe_zero_ein(env, b, eout_hbm, TCK)
                env.barrier()
                emit_moe_scatter(env, b, t2_h, 0, rout_h, ein_hbm, consts,
                                 NTK, TCK)
                env.barrier()
                emit_moe_expert(env, b, mm, ein_hbm, eout_hbm, w1_h, w2_h,
                                ident, TCK)
                env.barrier()
                emit_moe_combine(env, b, xin_h, 0, xout_h, t2_h, 0, rout_h,
                                 eout_hbm, consts, NTK, TCK, CAPK)
                env.barrier()

            moe_round(cc_in[0], cc_out[0], xo1_h, xo2_h)

            _use(b, lnC)
            _use(b, cb)
            for bi in range(2):
                emit_attnb_batch(env, b, mm, xo2_h, xo3_h, SEQK * bi,
                                 B["wq"], B["wk"], B["wv"], B["wo"], w_bufC,
                                 ident, cs["c_negmask"], c_scal[:, 0:1],
                                 seq=SEQK)
                env.barrier()
            emit_moe_prep(env, b, mm, xo3_h, 0, t2_h, 0, rout_h, cc_in[1],
                          wr_sb, consts, NTK, ident)
            env.barrier()

            moe_round(cc_in[1], cc_out[1], xo3_h, xo_h)
            env.finalize(block)
    return env



_ENV_CACHE = {}


def kernel(**inputs):
    import numpy as np
    from concourse.bass_utils import run_bass_kernel_spmd

    if "A" not in _ENV_CACHE:
        _ENV_CACHE["A"] = build_launch_A()
        _ENV_CACHE["B"] = build_launch_B()
        _ENV_CACHE["C"] = build_launch_C()
    envA, envB, envC = _ENV_CACHE["A"], _ENV_CACHE["B"], _ENV_CACHE["C"]

    cc = host_const_arrays()
    f32 = lambda k: np.ascontiguousarray(np.asarray(inputs[k], np.float32))
    x = f32("L0_attn_a_x").reshape(16, SEQK, 1024)
    wq_a, wk_a, wv_a = f32("L0_attn_a_Wq"), f32("L0_attn_a_Wk"), \
        f32("L0_attn_a_Wv")
    wg_a, wo_a = f32("L0_attn_a_Wg"), f32("L0_attn_a_Wo")
    wb_a, wa_a = f32("L0_attn_a_Wb"), f32("L0_attn_a_Wa")
    dtb = np.tile(f32("L0_attn_a_dt_bias").reshape(1, 16), (128, 1))
    alog = np.tile(f32("L0_attn_a_A_log").reshape(1, 16), (128, 1))
    nw64 = np.tile(f32("L0_attn_a_norm_w").reshape(1, 64), (128, 1))
    wq_b, wk_b = f32("L0_attn_b_Wq"), f32("L0_attn_b_Wk")
    wv_b, wo_b = f32("L0_attn_b_Wv"), f32("L0_attn_b_Wo")
    wr = f32("L0_ffn_Wr")
    w1 = f32("L0_ffn_W1")
    w2 = f32("L0_ffn_W2")

    ccA = {n: cc[n] for n in ["c_ident_bf", "c_ident_f", "c_cum2",
                              "c_sellast", "c_onesmask", "c_masks", "c_selb",
                              "c_scal", "c_iota8", "c_iota8r", "c_cumt",
                              "c_ones128c", "c_onesrow"]}
    ccB = {n: cc[n] for n in ["c_ident_bf", "c_iota8", "c_onesrow"]}
    ccC = {n: cc[n] for n in ["c_ident_bf", "c_negmask", "c_scal",
                              "c_iota8", "c_iota8r", "c_cumt", "c_ones128c",
                              "c_onesrow"]}

    mapsA = [{"x": x[2 * c:2 * c + 2].reshape(TCK, 1024),
              "wq": wq_a, "wk": wk_a, "wv": wv_a, "wg": wg_a, "wo": wo_a,
              "wb": wb_a, "wa": wa_a, "dtb": dtb, "alog": alog,
              "nw64": nw64, "wr": wr, **ccA} for c in range(8)]
    resA = run_bass_kernel_spmd(envA.nc, mapsA, core_ids=list(range(8)))
    x1 = [resA.results[c]["xo"] for c in range(8)]
    t2a = [resA.results[c]["t2"] for c in range(8)]
    routa = [resA.results[c]["rout"] for c in range(8)]
    cnts1 = np.concatenate([resA.results[c]["cnts"] for c in range(8)], 0)

    myms = [np.asarray([[1.0 if cp < c else 0.0] for cp in range(8)],
                       np.float32) for c in range(8)]

    mapsB = [{"x": x1[c], "t2": t2a[c], "rout": routa[c],
              "cnts8": cnts1, "mym": myms[c],
              "mycnt": cnts1[c:c + 1], "w1": w1, "w2": w2, **ccB}
             for c in range(8)]
    resB = run_bass_kernel_spmd(envB.nc, mapsB, core_ids=list(range(8)))
    x2 = [resB.results[c]["xo"] for c in range(8)]

    mapsC = [{"x": x2[c], "wq": wq_b, "wk": wk_b, "wv": wv_b, "wo": wo_b,
              "wr": wr, **ccC} for c in range(8)]
    resC = run_bass_kernel_spmd(envC.nc, mapsC, core_ids=list(range(8)))
    x3 = [resC.results[c]["xo"] for c in range(8)]
    t2b = [resC.results[c]["t2"] for c in range(8)]
    routb = [resC.results[c]["rout"] for c in range(8)]
    cnts2 = np.concatenate([resC.results[c]["cnts"] for c in range(8)], 0)

    mapsD = [{"x": x3[c], "t2": t2b[c], "rout": routb[c],
              "cnts8": cnts2, "mym": myms[c],
              "mycnt": cnts2[c:c + 1], "w1": w1, "w2": w2, **ccB}
             for c in range(8)]
    resD = run_bass_kernel_spmd(envB.nc, mapsD, core_ids=list(range(8)))

    out = np.concatenate([np.asarray(resD.results[c]["xo"], np.float32)
                          .reshape(2, SEQK, 1024) for c in range(8)], 0)
    return out


# version 05 (final)
# speedup vs baseline: 1.0258x; 1.0258x over previous
import numpy as np
import concourse.bass as bass
import concourse.mybir as mybir

F32 = mybir.dt.float32
BF16 = mybir.dt.bfloat16
I32 = mybir.dt.int32
U32 = mybir.dt.uint32
AF = mybir.ActivationFunctionType
ALU = mybir.AluOpType
AX = mybir.AxisListType

ENGINES = ("sync", "gpsimd", "scalar", "vector", "tensor")


class Env:

    NDMA = 4

    def __init__(self, nc):
        self.nc = nc
        self.prog = {e: [] for e in ENGINES}
        self.count = {}
        self.sems = {}
        self.dma_rr = {}

    def make_sems(self, stack):
        for e in ENGINES:
            self.sems[e] = stack.enter_context(self.nc.semaphore(f"sem_{e}"))
            self.count[e] = 0
        for e in ("sync", "gpsimd", "scalar"):
            self.dma_rr[e] = 0
            for k in range(self.NDMA):
                key = f"{e}_d{k}"
                self.sems[key] = stack.enter_context(self.nc.semaphore(f"sem_{key}"))
                self.count[key] = 0

    def _wait(self, engine, deps):
        if not deps:
            return
        items = [(e, v) for e, v in deps.items() if v > 0]

        def go(eng, items=items):
            for e, v in items:
                eng.wait_ge(self.sems[e], v)
        self.prog[engine].append(go)

    def op(self, engine, fn, deps=None, inc=1, pe_row="full"):
        if engine != "tensor" and self.count[engine] > 0:
            deps = merge(deps, {engine: self.count[engine]})
        if engine == "tensor":
            if getattr(self, "_pe_row", None) is not None \
               and self._pe_row != pe_row and self.count[engine] > 0:
                deps = merge(deps, {engine: self.count[engine]})
            self._pe_row = pe_row
        if deps:
            self._wait(engine, deps)
        self.count[engine] += inc
        tgt = self.count[engine]
        sem = self.sems[engine]

        def go(eng, fn=fn, sem=sem, inc=inc):
            fn(eng).then_inc(sem, inc)
        self.prog[engine].append(go)
        return {engine: tgt}

    def ops(self, engine, fns, deps=None, inc_each=1):
        out = None
        for f in fns:
            out = self.op(engine, f, deps=deps, inc=inc_each)
            deps = None
        return out

    def raw(self, engine, fn):
        self.prog[engine].append(fn)

    def dma(self, engine, fn, deps=None):
        k = self.dma_rr[engine]
        self.dma_rr[engine] = (k + 1) % self.NDMA
        key = f"{engine}_d{k}"
        prev = {key: self.count[key]}
        self._wait(engine, merge(prev, deps))
        self.count[key] += 16
        tgt = self.count[key]
        sem = self.sems[key]

        def go(eng, fn=fn, sem=sem):
            fn(eng).then_inc(sem, 16)
        self.prog[engine].append(go)
        return {key: tgt}

    def now(self, engine):
        return {engine: self.count[engine]}

    def all_now(self):
        return dict(self.count)

    def barrier(self):
        snap = self.all_now()
        for e in ENGINES:
            deps = {k: v for k, v in snap.items() if k != e and v > 0}
            self._wait(e, deps)
        return snap

    def finalize(self, block):
        for e in ENGINES:
            prog = self.prog[e]

            def body(eng, prog=prog):
                for fn in prog:
                    fn(eng)
            getattr(block, e)(body)


def cp(e, d, src):
    return e.tensor_copy(d, src) if hasattr(e, "tensor_copy") else e.copy(d, src)


def merge(*deps):
    out = {}
    for d in deps:
        if not d:
            continue
        for k, v in d.items():
            out[k] = max(out.get(k, 0), v)
    return out



def host_constants():
    c = {}
    c["c_ident"] = np.eye(128, dtype=np.float32)
    import ml_dtypes
    c["c_ident_bf"] = np.eye(128, dtype=ml_dtypes.bfloat16)
    tril = np.tril(np.ones((64, 64), np.float32))
    strict = np.tril(np.ones((64, 64), np.float32), -1)
    c["c_tril2"] = np.concatenate([tril, tril], 0)
    c["c_strict2"] = np.concatenate([strict, strict], 0)
    triu = tril.T
    striu = strict.T
    c["c_triu2"] = np.concatenate([triu, triu], 0)
    c["c_negtriu2"] = -c["c_triu2"]
    c["c_striu2"] = np.concatenate([striu, striu], 0)
    cum = np.triu(np.ones((64, 64), np.float32))
    z = np.zeros((64, 64), np.float32)
    c["c_cum2"] = np.block([[cum, z], [z, cum]])
    sel = np.zeros((128, 128), np.float32)
    sel[63, 0:64] = 1.0
    sel[127, 64:128] = 1.0
    c["c_sellast"] = sel
    c["c_selh"] = np.eye(16, dtype=np.float32)
    om = np.zeros((16, 16 * 64), np.float32)
    for h in range(16):
        om[h, 64 * h:64 * (h + 1)] = 1.0
    c["c_onesmask"] = om
    c["c_ones128"] = np.ones((128, 1), np.float32)
    c["c_iota8"] = np.tile(np.arange(8, dtype=np.float32), (128, 1))
    c["c_iota8r"] = np.tile(np.arange(7, -1, -1, dtype=np.float32), (128, 1))
    return c



class Bufs:
    pass


def alloc_psum(env):
    nc = env.nc
    b = Bufs()
    b.ps_mm0 = nc.alloc_psum_tensor("ps_mm0", [128, 512], F32)
    b.ps_mm1 = nc.alloc_psum_tensor("ps_mm1", [128, 512], F32)
    b.ps_s = nc.alloc_psum_tensor("ps_s", [128, 512], F32)
    b.ps_aux = nc.alloc_psum_tensor("ps_aux", [128, 512], F32)
    b.ps_d = nc.alloc_psum_tensor("ps_d", [128, 512], F32)
    b.ps_tr = nc.alloc_psum_tensor("ps_tr", [128, 512], BF16)
    b.ps_tr2 = nc.alloc_psum_tensor("ps_tr2", [128, 512], BF16)
    b.ps_sc = nc.alloc_psum_tensor("ps_sc", [128, 512], F32)
    return b



def emit_load_weight(env, w_hbm, w_sb, K, N, deps=None):
    ap = w_hbm.ap().rearrange("(c p) n -> p c n", p=128)
    return env.dma("gpsimd", lambda e: e.dma_start(
        out=w_sb.ap().rearrange("p (c n) -> p c n", n=N), in_=ap), deps=deps)



def emit_ln_tile(env, b, x_f32, tn_bf, eps_ap, d_model=1024, deps=None):
    d1 = env.op("vector", lambda e: e.reduce_sum(b.ln_mean[:, 0:1], x_f32, axis=AX.X),
                deps=deps)
    d2 = env.op("vector", lambda e: e.tensor_scalar(b.ln_mean[:, 1:2], b.ln_mean[:, 0:1],
                                                    1.0 / d_model, None, ALU.mult),
                deps=d1)
    d3 = env.op("vector", lambda e: e.tensor_scalar(b.ln_xm[:, :], x_f32,
                                                    b.ln_mean[:, 1:2], None, ALU.subtract),
                deps=d2)
    d4 = env.op("scalar", lambda e: e.activation(b.ln_sq[:, :], b.ln_xm[:, :], AF.Square),
                deps=d3)
    d5 = env.op("vector", lambda e: e.reduce_sum(b.ln_var[:, 0:1], b.ln_sq[:, :], axis=AX.X),
                deps=d4)
    d6 = env.op("scalar", lambda e: e.activation(b.ln_var[:, 1:2], b.ln_var[:, 0:1],
                                                 AF.Sqrt, bias=eps_ap, scale=1.0 / d_model),
                deps=d5)
    d7 = env.op("vector", lambda e: e.reciprocal(b.ln_rstd[:, 0:1], b.ln_var[:, 1:2]),
                deps=d6)
    d8 = env.op("vector", lambda e: e.tensor_scalar(tn_bf, b.ln_xm[:, :],
                                                    b.ln_rstd[:, 0:1], None, ALU.mult),
                deps=merge(d3, d7))
    return d8


def emit_transpose_blocks(env, b, blocks, ident, deps=None, evac="vector"):
    banks = [b.ps_tr, b.ps_tr2]
    evac_dep = [None, None]
    last = None
    for i, (s_ap, d_ap) in enumerate(blocks):
        bk = banks[i % 2]
        war = merge(deps, evac_dep[i % 2])
        deps = None
        dtr = env.op("tensor", lambda e, s=s_ap, p=bk[:, 0:128]:
                     e.transpose(p, s, ident), deps=war)
        last = env.op(evac, lambda e, p=bk[:, 0:128], d=d_ap: e.tensor_copy(d, p),
                      deps=dtr)
        evac_dep[i % 2] = last
    return last



class MMGroups:

    def __init__(self, env, b, banks=None):
        self.env = env
        self.banks = banks or [b.ps_mm0, b.ps_mm1]
        self.evac_dep = [None] * len(self.banks)
        self.i = 0

    def group(self, mms, evac_fn, deps=None, evac_engine="vector", width=512):
        k = self.i % len(self.banks)
        self.i += 1
        bk = self.banks[k]
        p_ap = bk[:, 0:width]
        war = merge(deps, self.evac_dep[k])
        n = len(mms)
        dmm = None
        for ci, fn in enumerate(mms):
            dmm = self.env.op("tensor",
                              lambda e, fn=fn, st=(ci == 0), sp=(ci == n - 1),
                              p=p_ap: fn(e, p, st, sp),
                              deps=war if ci == 0 else None)
            war = None
        out = self.env.op(evac_engine, lambda e, p=p_ap: evac_fn(e, p), deps=dmm)
        self.evac_dep[k] = out
        return out


def emit_proj_f(env, b, tT, w_sb, out_f, n_tok, mm, w_col0=0, w_ncols=1024,
                deps=None, evac_engine="vector", tT_w=None, nblk=512,
                out_w=None):
    if tT_w is None:
        tT_w = n_tok
    if out_w is None:
        out_w = n_tok
    last = None
    for j in range(w_ncols // 128):
        for nb in range(0, n_tok, nblk):
            nw = min(nblk, n_tok - nb)
            mms = []
            for c in range(8):
                lhs = w_sb[:, w_ncols * c + w_col0 + 128 * j:
                           w_ncols * c + w_col0 + 128 * (j + 1)]
                rhs = tT[:, tT_w * c + nb: tT_w * c + nb + nw]
                mms.append(lambda e, p, st, sp, l=lhs, r=rhs:
                           e.matmul(p[:, 0:r.shape[1]], l, r, start=st, stop=sp))
            dst = out_f[:, out_w * j + nb: out_w * j + nb + nw]
            last = mm.group(mms, lambda e, p, d=dst, nw=nw: cp(e, d, p[:, 0:nw]),
                            deps=deps, evac_engine=evac_engine, width=nw)
            deps = None
    return last


def emit_proj_t(env, b, tT, w_sb, out_t, n_tok, mm, w_ncols=1024,
                deps=None, tT_w=None, out_stride=None, out_off=0,
                evac_engine="scalar"):
    if tT_w is None:
        tT_w = n_tok
    if out_stride is None:
        out_stride = w_ncols
    last = None
    for tt in range(n_tok // 128):
        for nb in range(0, w_ncols, 512):
            nw = min(512, w_ncols - nb)
            mms = []
            for c in range(8):
                lhs = tT[:, tT_w * c + 128 * tt: tT_w * c + 128 * (tt + 1)]
                rhs = w_sb[:, w_ncols * c + nb: w_ncols * c + nb + nw]
                mms.append(lambda e, p, st, sp, l=lhs, r=rhs:
                           e.matmul(p[:, 0:r.shape[1]], l, r, start=st, stop=sp))
            dst = out_t[:, out_off + out_stride * tt + nb:
                        out_off + out_stride * tt + nb + nw]
            last = mm.group(mms, lambda e, p, d=dst, nw=nw: cp(e, d, p[:, 0:nw]),
                            deps=deps, evac_engine=evac_engine, width=nw)
            deps = None
    return last



def alloc_attnb_bufs(nc, b, seq=2048):
    nt = seq // 128
    b.tT = nc.alloc_sbuf_tensor("tT", [128, 8 * seq], BF16)
    b.qT = nc.alloc_sbuf_tensor("qT", [128, 8 * seq], BF16)
    b.kT = nc.alloc_sbuf_tensor("kT", [128, 8 * seq], BF16)
    b.v_tok = nc.alloc_sbuf_tensor("v_tok", [128, nt * 1024], BF16)
    b.srow = nc.alloc_sbuf_tensor("srow", [128, seq], F32)
    b.sexp = nc.alloc_sbuf_tensor("sexp", [128, seq], BF16)
    b.attnT = nc.alloc_sbuf_tensor("attnT", [128, nt * 512], BF16)
    b.stats = nc.alloc_sbuf_tensor("stats", [128, 8], F32)
    b.tn = nc.alloc_sbuf_tensor("tn", [128, 1024], BF16)
    b.xf = nc.alloc_sbuf_tensor("xf", [128, 1024], F32)
    b.xo = nc.alloc_sbuf_tensor("xo_sb", [128, 1024], F32)
    b.ln_mean = nc.alloc_sbuf_tensor("ln_mean", [128, 2], F32)
    b.ln_var = nc.alloc_sbuf_tensor("ln_var", [128, 2], F32)
    b.ln_rstd = nc.alloc_sbuf_tensor("ln_rstd", [128, 1], F32)
    b.ln_xm = nc.alloc_sbuf_tensor("ln_xm", [128, 1024], F32)
    b.ln_sq = nc.alloc_sbuf_tensor("ln_sq", [128, 1024], F32)


def emit_ln_transpose_all(env, b, x_hbm, row0, n_tiles, tT, tT_w, ident,
                          eps_ap, deps=None, x_dst_hbm=None, tok_dst=None,
                          tok_stride=1024, tok_off=0):
    last = None
    for t in range(n_tiles):
        dl = env.dma("sync", lambda e, r=row0 + 128 * t: e.dma_start(
            out=b.xf[:, :], in_=x_hbm[r:r + 128, :]), deps=deps)
        deps = None
        dln = emit_ln_tile(env, b, b.xf[:, :], b.tn[:, :], eps_ap, deps=dl)
        if tok_dst is not None:
            env.op("vector", lambda e, t=t: e.tensor_copy(
                tok_dst[:, tok_off + tok_stride * t: tok_off + tok_stride * t + 1024],
                b.tn[:, :]), deps=dln)
        if x_dst_hbm is not None:
            env.dma("sync", lambda e, r=row0 + 128 * t, t=t: e.dma_start(
                out=x_dst_hbm[r:r + 128, :], in_=b.tn[:, :]), deps=dln)
        blocks = [(b.tn[:, 128 * c:128 * (c + 1)],
                   tT[:, tT_w * c + 128 * t: tT_w * c + 128 * t + 128])
                  for c in range(8)]
        last = emit_transpose_blocks(env, b, blocks, ident, deps=dln)
        deps = merge(last, dln)
    return last


def emit_attnb_batch(env, b, mm, x_hbm, xo_hbm, row0, wq_h, wk_h, wv_h, wo_h,
                     w_buf, ident, negmask, eps_ap, seq=2048, deps=None):
    nt = seq // 128

    d_tT = emit_ln_transpose_all(env, b, x_hbm, row0, nt, b.tT, seq, ident,
                                 eps_ap, deps=deps)
    snap = env.barrier()

    dw = emit_load_weight(env, wq_h, w_buf, 1024, 1024)
    d_q = emit_proj_f(env, b, b.tT, w_buf, b.qT, seq, mm, deps=dw)
    dw = emit_load_weight(env, wk_h, w_buf, 1024, 1024, deps=d_q)
    d_k = emit_proj_f(env, b, b.tT, w_buf, b.kT, seq, mm, deps=dw)
    dw = emit_load_weight(env, wv_h, w_buf, 1024, 1024, deps=d_k)
    d_v = emit_proj_t(env, b, b.tT, w_buf, b.v_tok, seq, mm, deps=dw)
    env.barrier()

    for sg in range(seq // 512):
        n_kt = 4 * sg + 4
        kw = 512 * sg + 512
        d_grp = []
        prev_s = None
        for si in range(4):
            s = 4 * sg + si
            for kb in range(sg + 1):
                mms = []
                for c in range(8):
                    lhs = b.qT[:, seq * c + 128 * s: seq * c + 128 * (s + 1)]
                    rhs = b.kT[:, seq * c + 512 * kb: seq * c + 512 * (kb + 1)]
                    mms.append(lambda e, p, st, sp, l=lhs, r=rhs:
                               e.matmul(p, l, r, start=st, stop=sp))
                if kb == sg:
                    def evac(e, p, si=si, kb=kb):
                        return e.tensor_add(b.srow[:, 512 * kb:512 * (kb + 1)],
                                            p, negmask[:, 512 * si:512 * (si + 1)])
                else:
                    def evac(e, p, kb=kb):
                        return e.tensor_copy(b.srow[:, 512 * kb:512 * (kb + 1)], p)
                d_sc = mm.group(mms, evac, deps=prev_s if kb == 0 else None)
            d_m = env.op("vector", lambda e, kw=kw: e.reduce_max(
                b.stats[:, 0:1], b.srow[:, 0:kw], axis=AX.X), deps=d_sc)
            d_b = env.op("vector", lambda e: e.tensor_scalar(
                b.stats[:, 1:2], b.stats[:, 0:1], -1.0 / 32.0, None, ALU.mult),
                deps=merge(d_m, prev_s))
            d_e = env.op("scalar", lambda e, kw=kw: e.activation(
                b.sexp[:, 0:kw], b.srow[:, 0:kw], AF.Exp,
                bias=b.stats[:, 1:2], scale=1.0 / 32.0,
                accum_out=b.stats[:, 2:3]), deps=d_b)
            d_r = env.op("vector", lambda e: e.reciprocal(
                b.stats[:, 3:4], b.stats[:, 2:3]), deps=d_e)
            d_s = env.op("vector", lambda e, kw=kw: e.tensor_scalar(
                b.sexp[:, 0:kw], b.sexp[:, 0:kw], b.stats[:, 3:4], None,
                ALU.mult), deps=d_r)
            blocks = [(b.sexp[:, 128 * kt:128 * (kt + 1)],
                       b.attnT[:, 512 * kt + 128 * si: 512 * kt + 128 * si + 128])
                      for kt in range(n_kt)]
            d_tr = emit_transpose_blocks(env, b, blocks, ident, deps=d_s)
            d_grp.append(d_tr)
            prev_s = merge(d_e, d_s, d_tr)
        dep_attnT = merge(*d_grp)
        for j in range(8):
            mms = []
            for kt in range(n_kt):
                lhs = b.v_tok[:, 1024 * kt + 128 * j: 1024 * kt + 128 * (j + 1)]
                rhs = b.attnT[:, 512 * kt: 512 * (kt + 1)]
                mms.append(lambda e, p, st, sp, l=lhs, r=rhs:
                           e.matmul(p, l, r, start=st, stop=sp))
            dst = b.tT[:, seq * j + 512 * sg: seq * j + 512 * sg + 512]
            mm.group(mms, lambda e, p, d=dst: cp(e, d, p),
                     deps=dep_attnT, evac_engine="scalar")
    env.barrier()

    dw = emit_load_weight(env, wo_h, w_buf, 1024, 1024)
    d_o = emit_proj_f(env, b, b.tT, w_buf, b.qT, seq, mm, tT_w=seq, deps=dw)
    env.barrier()

    prev_tr = None
    for t in range(nt):
        dl = env.dma("sync", lambda e, r=row0 + 128 * t: e.dma_start(
            out=b.xf[:, :], in_=x_hbm[r:r + 128, :]))
        blocks = [(b.qT[:, seq * c + 128 * t: seq * c + 128 * t + 128],
                   b.tn[:, 128 * c:128 * (c + 1)]) for c in range(8)]
        d_tr = emit_transpose_blocks(env, b, blocks, ident, deps=prev_tr)
        prev_tr = d_tr
        d_add = env.op("vector", lambda e: e.tensor_add(
            b.xo[:, :], b.xf[:, :], b.tn[:, :]), deps=merge(dl, d_tr))
        d_st = env.dma("sync", lambda e, r=row0 + 128 * t: e.dma_start(
            out=xo_hbm[r:r + 128, :], in_=b.xo[:, :]), deps=d_add)
        env._wait("sync", d_st)
        env._wait("vector", d_st)
    return env.all_now()



def alloc_attna_bufs(nc, b, seg=512):
    nt = seg // 128
    b.a_tT = nc.alloc_sbuf_tensor("a_tT", [128, 8 * seg], BF16)
    b.a_q = nc.alloc_sbuf_tensor("a_q", [128, nt * 1024], BF16)
    b.a_k = nc.alloc_sbuf_tensor("a_k", [128, nt * 1024], BF16)
    b.a_v = nc.alloc_sbuf_tensor("a_v", [128, nt * 1024], BF16)
    b.a_gate = nc.alloc_sbuf_tensor("a_gate", [128, nt * 1024], BF16)
    b.a_kbg = nc.alloc_sbuf_tensor("a_kbg", [128, nt * 1024], BF16)
    b.a_kw = nc.alloc_sbuf_tensor("a_kw", [128, nt * 1024], BF16)
    b.a_qT = nc.alloc_sbuf_tensor("a_qT", [128, 8 * seg], BF16)
    b.a_kT = nc.alloc_sbuf_tensor("a_kT", [128, 8 * seg], BF16)
    b.a_P = nc.alloc_sbuf_tensor("a_P", [128, nt * 1024], BF16)
    b.a_attnT = nc.alloc_sbuf_tensor("a_attnT", [128, nt * 1024], BF16)
    b.a_vp = nc.alloc_sbuf_tensor("a_vp", [128, nt * 1024], BF16)
    b.a_kcd = nc.alloc_sbuf_tensor("a_kcd", [128, nt * 1024], BF16)
    b.a_kcdT = nc.alloc_sbuf_tensor("a_kcdT", [128, nt * 1024], BF16)
    b.a_vn = nc.alloc_sbuf_tensor("a_vn", [128, nt * 1024], BF16)
    b.a_o = nc.alloc_sbuf_tensor("a_o", [128, nt * 1024], BF16)
    for nm in ("a_beta", "a_g", "a_eg", "a_egl", "a_betaeg", "a_sc1",
               "a_sc2", "a_raw"):
        setattr(b, nm, nc.alloc_sbuf_tensor(nm, [128, nt * 16], F32))
    b.a_gF = nc.alloc_sbuf_tensor("a_gF", [16, seg], F32)
    b.a_glb = nc.alloc_sbuf_tensor("a_glb", [128, (seg // 128) * 16], F32)
    b.a_ngF = nc.alloc_sbuf_tensor("a_ngF", [16, seg], F32)
    b.a_S = nc.alloc_sbuf_tensor("a_S", [128, 512], F32)
    b.a_Sbf = nc.alloc_sbuf_tensor("a_Sbf", [128, 512], BF16)
    b.a_egS = nc.alloc_sbuf_tensor("a_egS", [128, 16], F32)
    b.a_bdpk = nc.alloc_sbuf_tensor("a_bdpk", [128, 128], BF16)
    b.a_bdpkt = nc.alloc_sbuf_tensor("a_bdpkt", [128, 128], BF16)
    b.a_u = nc.alloc_sbuf_tensor("a_u", [128, 128], BF16)
    b.a_w1 = nc.alloc_sbuf_tensor("a_w1", [128, 512], F32)
    b.a_w2 = nc.alloc_sbuf_tensor("a_w2", [128, 512], F32)
    b.a_Etl = nc.alloc_sbuf_tensor("a_Etl", [128, 512], BF16)
    b.a_Etu = nc.alloc_sbuf_tensor("a_Etu", [128, 512], BF16)
    b.a_sqsc = nc.alloc_sbuf_tensor("a_sqsc", [128, 1024], F32)


def emit_attna_batch(env, b, mm, x_hbm, xo_hbm, row0, W, consts, seq=2048,
                     seg=512, deps=None):
    import os
    STOP = int(os.environ.get("ATTNA_STOP", "99"))
    """One batch of deltanet. W: dict with HBM handles wq,wk,wv,wg,wo and
    SBUF wa,wb (preloaded [128,8*16]), w_buf streaming buffer.
    consts: dict of SBUF const APs: ident, cum2, sellast, onesmask, trilw,
    strictw, negtriuw, triuw, dtb (dt_bias bcast [128,16]), negexpA
    ([128,16], prebuilt), normw64 ([128,64]), eps_ln, eps_q64, eps_k,
    eps_rms, selb0/selb64 ([128,64] row-selector consts).
    """
    nc = env.nc
    nt = seg // 128
    n_seg = seq // seg
    ident = consts["ident"]

    d0 = env.op("vector", lambda e: e.memset(b.a_S[:, :], 0.0), deps=deps)
    env.op("vector", lambda e: e.memset(b.a_Sbf[:, :], 0.0), deps=d0)
    env.op("vector", lambda e: e.memset(b.a_bdpk[:, :], 0.0))
    env.op("vector", lambda e: e.memset(b.a_u[:, :], 0.0))
    env.barrier()

    for s in range(n_seg):
        srow = row0 + seg * s
        emit_ln_transpose_all(env, b, x_hbm, srow, nt, b.a_tT, seg, ident,
                              consts["eps_ln"])
        env.barrier()

        dw = emit_load_weight(env, W["wq_h"], W["w_buf"], 1024, 1024)
        emit_proj_t(env, b, b.a_tT, W["w_buf"], b.a_q, seg, mm, deps=dw,
                    tT_w=seg)
        dw = emit_load_weight(env, W["wk_h"], W["w_buf"], 1024, 1024,
                              deps=env.now("tensor"))
        emit_proj_t(env, b, b.a_tT, W["w_buf"], b.a_k, seg, mm, deps=dw,
                    tT_w=seg)
        dw = emit_load_weight(env, W["wv_h"], W["w_buf"], 1024, 1024,
                              deps=env.now("tensor"))
        emit_proj_t(env, b, b.a_tT, W["w_buf"], b.a_v, seg, mm, deps=dw,
                    tT_w=seg)
        dw = emit_load_weight(env, W["wg_h"], W["w_buf"], 1024, 1024,
                              deps=env.now("tensor"))
        emit_proj_t(env, b, b.a_tT, W["w_buf"], b.a_gate, seg, mm, deps=dw,
                    tT_w=seg)
        emit_proj_t(env, b, b.a_tT, W["wb"], b.a_raw, seg, mm, w_ncols=16,
                    tT_w=seg, out_stride=16)
        env.barrier()
        for t in range(nt):
            env.op("vector", lambda e, t=t: e.tensor_copy(
                b.a_sc1[:, 16 * t:16 * (t + 1)], b.a_raw[:, 16 * t:16 * (t + 1)]))
        env.barrier()
        emit_proj_t(env, b, b.a_tT, W["wa"], b.a_raw, seg, mm, w_ncols=16,
                    tT_w=seg, out_stride=16)
        env.barrier()

        for t in range(nt):
            rb = b.a_sc1[:, 16 * t:16 * (t + 1)]
            ra = b.a_raw[:, 16 * t:16 * (t + 1)]
            bcol = b.a_beta[:, 16 * t:16 * (t + 1)]
            gcol = b.a_g[:, 16 * t:16 * (t + 1)]
            d_sg = env.op("scalar", lambda e, rb=rb, bcol=bcol: e.activation(
                bcol, rb, AF.Sigmoid))
            z = b.a_sc2[:, 16 * t:16 * (t + 1)]
            d_z = env.op("vector", lambda e, ra=ra, z=z: e.tensor_add(
                z, ra, consts["dtb"]))
            tmp = b.a_sc1[:, 16 * t:16 * (t + 1)]
            d_ab = env.op("scalar", lambda e, z=z, tmp=tmp: e.activation(
                tmp, z, AF.Abs), deps=merge(d_z, d_sg))
            env.op("scalar", lambda e, tmp=tmp: e.activation(
                tmp, tmp, AF.Exp, scale=-1.0))
            d_ln = env.op("scalar", lambda e, tmp=tmp: e.activation(
                tmp, tmp, AF.Ln, bias=1.0))
            env.op("vector", lambda e, z=z: e.tensor_scalar_max(z, z, 0.0),
                   deps=d_ab)
            env.op("vector", lambda e, tmp=tmp, z=z: e.tensor_add(
                tmp, tmp, z), deps=d_ln)
            dec = z
            env.op("vector", lambda e, tmp=tmp, dec=dec: e.tensor_mul(
                dec, tmp, consts["negexpA"]))
            dmm = env.op("tensor", lambda e, dec=dec: e.matmul(
                b.ps_d[:, 0:16], consts["cum2"], dec, start=True, stop=True),
                deps=env.now("vector"))
            dgv = env.op("vector", lambda e, gcol=gcol: e.tensor_copy(
                gcol, b.ps_d[:, 0:16]), deps=dmm)
            env.op("scalar", lambda e, gcol=gcol, t=t: e.activation(
                b.a_eg[:, 16 * t:16 * (t + 1)], gcol, AF.Exp), deps=dgv)
            dm2 = env.op("tensor", lambda e, gcol=gcol: e.matmul(
                b.ps_d[:, 16:32], consts["sellast"], gcol, start=True,
                stop=True), deps=dgv)
            env.op("vector", lambda e, t=t: e.tensor_copy(
                b.a_glb[:, 16 * t:16 * (t + 1)], b.ps_d[:, 16:32]), deps=dm2)
            dsub = env.op("vector", lambda e, gcol=gcol, t=t: e.tensor_sub(
                b.a_sc2[:, 16 * t:16 * (t + 1)], b.ps_d[:, 16:32], gcol))
            env.op("scalar", lambda e, t=t: e.activation(
                b.a_egl[:, 16 * t:16 * (t + 1)],
                b.a_sc2[:, 16 * t:16 * (t + 1)], AF.Exp), deps=dsub)
            env.op("vector", lambda e, t=t: e.tensor_mul(
                b.a_betaeg[:, 16 * t:16 * (t + 1)],
                b.a_beta[:, 16 * t:16 * (t + 1)],
                b.a_eg[:, 16 * t:16 * (t + 1)]),
                deps=env.now("scalar"))
        env.barrier()
        blocks = []
        prev_ev = None
        for t in range(nt):
            dtr = env.op("tensor", lambda e, t=t: e.transpose(
                b.ps_sc[0:16, 0:128], b.a_g[:, 16 * t:16 * (t + 1)],
                consts["identf"]), deps=prev_ev)
            prev_ev = env.op("vector", lambda e, t=t: e.tensor_copy(
                b.a_gF[:, 128 * t:128 * (t + 1)], b.ps_sc[0:16, 0:128]),
                deps=dtr)
        denf = env.op("vector", lambda e: e.tensor_scalar_mul(
            b.a_ngF[:, :], b.a_gF[:, :], -1.0))
        env.barrier()

        if STOP < 2:
            continue
        for (src, dst, epsk) in ((b.a_q, b.a_qT, "eps_q64"),
                                 (b.a_k, b.a_kT, "eps_k")):
            for t in range(nt):
                dsq = env.op("scalar", lambda e, t=t, src=src: e.activation(
                    b.a_sqsc[:, :], src[:, 1024 * t:1024 * (t + 1)], AF.Square))
                for h in range(16):
                    env.op("vector", lambda e, h=h: e.reduce_sum(
                        b.a_sc2[:, h:h + 1], b.a_sqsc[:, 64 * h:64 * (h + 1)],
                        axis=AX.X), deps=dsq if h == 0 else None)
                if epsk == "eps_q64":
                    dsq2 = env.op("scalar", lambda e: e.activation(
                        b.a_sc2[:, 16:32], b.a_sc2[:, 0:16], AF.Sqrt,
                        bias=consts["eps_q64"], scale=64.0),
                        deps=env.now("vector"))
                else:
                    dsq2 = env.op("scalar", lambda e: e.activation(
                        b.a_sc2[:, 16:32], b.a_sc2[:, 0:16], AF.Sqrt,
                        bias=consts["eps_k"], scale=1.0),
                        deps=env.now("vector"))
                drc = env.op("vector", lambda e: e.reciprocal(
                    b.a_sc2[:, 32:48], b.a_sc2[:, 16:32]), deps=dsq2)
                for h in range(16):
                    env.op("vector", lambda e, t=t, h=h, src=src:
                           e.tensor_scalar(
                               src[:, 1024 * t + 64 * h:1024 * t + 64 * (h + 1)],
                               src[:, 1024 * t + 64 * h:1024 * t + 64 * (h + 1)],
                               b.a_sc2[:, 32 + h:33 + h], None, ALU.mult),
                           deps=drc if h == 0 else None)
            env.barrier()
            for t in range(nt):
                blocks = [(src[:, 1024 * t + 128 * c:1024 * t + 128 * (c + 1)],
                           dst[:, seg * c + 128 * t:seg * c + 128 * t + 128])
                          for c in range(8)]
                emit_transpose_blocks(env, b, blocks, ident,
                                      deps=env.now("vector"))
            env.barrier()

        for t in range(nt):
            for h in range(16):
                cc = 1024 * t + 64 * h
                env.op("vector", lambda e, cc=cc, t=t, h=h: e.tensor_scalar(
                    b.a_kbg[:, cc:cc + 64], b.a_k[:, cc:cc + 64],
                    b.a_betaeg[:, 16 * t + h:16 * t + h + 1], None, ALU.mult))
                env.op("vector", lambda e, cc=cc, t=t, h=h: e.tensor_scalar(
                    b.a_kw[:, cc:cc + 64], b.a_k[:, cc:cc + 64],
                    b.a_egl[:, 16 * t + h:16 * t + h + 1], None, ALU.mult))
                env.op("vector", lambda e, cc=cc, t=t, h=h: e.tensor_scalar(
                    b.a_v[:, cc:cc + 64], b.a_v[:, cc:cc + 64],
                    b.a_beta[:, 16 * t + h:16 * t + h + 1], None, ALU.mult))
        env.barrier()

        if STOP < 3:
            continue
        SUB = int(os.environ.get("ATTNA_SUB", "9"))
        for t in range(nt):
            dl = {}
            for i2 in range(2):
                i = 2 * t + i2
                bi = 64 * i2
                gsl = slice(128 * t + bi, 128 * t + bi + 64)
                for h in range(16):
                    ps = b.ps_d if h < 8 else b.ps_s
                    co = 64 * (h % 8)
                    dmm = env.op("tensor", lambda e, ps=ps, co=co, bi=bi,
                                 gsl=gsl, h=h: e.matmul(
                                     ps[bi:bi + 64, co:co + 64],
                                     b.a_gF[:, gsl],
                                     consts["onesmask"][:, 64 * h:64 * (h + 1)],
                                     start=True, stop=False,
                                     tile_position=(0, bi)),
                                 deps=env.now("vector") if (h == 0 and i2 == 0)
                                 else None, pe_row="k16")
                    dl = env.op("tensor", lambda e, ps=ps, co=co, bi=bi,
                                gsl=gsl, h=h: e.matmul(
                                    ps[bi:bi + 64, co:co + 64],
                                    consts["onesmask"][:, 64 * h:64 * (h + 1)],
                                    b.a_ngF[:, gsl],
                                    start=False, stop=True,
                                    tile_position=(0, bi)), pe_row="k16")
            if SUB < 2:
                env.op("vector", lambda e: e.tensor_copy(b.a_w1[:, :],
                                                         b.ps_d[:, :]),
                       deps=dl)
                continue
            for (ps, off) in ((b.ps_d, 0), (b.ps_s, 1)):
                d1 = env.op("vector", lambda e, ps=ps: e.tensor_mul(
                    b.a_w1[:, :], ps[:, :], consts["trilw"]), deps=dl)
                d2 = env.op("vector", lambda e, ps=ps: e.tensor_mul(
                    b.a_w2[:, :], ps[:, :], consts["negtriuw"]), deps=d1)
                d3 = env.op("scalar", lambda e: e.activation(
                    b.a_Etl[:, :], b.a_w1[:, :], AF.Exp), deps=d1)
                d4 = env.op("scalar", lambda e: e.activation(
                    b.a_Etu[:, :], b.a_w2[:, :], AF.Exp), deps=d2)
                if SUB < 3:
                    continue
                first = True
                for hh in (0, 2, 4, 6, 1, 3, 5, 7):
                    h = 8 * off + hh
                    hb = 64 * (h % 2)
                    jc = 512 * (h // 2)
                    for i2 in range(2):
                        i = 2 * t + i2
                        bi = 64 * i2
                        tc = jc + 128 * t + bi
                        ksl = b.a_kT[hb:hb + 64, tc:tc + 64]
                        qsl = b.a_qT[hb:hb + 64, tc:tc + 64]
                        env.op("tensor", lambda e, ksl=ksl, bi=bi, hh=hh, hb=hb:
                               e.matmul(b.ps_mm0[bi:bi + 64,
                                                 64 * hh:64 * hh + 64],
                                        ksl, ksl, start=True, stop=True,
                                        tile_position=(hb, bi)),
                               deps=merge(d3, d4) if first
                               else None, pe_row=hb)
                        first = False
                        env.op("tensor", lambda e, ksl=ksl, qsl=qsl, bi=bi,
                               hh=hh, hb=hb:
                               e.matmul(b.ps_mm1[bi:bi + 64,
                                                 64 * hh:64 * hh + 64],
                                        ksl, qsl, start=True, stop=True,
                                        tile_position=(hb, bi)), pe_row=hb)
                dmm = env.now("tensor")
                if SUB < 4:
                    continue
                d5 = env.op("vector", lambda e: e.tensor_mul(
                    b.a_w1[:, :], b.ps_mm0[:, :], b.a_Etl[:, :]),
                    deps=merge(dmm, d3))
                d6 = env.op("vector", lambda e: e.tensor_mul(
                    b.a_w2[:, :], b.ps_mm1[:, :], b.a_Etu[:, :]),
                    deps=merge(dmm, d4))
                d7 = env.op("vector", lambda e, t=t, off=off: e.tensor_mul(
                    b.a_attnT[:, 1024 * t + 512 * off:1024 * t + 512 * off + 512],
                    b.a_w2[:, :], consts["triuw"]), deps=d6)
                for hh in range(8):
                    h = 8 * off + hh
                    env.op("vector", lambda e, t=t, h=h, hh=hh:
                           e.scalar_tensor_tensor(
                               b.a_P[:, 1024 * t + 64 * h:1024 * t + 64 * h + 64],
                               b.a_w1[:, 64 * hh:64 * hh + 64],
                               b.a_beta[:, 16 * t + h:16 * t + h + 1],
                               consts["strictw"][:, 64 * hh:64 * hh + 64],
                               ALU.mult, ALU.mult),
                           deps=d5 if hh == 0 else None)
        env.barrier()

        if STOP < 4:
            continue
        prev_g = None
        for t in range(nt):
            for h in range(16):
                ct = 1024 * t + 64 * h
                d_i1 = env.op("vector", lambda e, ct=ct: e.tensor_copy(
                    b.a_bdpk[0:64, 0:64], b.a_P[0:64, ct:ct + 64]),
                    deps=prev_g)
                d_i2 = env.op("vector", lambda e, ct=ct: e.tensor_copy(
                    b.a_bdpk[64:128, 64:128], b.a_P[64:128, ct:ct + 64]))
                d_tr = env.op("tensor", lambda e: e.transpose(
                    b.ps_tr[:, 0:128], b.a_bdpk[:, :], ident),
                    deps=merge(d_i1, d_i2))
                d_pt = env.op("vector", lambda e: e.tensor_copy(
                    b.a_bdpkt[:, :], b.ps_tr[:, 0:128]), deps=d_tr)
                d_u1 = env.op("vector", lambda e, ct=ct: e.tensor_copy(
                    b.a_u[:, 0:64], b.a_v[:, ct:ct + 64]))
                d_u2 = env.op("vector", lambda e, ct=ct: e.tensor_copy(
                    b.a_u[:, 64:128], b.a_kbg[:, ct:ct + 64]))
                dep = merge(d_pt, d_u1, d_u2)
                ESUB = int(os.environ.get("ATTNA_ESUB", "9"))
                for j in range(6 if ESUB >= 4 else (1 if ESUB >= 2 else 0)):
                    d_um = env.op("tensor", lambda e: e.matmul(
                        b.ps_mm0[:, 0:128], b.a_bdpkt[:, :], b.a_u[:, :],
                        start=True, stop=True), deps=dep)
                    d_ua = env.op("vector", lambda e: e.tensor_add(
                        b.a_u[:, :], b.ps_mm0[:, 0:128], b.a_u[:, :]),
                        deps=d_um)
                    if j < 5 and ESUB >= 3:
                        d_s1 = env.op("tensor", lambda e: e.matmul(
                            b.ps_mm1[:, 0:128], b.a_bdpkt[:, :],
                            b.a_bdpk[:, :], start=True, stop=True),
                            deps=dep)
                        d_s2 = env.op("tensor", lambda e: e.matmul(
                            b.ps_aux[:, 0:128], b.a_bdpk[:, :],
                            b.a_bdpkt[:, :], start=True, stop=True))
                        d_e1 = env.op("vector", lambda e: e.tensor_copy(
                            b.a_bdpk[:, :], b.ps_mm1[:, 0:128]),
                            deps=merge(d_s2, d_um))
                        d_e2 = env.op("scalar", lambda e: e.copy(
                            b.a_bdpkt[:, :], b.ps_aux[:, 0:128]),
                            deps=merge(d_s2, d_um))
                        dep = merge(d_ua, d_e1, d_e2)
                    else:
                        dep = d_ua
                d_x1 = env.op("vector", lambda e, ct=ct: e.tensor_copy(
                    b.a_vp[:, ct:ct + 64], b.a_u[:, 0:64]), deps=dep)
                d_x2 = env.op("vector", lambda e, ct=ct: e.tensor_copy(
                    b.a_kcd[:, ct:ct + 64], b.a_u[:, 64:128]))
                prev_g = merge(d_x1, d_x2)
        env.barrier()
        for t in range(nt):
            blocks = [(b.a_kcd[:, 1024 * t + 128 * jp:1024 * t + 128 * (jp + 1)],
                       b.a_kcdT[:, 1024 * t + 128 * jp:1024 * t + 128 * (jp + 1)])
                      for jp in range(8)]
            emit_transpose_blocks(env, b, blocks, ident,
                                  deps=env.now("vector"))
        env.barrier()

        if STOP < 5:
            continue
        prev_dg3 = None
        for i in range(2 * nt):
            t = i // 2
            bi = 64 * (i % 2)
            dS = env.now("vector")
            dmm = None
            first = True
            for h in (0, 2, 4, 6, 8, 10, 12, 14, 1, 3, 5, 7, 9, 11, 13, 15):
                hb = 64 * (h % 2)
                jc = 64 * (h // 2)
                ps = b.ps_aux if h < 8 else b.ps_sc
                lhs = b.a_kcdT[hb:hb + 64,
                               1024 * t + 128 * (h // 2) + bi:
                               1024 * t + 128 * (h // 2) + bi + 64]
                rhs = b.a_Sbf[hb:hb + 64, jc:jc + 64]
                dmm = env.op("tensor", lambda e, l=lhs, r=rhs, ps=ps, bi=bi,
                             h=h, hb=hb: e.matmul(
                                 ps[bi:bi + 64, 64 * (h % 8):64 * (h % 8) + 64],
                                 l, r, start=True, stop=True,
                                 tile_position=(hb, bi)),
                             deps=dS if first else None, pe_row=hb)
                first = False
            d_vn = None
            for (ps, off) in ((b.ps_aux, 0), (b.ps_sc, 1)):
                d_vn = env.op("vector", lambda e, ps=ps, off=off, t=t, bi=bi:
                              e.tensor_sub(
                                  b.a_vn[bi:bi + 64,
                                         1024 * t + 512 * off:
                                         1024 * t + 512 * off + 512],
                                  b.a_vp[bi:bi + 64,
                                         1024 * t + 512 * off:
                                         1024 * t + 512 * off + 512],
                                  ps[bi:bi + 64, 0:512]), deps=dmm)
            dmm2 = None
            for h in range(16):
                ps = b.ps_mm0 if h < 8 else b.ps_mm1
                ct = 1024 * t + 64 * h
                dmm2 = env.op("tensor", lambda e, ct=ct, ps=ps, bi=bi, h=h:
                              e.matmul(
                                  ps[bi:bi + 64, 64 * (h % 8):64 * (h % 8) + 64],
                                  b.a_attnT[bi:bi + 64, ct:ct + 64],
                                  b.a_vn[bi:bi + 64, ct:ct + 64],
                                  start=True, stop=True,
                                  tile_position=(bi, bi)),
                              deps=d_vn if h == 0 else None, pe_row=bi)
            d_o2 = None
            for (ps, off) in ((b.ps_mm0, 0), (b.ps_mm1, 1)):
                d_o2 = env.op("vector", lambda e, ps=ps, off=off, t=t, bi=bi:
                              e.tensor_copy(
                                  b.a_o[bi:bi + 64,
                                        1024 * t + 512 * off:
                                        1024 * t + 512 * off + 512],
                                  ps[bi:bi + 64, 0:512]), deps=dmm2)
            dmm3 = None
            first = True
            for h in (0, 2, 4, 6, 8, 10, 12, 14, 1, 3, 5, 7, 9, 11, 13, 15):
                hb = 64 * (h % 2)
                ps = b.ps_aux if h < 8 else b.ps_sc
                lhs = b.a_qT[hb:hb + 64,
                             512 * (h // 2) + 128 * t + bi:
                             512 * (h // 2) + 128 * t + bi + 64]
                rhs = b.a_Sbf[hb:hb + 64, 64 * (h // 2):64 * (h // 2) + 64]
                dmm3 = env.op("tensor", lambda e, l=lhs, r=rhs, ps=ps, bi=bi,
                              h=h, hb=hb: e.matmul(
                                  ps[bi:bi + 64,
                                     64 * (h % 8):64 * (h % 8) + 64],
                                  l, r, start=True, stop=True,
                                  tile_position=(hb, bi)),
                              deps=merge(d_vn, d_o2) if first else None,
                              pe_row=hb)
                first = False
            d_o1 = None
            for h in range(16):
                ps = b.ps_aux if h < 8 else b.ps_sc
                ct = 1024 * t + 64 * h
                d_o1 = env.op("vector", lambda e, ct=ct, ps=ps, bi=bi, h=h,
                              t=t: e.scalar_tensor_tensor(
                                  b.a_o[bi:bi + 64, ct:ct + 64],
                                  ps[bi:bi + 64, 64 * (h % 8):64 * (h % 8) + 64],
                                  b.a_eg[bi:bi + 64, 16 * t + h:16 * t + h + 1],
                                  b.a_o[bi:bi + 64, ct:ct + 64],
                                  ALU.mult, ALU.add),
                              deps=dmm3 if h == 0 else None)
            dmm4 = None
            for h in range(16):
                hb = 64 * (h % 2)
                ct = 1024 * t + 64 * h
                dmm4 = env.op("tensor", lambda e, ct=ct, bi=bi, h=h, hb=hb:
                              e.matmul(
                                  b.ps_d[hb:hb + 64,
                                         64 * (h // 2):64 * (h // 2) + 64],
                                  b.a_kw[bi:bi + 64, ct:ct + 64],
                                  b.a_vn[bi:bi + 64, ct:ct + 64],
                                  start=True, stop=True,
                                  tile_position=(bi, hb)),
                              deps=merge(d_o1, d_vn) if h == 0 else None,
                              pe_row=bi)
            selb = consts["selb0"] if bi == 0 else consts["selb64"]
            dg1 = env.op("tensor", lambda e, t=t, selb=selb: e.matmul(
                b.ps_s[0:64, 0:16], selb,
                b.a_glb[:, 16 * t:16 * (t + 1)],
                start=True, stop=True, tile_position=(0, 0)),
                deps=merge(env.now("vector"), prev_dg3))
            dg2 = env.op("tensor", lambda e, t=t, selb=selb: e.matmul(
                b.ps_s[64:128, 0:16], selb,
                b.a_glb[:, 16 * t:16 * (t + 1)],
                start=True, stop=True, tile_position=(0, 64)))
            dg3 = env.op("scalar", lambda e: e.activation(
                b.a_egS[:, 0:16], b.ps_s[:, 0:16], AF.Exp), deps=dg2)
            prev_dg3 = dg3
            d_su = None
            for h in range(16):
                hb = 64 * (h % 2)
                jc = 64 * (h // 2)
                d_su = env.op("vector", lambda e, h=h, hb=hb, jc=jc:
                              e.scalar_tensor_tensor(
                                  b.a_S[hb:hb + 64, jc:jc + 64],
                                  b.a_S[hb:hb + 64, jc:jc + 64],
                                  b.a_egS[hb:hb + 64, h:h + 1],
                                  b.ps_d[hb:hb + 64, jc:jc + 64],
                                  ALU.mult, ALU.add),
                              deps=merge(dmm4, dg3) if h == 0 else None)
            env.op("vector", lambda e: e.tensor_copy(b.a_Sbf[:, :],
                                                     b.a_S[:, :]), deps=d_su)
        env.barrier()

        if STOP < 6:
            continue
        for t in range(nt):
            dsq = env.op("scalar", lambda e, t=t: e.activation(
                b.a_sqsc[:, :], b.a_o[:, 1024 * t:1024 * (t + 1)], AF.Square))
            for h in range(16):
                env.op("vector", lambda e, h=h: e.reduce_sum(
                    b.a_sc2[:, h:h + 1], b.a_sqsc[:, 64 * h:64 * (h + 1)],
                    axis=AX.X), deps=dsq if h == 0 else None)
            dsq2 = env.op("scalar", lambda e: e.activation(
                b.a_sc2[:, 16:32], b.a_sc2[:, 0:16], AF.Sqrt,
                bias=consts["eps_rms"], scale=1.0 / 64.0),
                deps=env.now("vector"))
            drc = env.op("vector", lambda e: e.reciprocal(
                b.a_sc2[:, 32:48], b.a_sc2[:, 16:32]), deps=dsq2)
            dsil = env.op("scalar", lambda e, t=t: e.activation(
                b.a_gate[:, 1024 * t:1024 * (t + 1)],
                b.a_gate[:, 1024 * t:1024 * (t + 1)], AF.Silu), deps=drc)
            for h in range(16):
                cc = 1024 * t + 64 * h
                env.op("vector", lambda e, cc=cc, h=h: e.scalar_tensor_tensor(
                    b.a_o[:, cc:cc + 64], b.a_o[:, cc:cc + 64],
                    b.a_sc2[:, 32 + h:33 + h], consts["normw64"],
                    ALU.mult, ALU.mult), deps=drc if h == 0 else None)
            dog = env.op("vector", lambda e, t=t: e.tensor_mul(
                b.a_o[:, 1024 * t:1024 * (t + 1)],
                b.a_o[:, 1024 * t:1024 * (t + 1)],
                b.a_gate[:, 1024 * t:1024 * (t + 1)]),
                deps=dsil)
            blocks = [(b.a_o[:, 1024 * t + 128 * c:1024 * t + 128 * (c + 1)],
                       b.a_tT[:, seg * c + 128 * t:seg * c + 128 * t + 128])
                      for c in range(8)]
            emit_transpose_blocks(env, b, blocks, ident, deps=dog)
        env.barrier()
        dw = emit_load_weight(env, W["wo_h"], W["w_buf"], 1024, 1024)
        emit_proj_f(env, b, b.a_tT, W["w_buf"], b.a_qT, seg, mm, deps=dw,
                    tT_w=seg)
        env.barrier()
        prev_tr = None
        for t in range(nt):
            dl = env.dma("sync", lambda e, r=srow + 128 * t: e.dma_start(
                out=b.xf[:, :], in_=x_hbm[r:r + 128, :]))
            blocks = [(b.a_qT[:, seg * c + 128 * t: seg * c + 128 * t + 128],
                       b.tn[:, 128 * c:128 * (c + 1)]) for c in range(8)]
            d_tr = emit_transpose_blocks(env, b, blocks, ident, deps=prev_tr)
            prev_tr = d_tr
            d_add = env.op("vector", lambda e: e.tensor_add(
                b.xo[:, :], b.xf[:, :], b.tn[:, :]), deps=merge(dl, d_tr))
            d_st = env.dma("sync", lambda e, r=srow + 128 * t: e.dma_start(
                out=xo_hbm[r:r + 128, :], in_=b.xo[:, :]), deps=d_add)
            env._wait("sync", d_st)
            env._wait("vector", d_st)
        env.barrier()
    return env.all_now()



class Branch:

    def __init__(self, env, cond):
        self.env = env
        self.cond = cond

    def __enter__(self):
        env = self.env
        self.saved_prog = {e: env.prog[e] for e in ENGINES}
        self.saved_count = dict(env.count)
        for e in ENGINES:
            env.prog[e] = []
        return self

    def __exit__(self, et, ev, tb):
        env = self.env
        if et is not None:
            return False
        body = {e: env.prog[e] for e in ENGINES}
        for e in ENGINES:
            env.prog[e] = self.saved_prog[e]
        deltas = {}
        for k, v in env.count.items():
            d = v - self.saved_count.get(k, 0)
            if d > 0:
                deltas[k] = d
        cond = self.cond
        for e in ENGINES:
            fns = body[e]
            eng_deltas = [(k, d) for k, d in deltas.items()
                          if k == e or k.startswith(e + "_d")]
            if not fns and not eng_deltas:
                continue

            pe_drain = (e == "tensor"
                        and self.saved_count.get("tensor", 0) > 0)

            def go(eng, fns=fns, eng_deltas=eng_deltas, cond=cond, env=env,
                   pe_drain=pe_drain, pe_tgt=self.saved_count.get("tensor",
                                                                  0)):
                if pe_drain:
                    eng.wait_ge(env.sems["tensor"], pe_tgt)
                left, right, op = cond(eng)
                with eng.If_cmp(left, right, op):
                    for f in fns:
                        f(eng)
                with eng.Else():
                    for k, d in eng_deltas:
                        if "_d" in k:
                            for _ in range(d // 16):
                                eng.sem_inc(env.sems[k], 16)
                        else:
                            eng.sem_inc(env.sems[k], d)
            env.prog[e].append(go)
        return False



def alloc_moe_bufs(nc, b, Tc):
    b.m_tT = nc.alloc_sbuf_tensor("m_tT", [128, 1024], BF16)
    b.m_logit = nc.alloc_sbuf_tensor("m_logit", [128, 8], F32)
    b.m_exp = nc.alloc_sbuf_tensor("m_exp", [128, 8], F32)
    b.m_stat = nc.alloc_sbuf_tensor("m_stat", [128, 16], F32)
    b.m_oh = nc.alloc_sbuf_tensor("m_oh", [128, 8], F32)
    b.m_pref = nc.alloc_sbuf_tensor("m_pref", [128, 8], F32)
    b.m_runbase = nc.alloc_sbuf_tensor("m_runbase", [1, 8], F32)
    b.m_tot = nc.alloc_sbuf_tensor("m_tot", [1, 8], F32)
    b.m_rw = nc.alloc_sbuf_tensor("m_rw", [128, 4], F32)
    b.m_base = nc.alloc_sbuf_tensor("m_base", [1, 8], F32)
    b.m_basebc = nc.alloc_sbuf_tensor("m_basebc", [128, 8], F32)
    b.m_cnti = nc.alloc_sbuf_tensor("m_cnti", [1, 8], I32)
    b.m_cntf = nc.alloc_sbuf_tensor("m_cntf", [1, 8], F32)
    b.m_cnts8 = nc.alloc_sbuf_tensor("m_cnts8", [8, 8], F32)
    b.m_mym = nc.alloc_sbuf_tensor("m_mym", [8, 1], F32)
    b.m_ein = nc.alloc_sbuf_tensor("m_ein", [128, 1024], BF16)
    b.m_einT = nc.alloc_sbuf_tensor("m_einT", [128, 1024], BF16)
    b.m_h = nc.alloc_sbuf_tensor("m_h", [128, 4096], BF16)
    b.m_hT = nc.alloc_sbuf_tensor("m_hT", [128, 4096], BF16)
    b.m_out = nc.alloc_sbuf_tensor("m_out", [128, 1024], BF16)
    b.m_eo = nc.alloc_sbuf_tensor("m_eo", [128, 1024], BF16)
    b.m_t2s = nc.alloc_sbuf_tensor("m_t2s", [128, 1024], BF16)
    b.m_slotf = nc.alloc_sbuf_tensor("m_slotf", [128, 1], F32)
    b.m_sloti = nc.alloc_sbuf_tensor("m_sloti", [128, 1], I32)
    b.m_w1 = nc.alloc_sbuf_tensor("m_w1", [128, 32768], BF16)
    b.m_w2 = nc.alloc_sbuf_tensor("m_w2", [128, 32768], BF16)


def emit_moe_prep(env, b, mm, x_hbm, row0, t2_hbm, t2row0, rout_hbm,
                  cnt_hbm, wr_sb, consts, n_tiles, ident):
    env.op("vector", lambda e: e.memset(b.m_runbase[:, :], 0.0))
    env.op("vector", lambda e: e.memset(b.m_rw[:, :], 0.0))
    prev = None
    for t in range(n_tiles):
        dl = env.dma("sync", lambda e, r=row0 + 128 * t: e.dma_start(
            out=b.xf[:, :], in_=x_hbm[r:r + 128, :]), deps=prev)
        dln = emit_ln_tile(env, b, b.xf[:, :], b.tn[:, :], consts["eps_ln"],
                           deps=dl)
        dst = env.dma("sync", lambda e, r=t2row0 + 128 * t: e.dma_start(
            out=t2_hbm[r:r + 128, :], in_=b.tn[:, :]), deps=dln)
        blocks = [(b.tn[:, 128 * c:128 * (c + 1)],
                   b.m_tT[:, 128 * c:128 * (c + 1)]) for c in range(8)]
        dtr = emit_transpose_blocks(env, b, blocks, ident, deps=dln)
        mms = []
        for c in range(8):
            mms.append(lambda e, p, st, sp, c=c: e.matmul(
                p[:, 0:8], b.m_tT[:, 128 * c:128 * (c + 1)],
                wr_sb[:, 8 * c:8 * (c + 1)], start=st, stop=sp))
        d_lg = mm.group(mms, lambda e, p: e.tensor_copy(b.m_logit[:, :],
                                                        p[:, 0:8]),
                        deps=dtr, width=8)
        d1 = env.op("vector", lambda e: e.reduce_max(
            b.m_stat[:, 0:1], b.m_logit[:, :], axis=AX.X), deps=d_lg)
        d2 = env.op("vector", lambda e: e.tensor_scalar_mul(
            b.m_stat[:, 1:2], b.m_stat[:, 0:1], -1.0))
        d3 = env.op("vector", lambda e: e.tensor_scalar(
            b.m_oh[:, :], b.m_logit[:, :], b.m_stat[:, 0:1], None,
            ALU.is_equal))
        d4 = env.op("vector", lambda e: e.tensor_mul(
            b.m_exp[:, :], b.m_oh[:, :], consts["iota8r"]))
        d5 = env.op("vector", lambda e: e.reduce_max(
            b.m_stat[:, 2:3], b.m_exp[:, :], axis=AX.X))
        d6 = env.op("vector", lambda e: e.tensor_scalar(
            b.m_stat[:, 3:4], b.m_stat[:, 2:3], -1.0, 7.0, ALU.mult, ALU.add))
        d7 = env.op("scalar", lambda e: e.activation(
            b.m_exp[:, :], b.m_logit[:, :], AF.Exp, bias=b.m_stat[:, 1:2]),
            deps=merge(d2, d5))
        d8 = env.op("vector", lambda e: e.reduce_sum(
            b.m_stat[:, 4:5], b.m_exp[:, :], axis=AX.X), deps=d7)
        d9 = env.op("vector", lambda e: e.reciprocal(
            b.m_stat[:, 5:6], b.m_stat[:, 4:5]))
        d10 = env.op("vector", lambda e: e.tensor_scalar(
            b.m_oh[:, :], consts["iota8"], b.m_stat[:, 3:4], None,
            ALU.is_equal), deps=d6)
        d11 = env.op("tensor", lambda e: e.matmul(
            b.ps_aux[:, 0:8], consts["cumt"], b.m_oh[:, :], start=True,
            stop=True), deps=d10)
        d12 = env.op("tensor", lambda e: e.matmul(
            b.ps_aux[0:1, 8:16], consts["ones128c"], b.m_oh[:, :],
            start=True, stop=True), deps=None)
        d13 = env.op("vector", lambda e: e.tensor_copy(
            b.m_pref[:, :], b.ps_aux[:, 0:8]), deps=d12)
        d14 = env.op("vector", lambda e: e.tensor_copy(
            b.m_tot[:, :], b.ps_aux[0:1, 8:16]))
        d15 = env.op("tensor", lambda e: e.matmul(
            b.ps_sc[:, 0:8], consts["onesrow"], b.m_runbase[:, :],
            start=True, stop=True), deps=merge(d13, d14))
        d16 = env.op("vector", lambda e: e.tensor_add(
            b.m_exp[:, :], b.m_pref[:, :], b.ps_sc[:, 0:8]), deps=d15)
        d17 = env.op("vector", lambda e: e.tensor_mul(
            b.m_exp[:, :], b.m_exp[:, :], b.m_oh[:, :]))
        d18 = env.op("vector", lambda e: e.reduce_sum(
            b.m_stat[:, 6:7], b.m_exp[:, :], axis=AX.X))
        d19 = env.op("vector", lambda e: e.tensor_scalar(
            b.m_stat[:, 6:7], b.m_stat[:, 6:7], -1.0, None, ALU.add))
        d20 = env.op("vector", lambda e: e.tensor_add(
            b.m_runbase[:, :], b.m_runbase[:, :], b.m_tot[:, :]))
        d21 = env.op("vector", lambda e: e.tensor_copy(
            b.m_rw[:, 0:1], b.m_stat[:, 3:4]), deps=d9)
        d22 = env.op("vector", lambda e: e.tensor_copy(
            b.m_rw[:, 1:2], b.m_stat[:, 5:6]))
        d23 = env.op("vector", lambda e: e.tensor_copy(
            b.m_rw[:, 2:3], b.m_stat[:, 6:7]), deps=d19)
        dw = env.dma("sync", lambda e, r=128 * t: e.dma_start(
            out=rout_hbm[r:r + 128, :], in_=b.m_rw[:, :]),
            deps=merge(d21, d22, d23))
        prev = merge(dw, dst, d20)
    dcnt = env.dma("sync", lambda e: e.dma_start(
        out=cnt_hbm[:, :], in_=b.m_runbase[:, :]), deps=prev)
    return dcnt


def emit_moe_base(env, b, cnts_hbm, mym_hbm, mycnt_hbm, consts, deps=None):
    d1 = env.dma("sync", lambda e: e.dma_start(out=b.m_cnts8[:, :],
                                               in_=cnts_hbm[:, :]), deps=deps)
    d1b = env.dma("sync", lambda e: e.dma_start(out=b.m_mym[:, :],
                                                in_=mym_hbm[:, :]))
    d1c = env.dma("sync", lambda e: e.dma_start(out=b.m_cntf[:, :],
                                                in_=mycnt_hbm[:, :]))
    d2 = env.op("tensor", lambda e: e.matmul(
        b.ps_aux[0:1, 0:8], b.m_mym[:, :], b.m_cnts8[:, :], start=True,
        stop=True), deps=merge(d1, d1b))
    d3 = env.op("vector", lambda e: e.tensor_copy(b.m_base[:, :],
                                                  b.ps_aux[0:1, 0:8]), deps=d2)
    d4 = env.op("tensor", lambda e: e.matmul(
        b.ps_aux[:, 8:16], consts["onesrow"], b.m_base[:, :], start=True,
        stop=True), deps=d3)
    d5 = env.op("vector", lambda e: e.tensor_copy(b.m_basebc[:, :],
                                                  b.ps_aux[:, 8:16]), deps=d4)
    d6 = env.op("vector", lambda e: e.tensor_copy(b.m_cnti[:, :],
                                                  b.m_cntf[:, :]), deps=d1c)
    return merge(d5, d6)


def emit_moe_zero_ein(env, b, ein_hbm, Tc, deps=None):
    dz = env.op("vector", lambda e: e.memset(b.m_h[:, :], 0.0), deps=deps)
    er = ein_hbm.ap().rearrange("(a p) n -> p a n", p=128)
    na = 8 * Tc // 128
    last = None
    for a0 in range(0, na, 4):
        last = env.dma("gpsimd", lambda e, a0=a0: e.dma_start(
            out=er[:, a0:a0 + 4, :],
            in_=b.m_h.ap().rearrange("p (a n) -> p a n", n=1024)), deps=dz)
        dz = None
    return last


def emit_moe_scatter(env, b, t2_hbm, t2row0, rout_hbm, ein_hbm, consts,
                     n_tiles, Tc):
    prev = None
    for t in range(n_tiles):
        d1 = env.dma("sync", lambda e, r=128 * t: e.dma_start(
            out=b.m_rw[:, :], in_=rout_hbm[r:r + 128, :]), deps=prev)
        d2 = env.dma("sync", lambda e, r=t2row0 + 128 * t: e.dma_start(
            out=b.m_t2s[:, :], in_=t2_hbm[r:r + 128, :]))
        d3 = env.op("vector", lambda e: e.tensor_scalar(
            b.m_slotf[:, :], b.m_rw[:, 0:1], float(Tc), b.m_rw[:, 2:3],
            ALU.mult, ALU.add), deps=d1)
        d4 = env.op("vector", lambda e: e.tensor_copy(
            b.m_sloti[:, :], b.m_slotf[:, :]))
        d5 = env.dma("gpsimd", lambda e: e.indirect_dma_start(
            ein_hbm[:, :], bass.IndirectOffsetOnAxis(b.m_sloti[:, 0:1], 0),
            b.m_t2s[:, :], None), deps=merge(d4, d2))
        prev = d5
    return prev


def emit_moe_expert(env, b, mm, ein_hbm, eout_hbm, w1_hbm, w2_hbm, ident,
                    Tc, n_exp=8):
    import os
    XSUB = int(os.environ.get("MOE_XSUB", "9"))
    NEXP = int(os.environ.get("MOE_NEXP", str(n_exp)))
    NJ = int(os.environ.get("MOE_NJ", "8"))
    NK = int(os.environ.get("MOE_NK", "0"))
    nc = env.nc
    if not hasattr(env, "_moe_regs"):
        env._moe_regs = {}
    regs = env._moe_regs

    def loadreg(engine, e_idx):
        def go(eng, engine=engine, e_idx=e_idx):
            key = (engine, "cnt")
            if key not in regs:
                regs[key] = eng.alloc_register(name=f"cnt_{engine}")
                regs[(engine, "k")] = eng.alloc_register(name=f"k_{engine}")
            eng.reg_load(regs[key], b.m_cnti[0:1, e_idx:e_idx + 1])
        env.raw(engine, go)

    w1r = w1_hbm.ap().rearrange("E (c p) n -> p E c n", p=128)
    w2r = w2_hbm.ap().rearrange("E (c p) n -> p E c n", p=128)
    for e_idx in range(NEXP):
        w1out = b.m_w1.ap().rearrange("p (c n) -> p c n", n=4096)
        w2out = b.m_w2.ap().rearrange("p (c n) -> p c n", n=1024)
        dw1s = []
        dep0 = env.all_now()
        for j in range(8):
            dw1s.append(env.dma("gpsimd", lambda e, e_idx=e_idx, j=j:
                                e.dma_start(
                                    out=w1out[:, :, 512 * j:512 * (j + 1)],
                                    in_=w1r[:, e_idx, :,
                                            512 * j:512 * (j + 1)]),
                                deps=dep0 if j == 0 else None))
        dw2s = []
        for j2 in range(2):
            dw2s.append(env.dma("gpsimd", lambda e, e_idx=e_idx, j2=j2:
                                e.dma_start(
                                    out=w2out[:, :, 512 * j2:512 * (j2 + 1)],
                                    in_=w2r[:, e_idx, :,
                                            512 * j2:512 * (j2 + 1)])))
        dw1 = dw1s[0]
        dw2 = dw2s[-1]
        for engine in ENGINES:
            loadreg(engine, e_idx)
        if XSUB < 2:
            continue
        for k in range(NK if NK else Tc // 128):
            def cond(eng, k=k):
                kreg = regs[(eng_name_of(eng), "k")]
                eng.reg_mov(kreg, 128 * k)
                return (kreg, regs[(eng_name_of(eng), "cnt")], "IS_LT")
            with Branch(env, cond):
                row = e_idx * Tc + 128 * k
                d1 = env.dma("sync", lambda e, row=row: e.dma_start(
                    out=b.m_ein[:, :], in_=ein_hbm[row:row + 128, :]),
                    deps=merge(env.now("tensor"), env.now("vector"),
                               env.now("scalar")))
                if XSUB < 3:
                    continue
                blocks = [(b.m_ein[:, 128 * c:128 * (c + 1)],
                           b.m_einT[:, 128 * c:128 * (c + 1)])
                          for c in range(8)]
                dtr = emit_transpose_blocks(env, b, blocks, ident, deps=d1)
                if XSUB < 4:
                    continue
                for j in range(NJ):
                    mms = []
                    for c in range(8):
                        mms.append(lambda e, p, st, sp, c=c, j=j: e.matmul(
                            p, b.m_einT[:, 128 * c:128 * (c + 1)],
                            b.m_w1[:, 4096 * c + 512 * j:
                                   4096 * c + 512 * j + 512],
                            start=st, stop=sp))
                    wdep = dw1s[j] if k == 0 else None
                    if os.environ.get("MOE_RELU", "1") == "1":
                        mm.group(mms, lambda e, p, j=j: e.activation(
                            b.m_h[:, 512 * j:512 * (j + 1)], p, AF.Relu),
                            deps=merge(dtr if j == 0 else None, wdep),
                            evac_engine="scalar")
                    else:
                        mm.group(mms, lambda e, p, j=j: e.tensor_copy(
                            b.m_h[:, 512 * j:512 * (j + 1)], p),
                            deps=merge(dtr if j == 0 else None, wdep))
                if XSUB < 5:
                    continue
                blocks = [(b.m_h[:, 128 * fc:128 * (fc + 1)],
                           b.m_hT[:, 128 * fc:128 * (fc + 1)])
                          for fc in range(32)]
                dtr2 = emit_transpose_blocks(env, b, blocks, ident,
                                             deps=env.now("scalar"))
                for j2 in range(2):
                    mms = []
                    for fc in range(32):
                        mms.append(lambda e, p, st, sp, fc=fc, j2=j2:
                                   e.matmul(
                                       p, b.m_hT[:, 128 * fc:128 * (fc + 1)],
                                       b.m_w2[:, 1024 * fc + 512 * j2:
                                              1024 * fc + 512 * j2 + 512],
                                       start=st, stop=sp))
                    mm.group(mms, lambda e, p, j2=j2: e.tensor_copy(
                        b.m_out[:, 512 * j2:512 * (j2 + 1)], p),
                        deps=merge(dtr2 if j2 == 0 else None,
                                   dw2s[j2] if k == 0 else None))
                dst = env.dma("sync", lambda e, row=row: e.dma_start(
                    out=eout_hbm[row:row + 128, :], in_=b.m_out[:, :]),
                    deps=env.now("vector"))
                env._wait("sync", dst)
            for i, d in enumerate(mm.evac_dep):
                if d:
                    mm.evac_dep[i] = {kk: env.count[kk] for kk in d}
    env.barrier()


def eng_name_of(eng):
    import concourse.mybir as mb
    m = {mb.EngineType.PE: "tensor", mb.EngineType.Activation: "scalar",
         mb.EngineType.Pool: "gpsimd", mb.EngineType.DVE: "vector",
         mb.EngineType.SP: "sync"}
    return m[eng.engine]


def emit_moe_combine(env, b, x_hbm, row0, xo_hbm, t2_hbm, t2row0, rout_hbm,
                     eout_hbm, consts, n_tiles, Tc, cap):
    prev = None
    for t in range(n_tiles):
        d1 = env.dma("sync", lambda e, r=128 * t: e.dma_start(
            out=b.m_rw[:, :], in_=rout_hbm[r:r + 128, :]), deps=prev)
        d2 = env.dma("sync", lambda e, r=t2row0 + 128 * t: e.dma_start(
            out=b.m_t2s[:, :], in_=t2_hbm[r:r + 128, :]))
        d2b = env.dma("sync", lambda e, r=row0 + 128 * t: e.dma_start(
            out=b.xf[:, :], in_=x_hbm[r:r + 128, :]))
        d3 = env.op("vector", lambda e: e.tensor_scalar(
            b.m_slotf[:, :], b.m_rw[:, 0:1], float(Tc), b.m_rw[:, 2:3],
            ALU.mult, ALU.add), deps=d1)
        d4 = env.op("vector", lambda e: e.tensor_copy(
            b.m_sloti[:, :], b.m_slotf[:, :]))
        d5 = env.dma("gpsimd", lambda e: e.indirect_dma_start(
            b.m_eo[:, :], None,
            eout_hbm[:, :], bass.IndirectOffsetOnAxis(b.m_sloti[:, 0:1], 0)),
            deps=d4)
        d6 = env.op("vector", lambda e: e.tensor_scalar(
            b.m_oh[:, :], consts["iota8"], b.m_rw[:, 0:1], None,
            ALU.is_equal), deps=d1)
        d7 = env.op("vector", lambda e: e.tensor_mul(
            b.m_exp[:, :], b.m_oh[:, :], b.m_basebc[:, :]))
        d8 = env.op("vector", lambda e: e.reduce_sum(
            b.m_stat[:, 8:9], b.m_exp[:, :], axis=AX.X))
        d9 = env.op("vector", lambda e: e.tensor_add(
            b.m_stat[:, 9:10], b.m_stat[:, 8:9], b.m_rw[:, 2:3]))
        d10 = env.op("vector", lambda e: e.tensor_scalar(
            b.m_stat[:, 10:11], b.m_stat[:, 9:10], float(cap), None,
            ALU.is_lt), deps=d9)
        d11 = env.op("vector", lambda e: e.scalar_tensor_tensor(
            b.m_eo[:, :], b.m_eo[:, :], b.m_rw[:, 1:2], b.m_t2s[:, :],
            ALU.mult, ALU.subtract), deps=merge(d5, d2, d10))
        d12 = env.op("vector", lambda e: e.scalar_tensor_tensor(
            b.m_eo[:, :], b.m_eo[:, :], b.m_stat[:, 10:11], b.m_t2s[:, :],
            ALU.mult, ALU.add))
        d13 = env.op("vector", lambda e: e.tensor_add(
            b.xo[:, :], b.xf[:, :], b.m_eo[:, :]), deps=d2b)
        d14 = env.dma("sync", lambda e, r=row0 + 128 * t: e.dma_start(
            out=xo_hbm[r:r + 128, :], in_=b.xo[:, :]), deps=d13)
        prev = merge(d14, d12)
    return prev

import ml_dtypes
from contextlib import ExitStack
import concourse.bass as bass

SEQK = 2048
TCK = 4096
NTK = TCK // 128
CAPK = 5120



def host_const_arrays():
    import numpy as np
    cc = {}
    cc["c_ident_bf"] = np.eye(128, dtype=ml_dtypes.bfloat16)
    cc["c_ident_f"] = np.eye(128, dtype=np.float32)
    cum = np.triu(np.ones((64, 64), np.float32))
    z = np.zeros((64, 64), np.float32)
    cc["c_cum2"] = np.block([[cum, z], [z, cum]])
    sel = np.zeros((128, 128), np.float32)
    sel[63, 0:64] = 1.0
    sel[127, 64:128] = 1.0
    cc["c_sellast"] = sel
    om = np.zeros((16, 1024), np.float32)
    for h in range(16):
        om[h, 64 * h:64 * (h + 1)] = 1.0
    cc["c_onesmask"] = om
    tril = np.tril(np.ones((64, 64), np.float32))
    strict = np.tril(np.ones((64, 64), np.float32), -1)
    trilw = np.tile(np.concatenate([tril, tril], 0), (1, 8))
    strictw = np.tile(np.concatenate([strict, strict], 0), (1, 8))
    triuw = np.tile(np.concatenate([tril.T, tril.T], 0), (1, 8))
    cc["c_masks"] = np.concatenate([trilw, strictw, triuw, -triuw], 1)
    selb = np.zeros((128, 128), np.float32)
    selb[0, 0:64] = 1.0
    selb[64, 64:128] = 1.0
    cc["c_selb"] = selb
    scal = np.zeros((128, 8), np.float32)
    scal[:, 0] = 1e-5
    scal[:, 1] = 1e-6
    scal[:, 2] = 64e-6
    scal[:, 3] = 1e-5
    cc["c_scal"] = scal
    cc["c_iota8"] = np.tile(np.arange(8, dtype=np.float32), (128, 1))
    cc["c_iota8r"] = np.tile(np.arange(7, -1, -1, dtype=np.float32),
                             (128, 1))
    cc["c_cumt"] = np.triu(np.ones((128, 128), np.float32))
    cc["c_ones128c"] = np.ones((128, 1), np.float32)
    cc["c_onesrow"] = np.ones((1, 128), np.float32)
    nm = np.zeros((128, 2048), np.float32)
    for v in range(4):
        for p in range(128):
            nm[p, 512 * v: 512 * v + 512] = np.where(
                np.arange(512) <= 128 * v + p, 0.0, -1e30)
    cc["c_negmask"] = nm
    return cc


def _declare_consts(nc, names):
    shapes = {
        "c_ident_bf": ([128, 128], BF16), "c_ident_f": ([128, 128], F32),
        "c_cum2": ([128, 128], F32), "c_sellast": ([128, 128], F32),
        "c_onesmask": ([16, 1024], F32), "c_masks": ([128, 2048], F32),
        "c_selb": ([128, 128], F32), "c_scal": ([128, 8], F32),
        "c_iota8": ([128, 8], F32), "c_iota8r": ([128, 8], F32),
        "c_cumt": ([128, 128], F32), "c_ones128c": ([128, 1], F32),
        "c_onesrow": ([1, 128], F32), "c_negmask": ([128, 2048], F32),
    }
    out = {}
    for n in names:
        sh, dt = shapes[n]
        out[n] = nc.dram_tensor(n, sh, dt, kind="ExternalInput")
    return out


def _alloc_prep_bufs(nc, b):
    b.m_tT = nc.alloc_sbuf_tensor("m_tT", [128, 1024], BF16)
    b.m_logit = nc.alloc_sbuf_tensor("m_logit", [128, 8], F32)
    b.m_exp = nc.alloc_sbuf_tensor("m_exp", [128, 8], F32)
    b.m_stat = nc.alloc_sbuf_tensor("m_stat", [128, 16], F32)
    b.m_oh = nc.alloc_sbuf_tensor("m_oh", [128, 8], F32)
    b.m_pref = nc.alloc_sbuf_tensor("m_pref", [128, 8], F32)
    b.m_runbase = nc.alloc_sbuf_tensor("m_runbase", [1, 8], F32)
    b.m_tot = nc.alloc_sbuf_tensor("m_tot", [1, 8], F32)
    b.m_rw = nc.alloc_sbuf_tensor("m_rw", [128, 4], F32)


def _load_consts(env, hd, sbufs):
    for n, ss in sbufs.items():
        env.dma("sync", lambda e, hh=hd[n], ss=ss: e.dma_start(
            out=ss[:, :], in_=hh[:, :]))



def build_launch_A():
    nc = bass.Bass("TRN2", target_bir_lowering=False)
    x_h = nc.dram_tensor("x", [TCK, 1024], F32, kind="ExternalInput")
    wn = ["wq", "wk", "wv", "wg", "wo"]
    H = {n: nc.dram_tensor(n, [1024, 1024], F32, kind="ExternalInput")
         for n in wn}
    wb_h = nc.dram_tensor("wb", [1024, 16], F32, kind="ExternalInput")
    wa_h = nc.dram_tensor("wa", [1024, 16], F32, kind="ExternalInput")
    dtb_h = nc.dram_tensor("dtb", [128, 16], F32, kind="ExternalInput")
    alog_h = nc.dram_tensor("alog", [128, 16], F32, kind="ExternalInput")
    nw_h = nc.dram_tensor("nw64", [128, 64], F32, kind="ExternalInput")
    wr_h = nc.dram_tensor("wr", [1024, 8], F32, kind="ExternalInput")
    hd = _declare_consts(nc, ["c_ident_bf", "c_ident_f", "c_cum2",
                              "c_sellast", "c_onesmask", "c_masks", "c_selb",
                              "c_scal", "c_iota8", "c_iota8r", "c_cumt",
                              "c_ones128c", "c_onesrow"])
    xo_h = nc.dram_tensor("xo", [TCK, 1024], F32, kind="ExternalOutput")
    t2_h = nc.dram_tensor("t2", [TCK, 1024], BF16, kind="ExternalOutput")
    rout_h = nc.dram_tensor("rout", [TCK, 4], F32, kind="ExternalOutput")
    cnts_h = nc.dram_tensor("cnts", [1, 8], F32, kind="ExternalOutput")

    env = Env(nc)
    b = alloc_psum(env)
    alloc_attna_bufs(nc, b, seg=512)
    b.tn = nc.alloc_sbuf_tensor("tn", [128, 1024], BF16)
    b.xf = nc.alloc_sbuf_tensor("xf", [128, 1024], F32)
    b.xo = nc.alloc_sbuf_tensor("xo_sb", [128, 1024], F32)
    b.ln_mean = nc.alloc_sbuf_tensor("ln_mean", [128, 2], F32)
    b.ln_var = nc.alloc_sbuf_tensor("ln_var", [128, 2], F32)
    b.ln_rstd = nc.alloc_sbuf_tensor("ln_rstd", [128, 1], F32)
    b.ln_xm = nc.alloc_sbuf_tensor("ln_xm", [128, 1024], F32)
    b.ln_sq = nc.alloc_sbuf_tensor("ln_sq", [128, 1024], F32)
    _alloc_prep_bufs(nc, b)
    w_buf = nc.alloc_sbuf_tensor("w_buf", [128, 8192], BF16)
    wa_sb = nc.alloc_sbuf_tensor("wa_sb", [128, 128], BF16)
    wb_sb = nc.alloc_sbuf_tensor("wb_sb", [128, 128], BF16)
    wr_sb = nc.alloc_sbuf_tensor("wr_sb", [128, 64], BF16)
    cs = {n: nc.alloc_sbuf_tensor(n + "_s", hd[n].shape, hd[n].dtype)
          for n in hd}
    dtb = nc.alloc_sbuf_tensor("dtb_s", [128, 16], F32)
    alog = nc.alloc_sbuf_tensor("alog_s", [128, 16], F32)
    negexpA = nc.alloc_sbuf_tensor("negexpA_s", [128, 16], F32)
    nw64 = nc.alloc_sbuf_tensor("nw64_s", [128, 64], F32)

    with ExitStack() as stack:
        env.make_sems(stack)
        with nc.Block() as block:
            _load_consts(env, hd, cs)
            for (hh, ss) in ((dtb_h, dtb), (alog_h, alog), (nw_h, nw64)):
                env.dma("sync", lambda e, hh=hh, ss=ss: e.dma_start(
                    out=ss[:, :], in_=hh[:, :]))
            emit_load_weight(env, wa_h, wa_sb, 1024, 16)
            emit_load_weight(env, wb_h, wb_sb, 1024, 16)
            emit_load_weight(env, wr_h, wr_sb, 1024, 8)
            env.barrier()
            d1 = env.op("scalar", lambda e: e.activation(
                negexpA[:, :], alog[:, :], AF.Exp))
            env.op("vector", lambda e: e.tensor_scalar_mul(
                negexpA[:, :], negexpA[:, :], -1.0), deps=d1)
            env.barrier()
            masks = cs["c_masks"]
            c_scal = cs["c_scal"]
            selb = cs["c_selb"]
            consts = {
                "ident": cs["c_ident_bf"][:, :],
                "identf": cs["c_ident_f"][:, :],
                "cum2": cs["c_cum2"][:, :], "sellast": cs["c_sellast"][:, :],
                "onesmask": cs["c_onesmask"],
                "trilw": masks[:, 0:512], "strictw": masks[:, 512:1024],
                "triuw": masks[:, 1024:1536],
                "negtriuw": masks[:, 1536:2048],
                "selb0": selb[:, 0:64], "selb64": selb[:, 64:128],
                "dtb": dtb[:, :], "negexpA": negexpA[:, :],
                "normw64": nw64[:, :],
                "eps_ln": c_scal[:, 0:1], "eps_k": c_scal[:, 1:2],
                "eps_q64": c_scal[:, 2:3], "eps_rms": c_scal[:, 3:4],
                "iota8": cs["c_iota8"][:, :], "iota8r": cs["c_iota8r"][:, :],
                "cumt": cs["c_cumt"][:, :],
                "ones128c": cs["c_ones128c"][:, :],
                "onesrow": cs["c_onesrow"][:, :],
            }
            W = {"wq_h": H["wq"], "wk_h": H["wk"], "wv_h": H["wv"],
                 "wg_h": H["wg"], "wo_h": H["wo"], "wa": wa_sb, "wb": wb_sb,
                 "w_buf": w_buf}
            mm = MMGroups(env, b)
            for bi in range(2):
                emit_attna_batch(env, b, mm, x_h, xo_h, SEQK * bi, W, consts,
                                 seq=SEQK, seg=512)
                env.barrier()
            emit_moe_prep(env, b, mm, xo_h, 0, t2_h, 0, rout_h, cnts_h,
                          wr_sb, consts, NTK, cs["c_ident_bf"][:, :])
            env.barrier()
            env.finalize(block)
    return env



def build_launch_B():
    nc = bass.Bass("TRN2", target_bir_lowering=False,
                   detect_race_conditions=False)
    x_h = nc.dram_tensor("x", [TCK, 1024], F32, kind="ExternalInput")
    t2_h = nc.dram_tensor("t2", [TCK, 1024], BF16, kind="ExternalInput")
    rout_h = nc.dram_tensor("rout", [TCK, 4], F32, kind="ExternalInput")
    cnts8_h = nc.dram_tensor("cnts8", [8, 8], F32, kind="ExternalInput")
    mym_h = nc.dram_tensor("mym", [8, 1], F32, kind="ExternalInput")
    mycnt_h = nc.dram_tensor("mycnt", [1, 8], F32, kind="ExternalInput")
    w1_h = nc.dram_tensor("w1", [8, 1024, 4096], F32, kind="ExternalInput")
    w2_h = nc.dram_tensor("w2", [8, 4096, 1024], F32, kind="ExternalInput")
    hd = _declare_consts(nc, ["c_ident_bf", "c_iota8", "c_onesrow"])
    xo_h = nc.dram_tensor("xo", [TCK, 1024], F32, kind="ExternalOutput")
    ein_hbm = nc.dram_tensor("ein_s", [8 * TCK, 1024], BF16)
    eout_hbm = nc.dram_tensor("eout_s", [8 * TCK, 1024], BF16)

    env = Env(nc)
    b = alloc_psum(env)
    alloc_moe_bufs(nc, b, TCK)
    b.xf = nc.alloc_sbuf_tensor("xf", [128, 1024], F32)
    b.xo = nc.alloc_sbuf_tensor("xo_sb", [128, 1024], F32)
    cs = {n: nc.alloc_sbuf_tensor(n + "_s", hd[n].shape, hd[n].dtype)
          for n in hd}

    with ExitStack() as stack:
        env.make_sems(stack)
        with nc.Block() as block:
            _load_consts(env, hd, cs)
            env.barrier()
            consts = {
                "ident": cs["c_ident_bf"][:, :],
                "iota8": cs["c_iota8"][:, :],
                "onesrow": cs["c_onesrow"][:, :],
            }
            mm = MMGroups(env, b)
            emit_moe_base(env, b, cnts8_h, mym_h, mycnt_h, consts)
            env.barrier()
            emit_moe_scatter(env, b, t2_h, 0, rout_h, ein_hbm, consts,
                             NTK, TCK)
            env.barrier()
            emit_moe_expert(env, b, mm, ein_hbm, eout_hbm, w1_h, w2_h,
                            cs["c_ident_bf"][:, :], TCK)
            env.barrier()
            emit_moe_combine(env, b, x_h, 0, xo_h, t2_h, 0, rout_h,
                             eout_hbm, consts, NTK, TCK, CAPK)
            env.barrier()
            env.finalize(block)
    return env



def build_launch_C():
    nc = bass.Bass("TRN2", target_bir_lowering=False)
    x_h = nc.dram_tensor("x", [TCK, 1024], F32, kind="ExternalInput")
    wq_h = nc.dram_tensor("wq", [1024, 1024], F32, kind="ExternalInput")
    wk_h = nc.dram_tensor("wk", [1024, 1024], F32, kind="ExternalInput")
    wv_h = nc.dram_tensor("wv", [1024, 1024], F32, kind="ExternalInput")
    wo_h = nc.dram_tensor("wo", [1024, 1024], F32, kind="ExternalInput")
    wr_h = nc.dram_tensor("wr", [1024, 8], F32, kind="ExternalInput")
    hd = _declare_consts(nc, ["c_ident_bf", "c_negmask", "c_scal",
                              "c_iota8", "c_iota8r", "c_cumt", "c_ones128c",
                              "c_onesrow"])
    xo_h = nc.dram_tensor("xo", [TCK, 1024], F32, kind="ExternalOutput")
    t2_h = nc.dram_tensor("t2", [TCK, 1024], BF16, kind="ExternalOutput")
    rout_h = nc.dram_tensor("rout", [TCK, 4], F32, kind="ExternalOutput")
    cnts_h = nc.dram_tensor("cnts", [1, 8], F32, kind="ExternalOutput")

    env = Env(nc)
    b = alloc_psum(env)
    alloc_attnb_bufs(nc, b, seq=SEQK)
    _alloc_prep_bufs(nc, b)
    w_buf = nc.alloc_sbuf_tensor("w_buf", [128, 8192], BF16)
    wr_sb = nc.alloc_sbuf_tensor("wr_sb", [128, 64], BF16)
    cs = {n: nc.alloc_sbuf_tensor(n + "_s", hd[n].shape, hd[n].dtype)
          for n in hd}

    with ExitStack() as stack:
        env.make_sems(stack)
        with nc.Block() as block:
            _load_consts(env, hd, cs)
            emit_load_weight(env, wr_h, wr_sb, 1024, 8)
            env.barrier()
            c_scal = cs["c_scal"]
            consts = {
                "ident": cs["c_ident_bf"][:, :],
                "iota8": cs["c_iota8"][:, :], "iota8r": cs["c_iota8r"][:, :],
                "cumt": cs["c_cumt"][:, :],
                "ones128c": cs["c_ones128c"][:, :],
                "onesrow": cs["c_onesrow"][:, :],
                "eps_ln": c_scal[:, 0:1],
            }
            mm = MMGroups(env, b)
            for bi in range(2):
                emit_attnb_batch(env, b, mm, x_h, xo_h, SEQK * bi, wq_h,
                                 wk_h, wv_h, wo_h, w_buf,
                                 cs["c_ident_bf"][:, :], cs["c_negmask"],
                                 c_scal[:, 0:1], seq=SEQK)
                env.barrier()
            emit_moe_prep(env, b, mm, xo_h, 0, t2_h, 0, rout_h, cnts_h,
                          wr_sb, consts, NTK, cs["c_ident_bf"][:, :])
            env.barrier()
            env.finalize(block)
    return env



_DSZ = {F32: 4, BF16: 2, I32: 4, U32: 4}


class ArenaNC:

    def __init__(self, nc, base, suffix, alias=None):
        self.nc = nc
        self.cur = base
        self.suffix = suffix
        self.alias = alias or {}
        self.offs = {}

    def alloc_sbuf_tensor(self, name, shape, dtype):
        if name in self.alias:
            off = self.offs[self.alias[name]]
        else:
            off = self.cur
            nbytes = shape[1] * _DSZ.get(dtype, 4)
            self.cur = (off + nbytes + 31) & ~31
        self.offs[name] = off
        return self.nc.alloc_sbuf_tensor_at(name + self.suffix, shape, dtype,
                                            offset=off)


def _use(b, d):
    for k, v in d.items():
        setattr(b, k, v)


def _alloc_lnset(an, b):
    d = {}
    d["tn"] = an.alloc_sbuf_tensor("tn", [128, 1024], BF16)
    d["xf"] = an.alloc_sbuf_tensor("xf", [128, 1024], F32)
    d["xo"] = an.alloc_sbuf_tensor("xo_sb", [128, 1024], F32)
    d["ln_mean"] = an.alloc_sbuf_tensor("ln_mean", [128, 2], F32)
    d["ln_var"] = an.alloc_sbuf_tensor("ln_var", [128, 2], F32)
    d["ln_rstd"] = an.alloc_sbuf_tensor("ln_rstd", [128, 1], F32)
    d["ln_xm"] = an.alloc_sbuf_tensor("ln_xm", [128, 1024], F32)
    d["ln_sq"] = an.alloc_sbuf_tensor("ln_sq", [128, 1024], F32)
    _use(b, d)
    return d


def build_fused():
    nc = bass.Bass("TRN2", target_bir_lowering=False,
                   detect_race_conditions=False)
    x_h = nc.dram_tensor("x", [TCK, 1024], F32, kind="ExternalInput")
    wn = ["wq", "wk", "wv", "wg", "wo"]
    H = {n: nc.dram_tensor(n, [1024, 1024], F32, kind="ExternalInput")
         for n in wn}
    wb_h = nc.dram_tensor("wb", [1024, 16], F32, kind="ExternalInput")
    wa_h = nc.dram_tensor("wa", [1024, 16], F32, kind="ExternalInput")
    dtb_h = nc.dram_tensor("dtb", [128, 16], F32, kind="ExternalInput")
    alog_h = nc.dram_tensor("alog", [128, 16], F32, kind="ExternalInput")
    nw_h = nc.dram_tensor("nw64", [128, 64], F32, kind="ExternalInput")
    wr_h = nc.dram_tensor("wr", [1024, 8], F32, kind="ExternalInput")
    w1_h = nc.dram_tensor("w1", [8, 1024, 4096], F32, kind="ExternalInput")
    w2_h = nc.dram_tensor("w2", [8, 4096, 1024], F32, kind="ExternalInput")
    B = {n: nc.dram_tensor("b" + n[1], [1024, 1024], F32,
                           kind="ExternalInput") for n in ["wq", "wk", "wv",
                                                           "wo"]}
    mym_h = nc.dram_tensor("mym", [8, 1], F32, kind="ExternalInput")
    hd = _declare_consts(nc, ["c_ident_bf", "c_ident_f", "c_cum2",
                              "c_sellast", "c_onesmask", "c_masks", "c_selb",
                              "c_scal", "c_iota8", "c_iota8r", "c_cumt",
                              "c_ones128c", "c_onesrow", "c_negmask"])
    xo_h = nc.dram_tensor("xo", [TCK, 1024], F32, kind="ExternalOutput")
    xo1_h = nc.dram_tensor("xo1_s", [TCK, 1024], F32)
    xo2_h = nc.dram_tensor("xo2_s", [TCK, 1024], F32)
    xo3_h = nc.dram_tensor("xo3_s", [TCK, 1024], F32)
    t2_h = nc.dram_tensor("t2_s", [TCK, 1024], BF16)
    rout_h = nc.dram_tensor("rout_s", [TCK, 4], F32)
    ein_hbm = nc.dram_tensor("ein_s", [8 * TCK, 1024], BF16)
    eout_hbm = nc.dram_tensor("eout_s", [8 * TCK, 1024], BF16)
    cc_in = [nc.dram_tensor(f"ccin{i}_s", [1, 8], F32) for i in range(2)]
    cc_out = [nc.dram_tensor(f"ccout{i}_s", [8, 8], F32) for i in range(2)]

    env = Env(nc)
    b = alloc_psum(env)
    cs = {n: nc.alloc_sbuf_tensor(n + "_s", hd[n].shape, hd[n].dtype)
          for n in hd}
    wa_sb = nc.alloc_sbuf_tensor("wa_sb", [128, 128], BF16)
    wb_sb = nc.alloc_sbuf_tensor("wb_sb", [128, 128], BF16)
    wr_sb = nc.alloc_sbuf_tensor("wr_sb", [128, 64], BF16)
    dtb = nc.alloc_sbuf_tensor("dtb_s", [128, 16], F32)
    alog = nc.alloc_sbuf_tensor("alog_s", [128, 16], F32)
    negexpA = nc.alloc_sbuf_tensor("negexpA_s", [128, 16], F32)
    nw64 = nc.alloc_sbuf_tensor("nw64_s", [128, 64], F32)
    _alloc_prep_bufs(nc, b)
    arena0 = (nc.sbuf_base + 31) & ~31
    cap = nc.sbuf_top

    anA = ArenaNC(nc, arena0, "_pa", alias={"a_sqsc": "ln_xm"})
    lnA = _alloc_lnset(anA, b)
    w_bufA = anA.alloc_sbuf_tensor("w_buf", [128, 8192], BF16)
    alloc_attna_bufs(anA, b, seg=512)
    assert anA.cur <= cap, ("A arena overflow", anA.cur, cap)

    anB = ArenaNC(nc, arena0, "_pb")
    lnB = {"xf": anB.alloc_sbuf_tensor("xf", [128, 1024], F32),
           "xo": anB.alloc_sbuf_tensor("xo_sb", [128, 1024], F32)}
    moeB = {}
    moeB["m_ein"] = anB.alloc_sbuf_tensor("m_ein", [128, 1024], BF16)
    moeB["m_einT"] = anB.alloc_sbuf_tensor("m_einT", [128, 1024], BF16)
    moeB["m_h"] = anB.alloc_sbuf_tensor("m_h", [128, 4096], BF16)
    moeB["m_hT"] = anB.alloc_sbuf_tensor("m_hT", [128, 4096], BF16)
    moeB["m_out"] = anB.alloc_sbuf_tensor("m_out", [128, 1024], BF16)
    moeB["m_eo"] = anB.alloc_sbuf_tensor("m_eo", [128, 1024], BF16)
    moeB["m_t2s"] = anB.alloc_sbuf_tensor("m_t2s", [128, 1024], BF16)
    moeB["m_slotf"] = anB.alloc_sbuf_tensor("m_slotf", [128, 1], F32)
    moeB["m_sloti"] = anB.alloc_sbuf_tensor("m_sloti", [128, 1], I32)
    moeB["m_w1"] = anB.alloc_sbuf_tensor("m_w1", [128, 32768], BF16)
    moeB["m_w2"] = anB.alloc_sbuf_tensor("m_w2", [128, 32768], BF16)
    moeB["m_base"] = anB.alloc_sbuf_tensor("m_base", [1, 8], F32)
    moeB["m_basebc"] = anB.alloc_sbuf_tensor("m_basebc", [128, 8], F32)
    moeB["m_cnti"] = anB.alloc_sbuf_tensor("m_cnti", [1, 8], I32)
    moeB["m_cntf"] = anB.alloc_sbuf_tensor("m_cntf", [1, 8], F32)
    moeB["m_cnts8"] = anB.alloc_sbuf_tensor("m_cnts8", [8, 8], F32)
    moeB["m_mym"] = anB.alloc_sbuf_tensor("m_mym", [8, 1], F32)
    assert anB.cur <= cap, ("B arena overflow", anB.cur, cap)
    _use(b, moeB)

    anC = ArenaNC(nc, arena0, "_pc", alias={"srow": "ln_xm", "sexp": "xf"})
    lnC = _alloc_lnset(anC, b)
    w_bufC = anC.alloc_sbuf_tensor("w_buf", [128, 8192], BF16)
    cb = {}
    cb["tT"] = anC.alloc_sbuf_tensor("tT", [128, 8 * SEQK], BF16)
    cb["qT"] = anC.alloc_sbuf_tensor("qT", [128, 8 * SEQK], BF16)
    cb["kT"] = anC.alloc_sbuf_tensor("kT", [128, 8 * SEQK], BF16)
    cb["v_tok"] = anC.alloc_sbuf_tensor("v_tok", [128, 16 * 1024], BF16)
    cb["attnT"] = anC.alloc_sbuf_tensor("attnT", [128, 16 * 512], BF16)
    cb["stats"] = anC.alloc_sbuf_tensor("stats", [128, 8], F32)
    cb["srow"] = anC.alloc_sbuf_tensor("srow", [128, SEQK], F32)
    cb["sexp"] = anC.alloc_sbuf_tensor("sexp", [128, SEQK], BF16)
    assert anC.cur <= cap, ("C arena overflow", anC.cur, cap)

    with ExitStack() as stack:
        env.make_sems(stack)
        with nc.Block() as block:
            _load_consts(env, hd, cs)
            for (hh, ss) in ((dtb_h, dtb), (alog_h, alog), (nw_h, nw64)):
                env.dma("sync", lambda e, hh=hh, ss=ss: e.dma_start(
                    out=ss[:, :], in_=hh[:, :]))
            emit_load_weight(env, wa_h, wa_sb, 1024, 16)
            emit_load_weight(env, wb_h, wb_sb, 1024, 16)
            emit_load_weight(env, wr_h, wr_sb, 1024, 8)
            env.barrier()
            d1 = env.op("scalar", lambda e: e.activation(
                negexpA[:, :], alog[:, :], AF.Exp))
            env.op("vector", lambda e: e.tensor_scalar_mul(
                negexpA[:, :], negexpA[:, :], -1.0), deps=d1)
            env.barrier()
            masks = cs["c_masks"]
            c_scal = cs["c_scal"]
            selb = cs["c_selb"]
            consts = {
                "ident": cs["c_ident_bf"][:, :],
                "identf": cs["c_ident_f"][:, :],
                "cum2": cs["c_cum2"][:, :], "sellast": cs["c_sellast"][:, :],
                "onesmask": cs["c_onesmask"],
                "trilw": masks[:, 0:512], "strictw": masks[:, 512:1024],
                "triuw": masks[:, 1024:1536],
                "negtriuw": masks[:, 1536:2048],
                "selb0": selb[:, 0:64], "selb64": selb[:, 64:128],
                "dtb": dtb[:, :], "negexpA": negexpA[:, :],
                "normw64": nw64[:, :],
                "eps_ln": c_scal[:, 0:1], "eps_k": c_scal[:, 1:2],
                "eps_q64": c_scal[:, 2:3], "eps_rms": c_scal[:, 3:4],
                "iota8": cs["c_iota8"][:, :], "iota8r": cs["c_iota8r"][:, :],
                "cumt": cs["c_cumt"][:, :],
                "ones128c": cs["c_ones128c"][:, :],
                "onesrow": cs["c_onesrow"][:, :],
            }
            ident = cs["c_ident_bf"][:, :]
            W = {"wq_h": H["wq"], "wk_h": H["wk"], "wv_h": H["wv"],
                 "wg_h": H["wg"], "wo_h": H["wo"], "wa": wa_sb, "wb": wb_sb,
                 "w_buf": w_bufA}
            mm = MMGroups(env, b)

            _use(b, lnA)
            for bi in range(2):
                emit_attna_batch(env, b, mm, x_h, xo1_h, SEQK * bi, W,
                                 consts, seq=SEQK, seg=512)
                env.barrier()
            emit_moe_prep(env, b, mm, xo1_h, 0, t2_h, 0, rout_h, cc_in[0],
                          wr_sb, consts, NTK, ident)
            env.barrier()

            def moe_round(cci, cco, xin_h, xout_h):
                dcc = env.dma("gpsimd", lambda e: e.collective_compute(
                    "AllGather", ALU.bypass,
                    replica_groups=[list(range(8))],
                    ins=[cci.ap().opt()], outs=[cco.ap().opt()]))
                emit_moe_base(env, b, cco, mym_h, cci, consts, deps=dcc)
                env.barrier()
                _use(b, lnB)
                emit_moe_zero_ein(env, b, ein_hbm, TCK)
                emit_moe_zero_ein(env, b, eout_hbm, TCK)
                env.barrier()
                emit_moe_scatter(env, b, t2_h, 0, rout_h, ein_hbm, consts,
                                 NTK, TCK)
                env.barrier()
                emit_moe_expert(env, b, mm, ein_hbm, eout_hbm, w1_h, w2_h,
                                ident, TCK)
                env.barrier()
                emit_moe_combine(env, b, xin_h, 0, xout_h, t2_h, 0, rout_h,
                                 eout_hbm, consts, NTK, TCK, CAPK)
                env.barrier()

            moe_round(cc_in[0], cc_out[0], xo1_h, xo2_h)

            _use(b, lnC)
            _use(b, cb)
            for bi in range(2):
                emit_attnb_batch(env, b, mm, xo2_h, xo3_h, SEQK * bi,
                                 B["wq"], B["wk"], B["wv"], B["wo"], w_bufC,
                                 ident, cs["c_negmask"], c_scal[:, 0:1],
                                 seq=SEQK)
                env.barrier()
            emit_moe_prep(env, b, mm, xo3_h, 0, t2_h, 0, rout_h, cc_in[1],
                          wr_sb, consts, NTK, ident)
            env.barrier()

            moe_round(cc_in[1], cc_out[1], xo3_h, xo_h)
            env.finalize(block)
    return env



_ENV_CACHE = {}


def kernel(**inputs):
    import numpy as np
    from concourse.bass_utils import run_bass_kernel_spmd

    if "A" not in _ENV_CACHE:
        _ENV_CACHE["A"] = build_launch_A()
        _ENV_CACHE["B"] = build_launch_B()
        _ENV_CACHE["C"] = build_launch_C()
    envA, envB, envC = _ENV_CACHE["A"], _ENV_CACHE["B"], _ENV_CACHE["C"]

    cc = host_const_arrays()
    f32 = lambda k: np.ascontiguousarray(np.asarray(inputs[k], np.float32))
    x = f32("L0_attn_a_x").reshape(16, SEQK, 1024)
    wq_a, wk_a, wv_a = f32("L0_attn_a_Wq"), f32("L0_attn_a_Wk"), \
        f32("L0_attn_a_Wv")
    wg_a, wo_a = f32("L0_attn_a_Wg"), f32("L0_attn_a_Wo")
    wb_a, wa_a = f32("L0_attn_a_Wb"), f32("L0_attn_a_Wa")
    dtb = np.tile(f32("L0_attn_a_dt_bias").reshape(1, 16), (128, 1))
    alog = np.tile(f32("L0_attn_a_A_log").reshape(1, 16), (128, 1))
    nw64 = np.tile(f32("L0_attn_a_norm_w").reshape(1, 64), (128, 1))
    wq_b, wk_b = f32("L0_attn_b_Wq"), f32("L0_attn_b_Wk")
    wv_b, wo_b = f32("L0_attn_b_Wv"), f32("L0_attn_b_Wo")
    wr = f32("L0_ffn_Wr")
    w1 = f32("L0_ffn_W1")
    w2 = f32("L0_ffn_W2")

    ccA = {n: cc[n] for n in ["c_ident_bf", "c_ident_f", "c_cum2",
                              "c_sellast", "c_onesmask", "c_masks", "c_selb",
                              "c_scal", "c_iota8", "c_iota8r", "c_cumt",
                              "c_ones128c", "c_onesrow"]}
    ccB = {n: cc[n] for n in ["c_ident_bf", "c_iota8", "c_onesrow"]}
    ccC = {n: cc[n] for n in ["c_ident_bf", "c_negmask", "c_scal",
                              "c_iota8", "c_iota8r", "c_cumt", "c_ones128c",
                              "c_onesrow"]}

    mapsA = [{"x": x[2 * c:2 * c + 2].reshape(TCK, 1024),
              "wq": wq_a, "wk": wk_a, "wv": wv_a, "wg": wg_a, "wo": wo_a,
              "wb": wb_a, "wa": wa_a, "dtb": dtb, "alog": alog,
              "nw64": nw64, "wr": wr, **ccA} for c in range(8)]
    resA = run_bass_kernel_spmd(envA.nc, mapsA, core_ids=list(range(8)))
    x1 = [resA.results[c]["xo"] for c in range(8)]
    t2a = [resA.results[c]["t2"] for c in range(8)]
    routa = [resA.results[c]["rout"] for c in range(8)]
    cnts1 = np.concatenate([resA.results[c]["cnts"] for c in range(8)], 0)

    myms = [np.asarray([[1.0 if cp < c else 0.0] for cp in range(8)],
                       np.float32) for c in range(8)]

    mapsB = [{"x": x1[c], "t2": t2a[c], "rout": routa[c],
              "cnts8": cnts1, "mym": myms[c],
              "mycnt": cnts1[c:c + 1], "w1": w1, "w2": w2, **ccB}
             for c in range(8)]
    resB = run_bass_kernel_spmd(envB.nc, mapsB, core_ids=list(range(8)))
    x2 = [resB.results[c]["xo"] for c in range(8)]

    mapsC = [{"x": x2[c], "wq": wq_b, "wk": wk_b, "wv": wv_b, "wo": wo_b,
              "wr": wr, **ccC} for c in range(8)]
    resC = run_bass_kernel_spmd(envC.nc, mapsC, core_ids=list(range(8)))
    x3 = [resC.results[c]["xo"] for c in range(8)]
    t2b = [resC.results[c]["t2"] for c in range(8)]
    routb = [resC.results[c]["rout"] for c in range(8)]
    cnts2 = np.concatenate([resC.results[c]["cnts"] for c in range(8)], 0)

    mapsD = [{"x": x3[c], "t2": t2b[c], "rout": routb[c],
              "cnts8": cnts2, "mym": myms[c],
              "mycnt": cnts2[c:c + 1], "w1": w1, "w2": w2, **ccB}
             for c in range(8)]
    resD = run_bass_kernel_spmd(envB.nc, mapsD, core_ids=list(range(8)))

    out = np.concatenate([np.asarray(resD.results[c]["xo"], np.float32)
                          .reshape(2, SEQK, 1024) for c in range(8)], 0)
    return out
